# version 09
# baseline (speedup 1.0000x reference)
import contextlib

import numpy as np
import ml_dtypes

import concourse.bass as bass
import concourse.mybir as mybir
from concourse.bass_utils import run_bass_kernel_spmd

F32 = mybir.dt.float32
BF16 = mybir.dt.bfloat16
AF = mybir.ActivationFunctionType
ALU = mybir.AluOpType
AX = mybir.AxisListType

T, D, H, C = 2048, 1024, 4, 128
NCH = T // C
DK, DV = D // H, 2 * D // H
EPS = 1e-5
N_CORES = 8

DUMP = ()
B_SPECIES = 9
B_SINGLEBUF = False


class Sem:
    def __init__(self, h):
        self.h = h
        self.n = 0

    def bump(self, k=1):
        self.n += k
        return self.n


class Build:
    def __init__(self, dump=()):
        self.dump = tuple(dump)
        nc = bass.Bass("TRN2", target_bir_lowering=False)
        self.nc = nc
        d = self.d = {}

        def din(name, shape, dt=F32):
            d[name] = nc.dram_tensor(name, shape, dt, kind="ExternalInput").ap()

        din("x", [T, D])
        din("Wq", [D, D]); din("Wk", [D, D]); din("Wv", [D, 2 * D]); din("Wg", [D, 2 * D])
        din("Wo", [2 * D, D])
        din("lnw_rep", [128, D]); din("lnb_rep", [128, D])
        din("fgw_rep", [128, D]); din("fgb_rep", [128, D])
        din("gnwP", [128, 16])
        din("router", [8, D])
        din("sw1", [2 * D, D]); din("sw2", [D, 2 * D])
        din("ew1", [8 * 2 * D, D]); din("ew2", [8 * D, 2 * D])
        din("cosT", [128, T], BF16); din("sinT", [128, T], BF16)
        din("dmaskT", [128, 4 * 128])
        din("crossrep", [128, 4 * 128], BF16)
        din("kdecP", [128, 4])
        din("ident", [128, 128], BF16)
        d["out"] = nc.dram_tensor("out", [T, D], F32, kind="ExternalOutput").ap()

        def scr(name, shape, dt):
            d[name] = nc.dram_tensor(name, shape, dt, kind="Internal").ap()

        scr("qT_s", [D, T], BF16)
        scr("kT_s", [D, T], BF16)
        scr("kk_s", [T, D], BF16)
        scr("v_s", [T, 2 * D], BF16)
        scr("g_s", [T, 2 * D], BF16)
        scr("xp_s", [T, D], F32)
        scr("w1n_s", [9 * 2 * D, D], BF16)
        scr("w2n_s", [9 * D, 2 * D], BF16)

        for name in self.dump:
            if name.startswith("@"):
                continue
            d["dump_" + name] = nc.dram_tensor(
                "dump_" + name, list(d[name].shape), d[name].dtype, kind="ExternalOutput").ap()

        g64 = 1.0 - np.exp2(-5.0 - np.arange(H, dtype=np.float64))
        self.gC = [float(np.float32(v)) for v in g64 ** C]

    def dinc(self, eng, sem, k=1):
        eng.maybe_drain_then_inc((sem.h, k))
        sem.bump(k)

    def minc(self, inst, sem, k=1):
        inst.then_inc(sem.h, k)
        sem.bump(k)

    def dbg(self, name, ap):
        if "@" + name not in self.dump:
            return
        nc, block = self.nc, self.block
        shape = [int(s) for s in ap.shape]
        out = nc.dram_tensor("dump_@" + name, shape, ap.dtype, kind="ExternalOutput").ap()

        @block.gpsimd
        def _(g):
            for s in (self.sv, self.sa, self.sp, self.si, self.sg, self.sc):
                g.wait_ge(s.h, s.n)
            self.minc(g.dma_start(out=out, in_=ap), self.so, 16)

    def bank_wait(self, eng, i):
        g = self.bfree.get(i)
        if g is not None:
            eng.wait_ge(g[0].h, g[1])

    def bank_set(self, i, sem):
        self.bfree[i] = (sem, sem.n)

    def barrier(self):
        block = self.block
        for name, sem in (("tensor", self.sp), ("vector", self.sv),
                          ("scalar", self.sa), ("gpsimd", self.sg)):
            def _f(eng, sem=sem):
                self.dinc(eng, sem, 1)
            getattr(block, name)(_f)
        tot = [(s.h, s.n) for s in (self.si, self.sg, self.sc, self.sp, self.sv, self.sa)]
        for name in ("tensor", "vector", "scalar", "gpsimd", "sync"):
            def _f(eng, tot=tot):
                for h, n in tot:
                    eng.wait_ge(h, n)
            getattr(block, name)(_f)

    def build(self):
        nc, d = self.nc, self.d
        with contextlib.ExitStack() as ctx:
            E = ctx.enter_context
            sb = lambda name, shape, dt: E(nc.sbuf_tensor(name, shape, dt))
            self.bank = [E(nc.psum_tensor(f"bank{i}", [128, 512], F32)) for i in range(8)]
            self.bfree = {}

            self.si = Sem(E(nc.semaphore("si")))
            self.sg = Sem(E(nc.semaphore("sg")))
            self.sc = Sem(E(nc.semaphore("sc")))
            self.sp = Sem(E(nc.semaphore("sp")))
            self.sv = Sem(E(nc.semaphore("sv")))
            self.sa = Sem(E(nc.semaphore("sa")))
            self.so = Sem(E(nc.semaphore("so")))
            self.sq = Sem(E(nc.semaphore("sq")))
            block = self.block = E(nc.Block())

            self.ident = sb("s_ident", [128, 128], BF16)
            self.a1v = sb("s_a1v", [128, NCH], F32)
            self.nma1 = sb("s_nma1", [128, NCH], F32)

            si = self.si

            @block.sync
            def _(sync):
                self.minc(sync.dma_start(out=self.ident[:], in_=d["ident"]), si, 16)
            self.consts_loaded = si.n

            with contextlib.ExitStack() as qctx:
                self.init_q(qctx)
                with contextlib.ExitStack() as pctx:
                    self.phase_a1(pctx)
                self.barrier()
                with contextlib.ExitStack() as pctx:
                    self.phase_a2(pctx)
                self.emit_q(10 ** 9)
            self.barrier()
            with contextlib.ExitStack() as pctx:
                self.phase_b(pctx)
            self.emit_dumps()
        return nc

    def phase_a1(self, ctx):
        nc, d, block = self.nc, self.d, self.block
        si, sg, sp, sv, sa = self.si, self.sg, self.sp, self.sv, self.sa
        bank = self.bank
        sb = lambda name, shape, dt: ctx.enter_context(nc.sbuf_tensor(name, shape, dt))

        wq = sb("a_wq", [128, 8 * D], BF16)
        wk = sb("a_wk", [128, 8 * D], BF16)
        wv = sb("a_wv", [128, 8 * 2 * D], BF16)
        wg = sb("a_wg", [128, 8 * 2 * D], BF16)
        cosT = sb("a_cosT", [128, 512], BF16)
        sinT = sb("a_sinT", [128, 512], BF16)
        xin = sb("a_xin", [128, 4 * D], F32)
        s1 = sb("a_s1", [128, NCH], F32)
        ssq = sb("a_ssq", [128, NCH], F32)
        mu = sb("a_mu", [128, NCH], F32)
        var = sb("a_var", [128, NCH], F32)
        tA = sb("a_tA", [128, NCH], F32)
        tB = sb("a_tB", [128, NCH], F32)
        beta = sb("a_beta", [128, NCH], F32)
        nmb = sb("a_nmb", [128, NCH], F32)
        u1 = sb("a_u1", [128, 4 * D], F32)
        scr = u1[:, 0:D]
        nrm = sb("a_nrm", [128, 4 * D], BF16)
        nrmT = sb("a_nrmT", [128, 8 * 512], BF16)
        praw = sb("a_praw", [128, 8 * 512], BF16)
        ra = sb("a_ra", [128, 1024], BF16)
        rb = sb("a_rb", [128, 1024], BF16)
        pT = praw
        kkb = sb("a_kkb", [128, 4 * D], BF16)
        vgb = sb("a_vgb", [128, 4 * 512], BF16)

        kdecP = sb("a_kdecP", [128, 4], F32)
        self.lnw = sb("a_lnw", [128, D], F32)
        self.lnb = sb("a_lnb", [128, D], F32)

        @block.sync
        def _(sync):
            self.minc(sync.dma_start(out=kdecP[:], in_=d["kdecP"]), si, 16)
            self.minc(sync.dma_start(out=self.lnw[:], in_=d["lnw_rep"]), si, 16)
            self.minc(sync.dma_start(out=self.lnb[:], in_=d["lnb_rep"]), si, 16)
        consts2 = si.n

        xfree = {}
        trig_free = {}

        @block.gpsimd
        def _(g):
            for name, dst, cn in (("Wq", wq, 8), ("Wk", wk, 8), ("Wv", wv, 8), ("Wg", wg, 8)):
                self.minc(g.dma_start(out=dst[:].rearrange("p (c f) -> p c f", c=cn),
                                      in_=d[name].rearrange("(c p) f -> p c f", p=128)), sg, 16)
        w_loaded = sg.n

        for tt in range(4):
            xbuf = xin
            cols = slice(tt * 4, tt * 4 + 4)

            @block.sync
            def _(sync, tt=tt, xbuf=xbuf):
                if tt >= 1:
                    sync.wait_ge(sv.h, xfree[tt - 1])
                    sync.wait_ge(sv.h, trig_free[tt - 1])
                src = d["x"].rearrange("(c p) f -> p c f", p=128)[:, tt * 4:tt * 4 + 4, :]
                self.minc(sync.dma_start(out=xbuf.rearrange("p (c f) -> p c f", c=4), in_=src), si, 16)
                self.minc(sync.dma_start(out=cosT[:], in_=d["cosT"][:, tt * 512:(tt + 1) * 512]), si, 16)
                self.minc(sync.dma_start(out=sinT[:], in_=d["sinT"][:, tt * 512:(tt + 1) * 512]), si, 16)
            x_loaded = si.n
            trig_loaded = si.n

            @block.scalar
            def _(s, tt=tt, xbuf=xbuf):
                s.wait_ge(si.h, x_loaded)
                for c in range(4):
                    tc = tt * 4 + c
                    xc = xbuf[:, c * D:(c + 1) * D]
                    s.activation(scr[:], xc, AF.Square, accum_out=ssq[:, tc:tc + 1])
                    s.activation(scr[:], xc, AF.Identity, accum_out=s1[:, tc:tc + 1])
                self.dinc(s, sa, 1)
            stats_done = sa.n
            if tt == 0:
                self.dbg("xin", xin[:])
                self.dbg("s1", s1[:])
                self.dbg("ssq", ssq[:])

            @block.vector
            def _(v, cols=cols):
                v.wait_ge(sa.h, stats_done)
                cl = cols
                v.tensor_scalar(mu[:, cl], s1[:, cl], 1.0 / D, None, ALU.mult)
                v.drain()
                v.tensor_tensor(tA[:, cl], mu[:, cl], mu[:, cl], ALU.mult)
                v.drain()
                v.scalar_tensor_tensor(var[:, cl], ssq[:, cl], 1.0 / D, tA[:, cl], ALU.mult, ALU.subtract)
                v.drain()
                v.tensor_scalar(tA[:, cl], var[:, cl], EPS, None, ALU.add)
                self.dinc(v, sv, 1)
            ve_done = sv.n

            @block.scalar
            def _(s, cols=cols):
                s.wait_ge(sv.h, ve_done)
                s.activation(tB[:, cols], tA[:, cols], AF.Ln)
                s.drain()
                s.activation(tB[:, cols], tB[:, cols], AF.Exp, scale=0.5)
                self.dinc(s, sa, 1)
            sd1_done = sa.n

            @block.vector
            def _(v, cols=cols):
                cl = cols
                v.wait_ge(sa.h, sd1_done)
                v.reciprocal(self.a1v[:, cl], tB[:, cl])
                v.reciprocal(tB[:, cl], tA[:, cl])
                v.drain()
                v.tensor_tensor(tB[:, cl], var[:, cl], tB[:, cl], ALU.mult)
                v.drain()
                v.tensor_scalar(tB[:, cl], tB[:, cl], EPS, None, ALU.add)
                self.dinc(v, sv, 1)
            v1_done = sv.n

            @block.scalar
            def _(s, cols=cols):
                s.wait_ge(sv.h, v1_done)
                s.activation(tA[:, cols], tB[:, cols], AF.Ln)
                s.drain()
                s.activation(tA[:, cols], tA[:, cols], AF.Exp, scale=0.5)
                self.dinc(s, sa, 1)
            sd2_done = sa.n

            @block.vector
            def _(v, cols=cols, tt=tt, xbuf=xbuf):
                cl = cols
                v.wait_ge(sa.h, sd2_done)
                if tt > 0:
                    v.wait_ge(sp.h, sp.n)
                v.wait_ge(si.h, self.consts_loaded)
                v.reciprocal(tB[:, cl], tA[:, cl])
                v.drain()
                v.tensor_tensor(beta[:, cl], self.a1v[:, cl], tB[:, cl], ALU.mult)
                v.drain()
                v.scalar_tensor_tensor(nmb[:, cl], mu[:, cl], -1.0, beta[:, cl], ALU.mult, ALU.mult)
                v.scalar_tensor_tensor(self.nma1[:, cl], mu[:, cl], -1.0, self.a1v[:, cl], ALU.mult, ALU.mult)
                v.drain()
                for c in range(4):
                    tc = tt * 4 + c
                    v.tensor_scalar(u1[:, c * D:(c + 1) * D], xbuf[:, c * D:(c + 1) * D],
                                    beta[:, tc:tc + 1], nmb[:, tc:tc + 1], ALU.mult, ALU.add)
                v.drain()
                for c in range(4):
                    v.tensor_tensor(u1[:, c * D:(c + 1) * D], u1[:, c * D:(c + 1) * D],
                                    self.lnw[:], ALU.mult)
                v.drain()
                for c in range(4):
                    v.tensor_tensor(nrm[:, c * D:(c + 1) * D], u1[:, c * D:(c + 1) * D],
                                    self.lnb[:], ALU.add)
                self.dinc(v, sv, 1)
            nrm_done = sv.n
            xfree[tt] = nrm_done
            if tt == 0:
                self.dbg("beta", beta[:])
                self.dbg("nmb", nmb[:])
                self.dbg("nrm", nrm[:])
                self.dbg("mu", mu[:])
                self.dbg("var", var[:])
                self.dbg("a1v", self.a1v[:])

            for grp in range(8):
                bi = grp % 2
                bk = bank[bi]

                @block.tensor
                def _(t, grp=grp, bk=bk, bi=bi):
                    t.wait_ge(sv.h, nrm_done)
                    if grp == 0:
                        t.wait_ge(si.h, 16)
                    self.bank_wait(t, bi)
                    for j in range(4):
                        i = grp * 4 + j
                        c, dc = i // 8, i % 8
                        self.minc(t.matmul(bk[:, j * 128:(j + 1) * 128],
                                           nrm[:, c * D + dc * 128:c * D + (dc + 1) * 128],
                                           self.ident[:], start=True, stop=True), sp, 1)
                pe_cnt = sp.n

                @block.scalar
                def _(s, grp=grp, bk=bk):
                    s.wait_ge(sp.h, pe_cnt)
                    for j in range(4):
                        i = grp * 4 + j
                        c, dc = i // 8, i % 8
                        s.copy(nrmT[:, dc * 512 + c * 128:dc * 512 + (c + 1) * 128],
                               bk[:, j * 128:(j + 1) * 128])
                    self.dinc(s, sa, 1)
                self.bank_set(bi, sa)
            nrmT_ready = sa.n

            for which, wmat, dst_s, scale in (("q", wq, "qT_s", DK ** -0.5),
                                              ("k", wk, "kT_s", 1.0)):
                for fp in range(4):
                    @block.tensor
                    def _(t, fp=fp, wmat=wmat, which=which, tt=tt):
                        if which == "q" and fp == 0:
                            t.wait_ge(sa.h, nrmT_ready)
                        if which == "q" and tt == 0 and fp == 0:
                            t.wait_ge(sg.h, w_loaded)
                        for half in range(2):
                            fc = fp * 2 + half
                            self.bank_wait(t, 2 + half)
                            for kc in range(8):
                                self.minc(t.matmul(bank[2 + half][:],
                                                   wmat[:, kc * D + fc * 128:kc * D + (fc + 1) * 128],
                                                   nrmT[:, kc * 512:(kc + 1) * 512],
                                                   start=(kc == 0), stop=(kc == 7)), sp, 1)
                    pe_cnt = sp.n

                    @block.scalar
                    def _(s, fp=fp, scale=scale, pe_cnt=pe_cnt):
                        s.wait_ge(sp.h, pe_cnt)
                        s.wait_ge(sv.h, sv.n)
                        s.wait_ge(sg.h, sg.n)
                        for half in range(2):
                            fc = fp * 2 + half
                            s.activation(praw[:, fc * 512:(fc + 1) * 512], bank[2 + half][:],
                                         AF.Copy, scale=float(scale))
                        self.dinc(s, sa, 1)
                    self.bank_set(2, sa)
                    self.bank_set(3, sa)
                praw_done = sa.n

                @block.vector
                def _(v, tt=tt):
                    v.wait_ge(sa.h, praw_done)
                    v.wait_ge(sg.h, sg.n)
                    v.wait_ge(si.h, trig_loaded)
                    cs = cosT[:]
                    sn = sinT[:]
                    for h in range(H):
                        t1 = praw[:, (2 * h) * 512:(2 * h + 1) * 512]
                        t2 = praw[:, (2 * h + 1) * 512:(2 * h + 2) * 512]
                        v.tensor_tensor(ra[:, 0:512], t1, cs, ALU.mult)
                        v.tensor_tensor(ra[:, 512:1024], t2, sn, ALU.mult)
                        v.tensor_tensor(rb[:, 0:512], t2, cs, ALU.mult)
                        v.tensor_tensor(rb[:, 512:1024], t1, sn, ALU.mult)
                        v.drain()
                        v.tensor_tensor(pT[:, (2 * h) * 512:(2 * h + 1) * 512],
                                        ra[:, 0:512], ra[:, 512:1024], ALU.subtract)
                        v.tensor_tensor(pT[:, (2 * h + 1) * 512:(2 * h + 2) * 512],
                                        rb[:, 0:512], rb[:, 512:1024], ALU.add)
                    self.dinc(v, sv, 1)
                rope_done = sv.n

                @block.gpsimd
                def _(g, tt=tt, dst_s=dst_s):
                    g.wait_ge(sv.h, rope_done)
                    dst = d[dst_s].rearrange("(fc p) t -> p fc t", p=128)[:, :, tt * 512:(tt + 1) * 512]
                    self.minc(g.dma_start(out=dst, in_=pT[:].rearrange("p (fc t) -> p fc t", fc=8)), sg, 16)

                if which == "k":
                    for grp in range(8):
                        bi = 4 + grp % 2
                        bk = bank[bi]

                        @block.tensor
                        def _(t, grp=grp, bk=bk, bi=bi):
                            t.wait_ge(sv.h, rope_done)
                            self.bank_wait(t, bi)
                            for j in range(4):
                                i = grp * 4 + j
                                dc, c = i // 4, i % 4
                                self.minc(t.matmul(bk[:, j * 128:(j + 1) * 128],
                                                   pT[:, dc * 512 + c * 128:dc * 512 + (c + 1) * 128],
                                                   self.ident[:], start=True, stop=True), sp, 1)
                        pe_cnt = sp.n

                        @block.vector
                        def _(v, grp=grp, bk=bk):
                            v.wait_ge(sp.h, pe_cnt)
                            if grp == 0:
                                v.wait_ge(sg.h, sg.n)
                            for j in range(4):
                                i = grp * 4 + j
                                dc, c = i // 4, i % 4
                                h = dc // 2
                                v.tensor_scalar(kkb[:, c * D + dc * 128:c * D + (dc + 1) * 128],
                                                bk[:, j * 128:(j + 1) * 128],
                                                kdecP[:, h:h + 1], None, ALU.mult)
                            self.dinc(v, sv, 1)
                        self.bank_set(bi, sv)
                    kk_done = sv.n

                    @block.gpsimd
                    def _(g, tt=tt):
                        g.wait_ge(sv.h, kk_done)
                        dst = d["kk_s"][tt * 512:(tt + 1) * 512, :].rearrange("(c p) f -> p c f", p=128)
                        self.minc(g.dma_start(out=dst, in_=kkb[:].rearrange("p (c f) -> p c f", c=4)), sg, 16)

            for which, wmat, dst_s in (("v", wv, "v_s"), ("g", wg, "g_s")):
                for c in range(4):
                    @block.tensor
                    def _(t, c=c, wmat=wmat):
                        for vc in range(4):
                            self.bank_wait(t, 4 + vc)
                            for kc in range(8):
                                self.minc(t.matmul(bank[4 + vc][:],
                                                   nrmT[:, kc * 512 + c * 128:kc * 512 + (c + 1) * 128],
                                                   wmat[:, kc * 2 * D + vc * 512:kc * 2 * D + (vc + 1) * 512],
                                                   start=(kc == 0), stop=(kc == 7)), sp, 1)
                    pe_cnt = sp.n

                    @block.scalar
                    def _(s, which=which, pe_cnt=pe_cnt):
                        s.wait_ge(sp.h, pe_cnt)
                        s.wait_ge(sg.h, sg.n)
                        for vc in range(4):
                            if which == "g":
                                s.activation(vgb[:, vc * 512:(vc + 1) * 512], bank[4 + vc][:], AF.Silu)
                            else:
                                s.copy(vgb[:, vc * 512:(vc + 1) * 512], bank[4 + vc][:])
                        self.dinc(s, sa, 1)
                    for vc in range(4):
                        self.bank_set(4 + vc, sa)
                    row_done = sa.n

                    @block.gpsimd
                    def _(g, tt=tt, c=c, dst_s=dst_s):
                        g.wait_ge(sa.h, row_done)
                        dst = d[dst_s][tt * 512 + c * 128:tt * 512 + (c + 1) * 128, :]
                        self.minc(g.dma_start(out=dst, in_=vgb[:]), sg, 16)

            trig_free[tt] = sv.n
            self.emit_q(30)

        self.a1_spills = sg.n

    def phase_a2(self, ctx):
        nc, d, block = self.nc, self.d, self.block
        si, sg, sp, sv, sa = self.si, self.sg, self.sp, self.sv, self.sa
        bank = self.bank
        sb = lambda name, shape, dt: ctx.enter_context(nc.sbuf_tensor(name, shape, dt))
        gCs = self.gC

        dmaskT = sb("c_dmaskT", [128, 512], F32)
        crossrep = sb("c_crossrep", [128, 512], BF16)
        gnwP = sb("c_gnwP", [128, 16], F32)
        self.S = sb("c_S", [128, 8 * 512], F32)
        self.Sb = sb("c_Sb", [128, 8 * 512], BF16)
        wo = sb("c_wo", [128, 16 * D], BF16)
        wostg = sb("c_wostg", [128, 2 * D], F32)
        qgrp = sb("c_qgrp", [128, 2 * 2048], BF16)
        kgrp = sb("c_kgrp", [128, 2 * 2048], BF16)
        kkgrp = sb("c_kkgrp", [128, 2 * 2048], BF16)
        vgrp = sb("c_vgrp", [128, 2 * 4096], BF16)
        ggrp = sb("c_ggrp", [128, 2 * 4096], BF16)
        xgrp = sb("c_xgrp", [128, 2 * 2048], F32)
        qx = sb("c_qx", [128, 8 * 128], BF16)
        at = sb("c_at", [128, 4 * 128], BF16)
        og = sb("c_og", [128, 2048], BF16)
        ogT = sb("c_ogT", [128, 16 * 128], BF16)
        scr5 = sb("c_scr5", [128, 512], F32)
        ssqo = sb("c_ssqo", [128, 4], F32)
        rstd = sb("c_rstd", [128, 4], F32)
        yv = sb("c_y", [128, D], F32)
        af = sb("c_af", [128, D], F32)
        xp = sb("c_xp", [128, D], F32)
        sm1 = sb("c_sm1", [128, 6], F32)

        @block.sync
        def _(sync):
            self.minc(sync.dma_start(out=dmaskT[:], in_=d["dmaskT"]), si, 16)
            self.minc(sync.dma_start(out=crossrep[:], in_=d["crossrep"]), si, 16)
            self.minc(sync.dma_start(out=gnwP[:], in_=d["gnwP"]), si, 16)
        a2c_loaded = si.n

        @block.vector
        def _(v):
            v.memset(self.S[:], 0.0)
            self.dinc(v, sv, 1)
        S_zeroed = sv.n

        wo_ready = []
        for kc in range(16):
            @block.sync
            def _(sync, kc=kc):
                if kc >= 2:
                    sync.wait_ge(sv.h, wo_ready[kc - 2])
                self.minc(sync.dma_start(out=wostg[:, (kc % 2) * D:((kc % 2) + 1) * D],
                                         in_=d["Wo"][kc * 128:(kc + 1) * 128, :]), si, 16)
            ld = si.n

            @block.vector
            def _(v, kc=kc):
                v.wait_ge(si.h, max(ld, a2c_loaded))
                v.tensor_scalar(wo[:, kc * D:(kc + 1) * D], wostg[:, (kc % 2) * D:((kc % 2) + 1) * D],
                                gnwP[:, kc:kc + 1], None, ALU.mult)
                self.dinc(v, sv, 1)
            wo_ready.append(sv.n)
        wo_done = sv.n

        group_done = {}
        group_loaded = {}
        prev = {}
        for n in range(NCH):
            gidx, par = n // 2, n % 2
            gb = (gidx % 2)

            if par == 0:
                @block.sync
                def _(sync, n=n, gidx=gidx, gb=gb):
                    if gidx >= 2:
                        sync.wait_ge(sv.h, group_done[gidx - 2])
                    r0 = n * 128
                    self.minc(sync.dma_start(
                        out=qgrp[:, gb * 2048:(gb + 1) * 2048].rearrange("p (fc t) -> p fc t", fc=8),
                        in_=d["qT_s"].rearrange("(fc p) t -> p fc t", p=128)[:, :, r0:r0 + 256]), si, 16)
                    self.minc(sync.dma_start(
                        out=kgrp[:, gb * 2048:(gb + 1) * 2048].rearrange("p (fc t) -> p fc t", fc=8),
                        in_=d["kT_s"].rearrange("(fc p) t -> p fc t", p=128)[:, :, r0:r0 + 256]), si, 16)
                    self.minc(sync.dma_start(
                        out=kkgrp[:, gb * 2048:(gb + 1) * 2048].rearrange("p (c f) -> p c f", c=2),
                        in_=d["kk_s"][r0:r0 + 256, :].rearrange("(c p) f -> p c f", p=128)), si, 16)
                    self.minc(sync.dma_start(
                        out=vgrp[:, gb * 4096:(gb + 1) * 4096].rearrange("p (c f) -> p c f", c=2),
                        in_=d["v_s"][r0:r0 + 256, :].rearrange("(c p) f -> p c f", p=128)), si, 16)
                    self.minc(sync.dma_start(
                        out=ggrp[:, gb * 4096:(gb + 1) * 4096].rearrange("p (c f) -> p c f", c=2),
                        in_=d["g_s"][r0:r0 + 256, :].rearrange("(c p) f -> p c f", p=128)), si, 16)
                    self.minc(sync.dma_start(
                        out=xgrp[:, gb * 2048:(gb + 1) * 2048].rearrange("p (c f) -> p c f", c=2),
                        in_=d["x"][r0:r0 + 256, :].rearrange("(c p) f -> p c f", p=128)), si, 16)
                group_loaded[gidx] = si.n

            ld = group_loaded[gidx]
            qch = lambda i, gb=gb, par=par: qgrp[:, gb * 2048 + i * 256 + par * 128:gb * 2048 + i * 256 + (par + 1) * 128]
            kch = lambda i, gb=gb, par=par: kgrp[:, gb * 2048 + i * 256 + par * 128:gb * 2048 + i * 256 + (par + 1) * 128]
            kkch = lambda i, gb=gb, par=par: kkgrp[:, gb * 2048 + par * 1024 + i * 128:gb * 2048 + par * 1024 + (i + 1) * 128]
            vch = lambda h, gb=gb, par=par: vgrp[:, gb * 4096 + par * 2048 + h * 512:gb * 4096 + par * 2048 + (h + 1) * 512]
            gch = lambda h, gb=gb, par=par: ggrp[:, gb * 4096 + par * 2048 + h * 512:gb * 4096 + par * 2048 + (h + 1) * 512]
            xch = lambda gb=gb, par=par: xgrp[:, gb * 2048 + par * 1024:gb * 2048 + (par + 1) * 1024]

            @block.vector
            def _(v, n=n):
                v.wait_ge(sv.h, S_zeroed if n == 0 else prev["supd"])
                if n > 0:
                    v.wait_ge(sp.h, prev["o_mm"])
                for i in range(8):
                    v.tensor_copy(self.Sb[:, i * 512:(i + 1) * 512], self.S[:, i * 512:(i + 1) * 512])
                self.dinc(v, sv, 1)
            cast_done = sv.n

            @block.vector
            def _(v, n=n):
                v.wait_ge(si.h, max(ld, a2c_loaded))
                if n > 0:
                    v.wait_ge(sp.h, prev["o_mm"])
                for i in range(8):
                    h = i // 2
                    v.tensor_tensor(qx[:, i * 128:(i + 1) * 128], qch(i),
                                    crossrep[:, h * 128:(h + 1) * 128], ALU.mult)
                self.dinc(v, sv, 1)
            qx_done = sv.n

            @block.tensor
            def _(t, n=n):
                t.wait_ge(si.h, ld)
                self.bank_wait(t, 0)
                for h in range(H):
                    for dc in range(2):
                        self.minc(t.matmul(bank[0][:, h * 128:(h + 1) * 128],
                                           kch(2 * h + dc), qch(2 * h + dc),
                                           start=(dc == 0), stop=(dc == 1)), sp, 1)
            attnT_mm = sp.n

            @block.vector
            def _(v, n=n):
                v.wait_ge(sp.h, attnT_mm)
                if n > 0:
                    v.wait_ge(sp.h, prev["o_mm"])
                for h in range(H):
                    v.tensor_tensor(at[:, h * 128:(h + 1) * 128], bank[0][:, h * 128:(h + 1) * 128],
                                    dmaskT[:, h * 128:(h + 1) * 128], ALU.mult)
                self.dinc(v, sv, 1)
            mask_done = sv.n
            self.bank_set(0, sv)

            @block.tensor
            def _(t, n=n):
                t.wait_ge(sv.h, mask_done)
                t.wait_ge(sv.h, cast_done)
                t.wait_ge(sv.h, qx_done)
                for h in range(H):
                    self.bank_wait(t, 2 + h)
                    self.minc(t.matmul(bank[2 + h][:], at[:, h * 128:(h + 1) * 128], vch(h),
                                       start=True, stop=False), sp, 1)
                    for dc in range(2):
                        self.minc(t.matmul(bank[2 + h][:], qx[:, (2 * h + dc) * 128:(2 * h + dc + 1) * 128],
                                           self.Sb[:, (2 * h + dc) * 512:(2 * h + dc + 1) * 512],
                                           start=False, stop=(dc == 1)), sp, 1)
            o_mm = sp.n

            supd_marks = []
            for h in range(H):
                @block.tensor
                def _(t, h=h, n=n):
                    for dc in range(2):
                        bi = 6 + dc
                        self.bank_wait(t, bi)
                        self.minc(t.matmul(bank[bi][:], kkch(2 * h + dc), vch(h),
                                           start=True, stop=True), sp, 1)
                ds_mm = sp.n

                @block.vector
                def _(v, h=h, n=n, ds_mm=ds_mm):
                    v.wait_ge(sp.h, ds_mm)
                    if n == 0:
                        v.wait_ge(sv.h, S_zeroed)
                    for dc in range(2):
                        sl = slice((2 * h + dc) * 512, (2 * h + dc + 1) * 512)
                        v.scalar_tensor_tensor(self.S[:, sl], self.S[:, sl], gCs[h],
                                               bank[6 + dc][:], ALU.mult, ALU.add)
                    self.dinc(v, sv, 1)
                self.bank_set(6, sv)
                self.bank_set(7, sv)
            supd = sv.n

            @block.scalar
            def _(s, n=n):
                s.wait_ge(sp.h, o_mm)
                for h in range(H):
                    s.activation(scr5[:], bank[2 + h][:], AF.Square, accum_out=ssqo[:, h:h + 1])
                self.dinc(s, sa, 1)
            sq_done = sa.n

            @block.vector
            def _(v):
                v.wait_ge(sa.h, sq_done)
                v.tensor_scalar(rstd[:], ssqo[:], 1.0 / DV, EPS, ALU.mult, ALU.add)
                self.dinc(v, sv, 1)
            ve2 = sv.n

            @block.scalar
            def _(s):
                s.wait_ge(sv.h, ve2)
                s.activation(rstd[:], rstd[:], AF.Ln)
                s.drain()
                s.activation(rstd[:], rstd[:], AF.Exp, scale=-0.5)
                self.dinc(s, sa, 1)
            rstd_done = sa.n

            @block.vector
            def _(v, n=n):
                v.wait_ge(sa.h, rstd_done)
                if n > 0:
                    v.wait_ge(sp.h, prev["ogT_mm"])
                for h in range(H):
                    v.scalar_tensor_tensor(og[:, h * 512:(h + 1) * 512], bank[2 + h][:],
                                           rstd[:, h:h + 1], gch(h),
                                           ALU.mult, ALU.mult)
                self.dinc(v, sv, 1)
            og_done = sv.n
            for h in range(H):
                self.bank_set(2 + h, sv)

            for grp in range(4):
                bi = grp % 2
                @block.tensor
                def _(t, grp=grp, bi=bi):
                    t.wait_ge(sv.h, og_done)
                    self.bank_wait(t, bi)
                    for j in range(4):
                        kc = grp * 4 + j
                        self.minc(t.matmul(bank[bi][:, j * 128:(j + 1) * 128],
                                           og[:, kc * 128:(kc + 1) * 128], self.ident[:],
                                           start=True, stop=True), sp, 1)
                pe_cnt = sp.n

                @block.scalar
                def _(s, grp=grp, bi=bi, pe_cnt=pe_cnt):
                    s.wait_ge(sp.h, pe_cnt)
                    for j in range(4):
                        kc = grp * 4 + j
                        s.copy(ogT[:, kc * 128:(kc + 1) * 128], bank[bi][:, j * 128:(j + 1) * 128])
                    self.dinc(s, sa, 1)
                self.bank_set(bi, sa)
            ogT_mm = sp.n
            ogT_done = sa.n

            @block.tensor
            def _(t, n=n):
                t.wait_ge(sa.h, ogT_done)
                t.wait_ge(sv.h, wo_done)
                self.bank_wait(t, 2)
                self.bank_wait(t, 3)
                for half in range(2):
                    for kc in range(16):
                        self.minc(t.matmul(bank[2 + half][:], ogT[:, kc * 128:(kc + 1) * 128],
                                           wo[:, kc * D + half * 512:kc * D + (half + 1) * 512],
                                           start=(kc == 0), stop=(kc == 15)), sp, 1)
            wo_mm = sp.n

            @block.vector
            def _(v, n=n):
                v.wait_ge(sp.h, wo_mm)
                v.tensor_scalar(yv[:], xch(), self.a1v[:, n:n + 1], self.nma1[:, n:n + 1],
                                ALU.mult, ALU.add)
                v.drain()
                for half in range(2):
                    v.scalar_tensor_tensor(af[:, half * 512:(half + 1) * 512], bank[2 + half][:],
                                           1.0, yv[:, half * 512:(half + 1) * 512], ALU.mult, ALU.add)
                self.dinc(v, sv, 1)
            af_done = sv.n
            self.bank_set(2, sv)
            self.bank_set(3, sv)

            @block.scalar
            def _(s):
                s.wait_ge(sv.h, af_done)
                s.activation(scr5[:], af[:, 0:512], AF.Square, accum_out=sm1[:, 0:1])
                s.activation(scr5[:], af[:, 512:1024], AF.Square, accum_out=sm1[:, 1:2])
                s.activation(scr5[:], af[:, 0:512], AF.Identity, accum_out=sm1[:, 2:3])
                s.activation(scr5[:], af[:, 512:1024], AF.Identity, accum_out=sm1[:, 3:4])
                self.dinc(s, sa, 1)
            ost_done = sa.n

            @block.vector
            def _(v):
                v.wait_ge(sa.h, ost_done)
                v.tensor_tensor(sm1[:, 4:5], sm1[:, 2:3], sm1[:, 3:4], ALU.add)
                v.drain()
                v.tensor_scalar(sm1[:, 5:6], sm1[:, 4:5], 1.0 / D, None, ALU.mult)
                v.drain()
                v.tensor_tensor(sm1[:, 4:5], sm1[:, 0:1], sm1[:, 1:2], ALU.add)
                v.drain()
                v.tensor_tensor(sm1[:, 2:3], sm1[:, 5:6], sm1[:, 5:6], ALU.mult)
                v.drain()
                v.scalar_tensor_tensor(sm1[:, 4:5], sm1[:, 4:5], 1.0 / D, sm1[:, 2:3],
                                       ALU.mult, ALU.subtract)
                v.drain()
                v.tensor_scalar(sm1[:, 4:5], sm1[:, 4:5], EPS, None, ALU.add)
                self.dinc(v, sv, 1)
            ve3 = sv.n

            @block.scalar
            def _(s):
                s.wait_ge(sv.h, ve3)
                s.activation(sm1[:, 4:5], sm1[:, 4:5], AF.Ln)
                s.drain()
                s.activation(sm1[:, 4:5], sm1[:, 4:5], AF.Exp, scale=-0.5)
                self.dinc(s, sa, 1)
            al_done = sa.n

            @block.vector
            def _(v, n=n):
                v.wait_ge(sa.h, al_done)
                if n > 0:
                    v.wait_ge(sg.h, prev["spill"])
                v.scalar_tensor_tensor(sm1[:, 2:3], sm1[:, 5:6], -1.0, sm1[:, 4:5],
                                       ALU.mult, ALU.mult)
                v.drain()
                v.tensor_scalar(xp[:], xch(), 1.0, sm1[:, 2:3], ALU.mult, ALU.add)
                v.drain()
                v.scalar_tensor_tensor(xp[:], af[:], sm1[:, 4:5], xp[:], ALU.mult, ALU.add)
                self.dinc(v, sv, 1)
            xp_done = sv.n

            @block.gpsimd
            def _(g, n=n):
                g.wait_ge(sv.h, xp_done)
                self.minc(g.dma_start(out=d["xp_s"][n * 128:(n + 1) * 128, :], in_=xp[:]), sg, 16)
            prev = dict(supd=supd, o_mm=o_mm, og=og_done, ogT_mm=ogT_mm, spill=sg.n)
            group_done[gidx] = xp_done
            self.emit_q(11)

    def init_q(self, ctx):
        nc, d = self.nc, self.d
        sb = lambda name, shape, dt: ctx.enter_context(nc.sbuf_tensor(name, shape, dt))
        self.q_wraw = sb("q_wraw", [128, 2 * 1024], F32)
        self.q_wab = sb("q_wab", [128, 2 * 1024], F32)
        self.q_wsgn = sb("q_wsgn", [128, 2 * 1024], BF16)
        self.q_wout = sb("q_wout", [128, 2 * 1024], BF16)
        self.q_red = sb("q_red", [128, 2], F32)
        self.q_thr = sb("q_thr", [128, 2], F32)
        self.q_mcol = sb("q_mcol", [128, 2], F32)

        units = []
        for s_i in range(9):
            w1src = d["sw1"] if s_i == 0 else d["ew1"][(s_i - 1) * 2048:s_i * 2048, :]
            w2src = d["sw2"] if s_i == 0 else d["ew2"][(s_i - 1) * 1024:s_i * 1024, :]
            for it in range(16):
                units.append((w1src[it * 128:(it + 1) * 128, :],
                              d["w1n_s"][s_i * 2048 + it * 128:s_i * 2048 + (it + 1) * 128, :], 0))
            for it in range(8):
                r0 = s_i * 1024 + it * 128
                units.append((w2src[it * 128:(it + 1) * 128, 0:1024],
                              d["w2n_s"][r0:r0 + 128, 0:1024], 1))
                units.append((w2src[it * 128:(it + 1) * 128, 1024:2048],
                              d["w2n_s"][r0:r0 + 128, 1024:2048], 2))
        self.q_units = units
        self.q_pos = 0
        self.q_qmark = {}
        self.q_stmark = {}

    def emit_q(self, k):
        block = self.block
        si, sg, sv, sa = self.si, self.sg, self.sv, self.sa
        n = 0
        while n < k and self.q_pos < len(self.q_units):
            u = self.q_pos
            src, dst, kind = self.q_units[u]
            p = u % 2
            rb = self.q_wraw[:, p * 1024:(p + 1) * 1024]
            ab = self.q_wab[:, p * 1024:(p + 1) * 1024]
            gb2 = self.q_wsgn[:, p * 1024:(p + 1) * 1024]
            ob = self.q_wout[:, p * 1024:(p + 1) * 1024]
            red, thr, mcol = self.q_red, self.q_thr, self.q_mcol
            rc = slice(0, 1) if kind == 1 else (slice(1, 2) if kind == 2 else slice(p, p + 1))

            @block.gpsimd
            def _(g, src=src, rb=rb, u=u):
                if u - 2 in self.q_qmark:
                    g.wait_ge(sv.h, self.q_qmark[u - 2])
                self.minc(g.dma_start(out=rb, in_=src), sg, 16)
            ld = sg.n

            @block.scalar
            def _(s, rb=rb, ab=ab, gb2=gb2, rc=rc, u=u):
                s.wait_ge(sg.h, ld)
                if u - 2 in self.q_qmark:
                    s.wait_ge(sv.h, self.q_qmark[u - 2])
                s.activation(ab, rb, AF.Abs, accum_out=red[:, rc])
                s.activation(gb2, rb, AF.Sign)
                self.dinc(s, sa, 1)
            absd = sa.n

            if kind != 1:
                @block.vector
                def _(v, u=u, kind=kind, rc=rc, p=p):
                    v.wait_ge(sa.h, absd)
                    if u - 2 in self.q_stmark:
                        v.wait_ge(sg.h, self.q_stmark[u - 2])
                    if kind == 2 and u - 3 in self.q_stmark:
                        v.wait_ge(sg.h, self.q_stmark[u - 3])
                    FD = 1024 if kind == 0 else 2048
                    if kind == 2:
                        v.tensor_tensor(red[:, 1:2], red[:, 0:1], red[:, 1:2], ALU.add)
                        v.drain()
                    v.tensor_scalar(thr[:, rc], red[:, rc], 0.5 / FD, 5e-6, ALU.mult, ALU.max)
                    v.tensor_scalar(mcol[:, rc], red[:, rc], 1.0 / FD, 1e-5, ALU.mult, ALU.max)
                    v.drain()
                    if kind == 0:
                        halves = [p]
                    else:
                        halves = [0, 1]
                    for hp in halves:
                        v.tensor_scalar(self.q_wab[:, hp * 1024:(hp + 1) * 1024],
                                        self.q_wab[:, hp * 1024:(hp + 1) * 1024],
                                        thr[:, rc], None, ALU.is_gt)
                    v.drain()
                    for hp in halves:
                        v.scalar_tensor_tensor(self.q_wout[:, hp * 1024:(hp + 1) * 1024],
                                               self.q_wab[:, hp * 1024:(hp + 1) * 1024],
                                               mcol[:, rc],
                                               self.q_wsgn[:, hp * 1024:(hp + 1) * 1024],
                                               ALU.mult, ALU.mult)
                    self.dinc(v, sv, 1)
                qm = sv.n
                self.q_qmark[u] = qm
                if kind == 2:
                    self.q_qmark[u - 1] = qm

                lo = [u] if kind == 0 else [u - 1, u]
                for uu in lo:
                    @block.gpsimd
                    def _(g, uu=uu, qm=qm):
                        g.wait_ge(sv.h, qm)
                        ssrc = self.q_wout[:, (uu % 2) * 1024:((uu % 2) + 1) * 1024]
                        self.minc(g.dma_start(out=self.q_units[uu][1], in_=ssrc), sg, 16)
                    self.q_stmark[uu] = sg.n

            self.q_pos += 1
            n += 1

    def phase_b(self, ctx):
        nc, d, block = self.nc, self.d, self.block
        si, sg, sc, sp, sv, sa = self.si, self.sg, self.sc, self.sp, self.sv, self.sa
        bank = self.bank
        sb = lambda name, shape, dt: ctx.enter_context(nc.sbuf_tensor(name, shape, dt))

        hT = sb("b_hT", [128, 8 * D], BF16)
        w1T = sb("b_w1T", [128, 2 * 8 * 2048], BF16)
        w2T = sb("b_w2T", [128, 2 * 16 * D], BF16)
        acc = sb("b_acc", [128, 8 * D], F32)
        mid = sb("b_mid", [128, 16 * 256], BF16)
        xph = sb("b_xph", [128, D], F32)
        scrB = sb("b_scrB", [128, D], F32)
        hb = sb("b_hb", [128, D], BF16)
        u1 = scrB
        rsb = sb("b_rsb", [8, D], BF16)
        rT = sb("b_rT", [128, 64], BF16)
        self.fgw = sb("b_fgw", [128, D], F32)
        self.fgb = sb("b_fgb", [128, D], F32)
        lgs = sb("b_lgs", [128, 8], F32)
        gates = sb("b_gates", [128, 8 * 8], F32)
        st8 = sb("b_st8", [128, 8], F32)
        sq8 = sb("b_sq8", [128, 8], F32)
        mu8 = sb("b_mu8", [128, 8], F32)
        va8 = sb("b_va8", [128, 8], F32)
        tA8 = sb("b_tA8", [128, 8], F32)
        tB8 = sb("b_tB8", [128, 8], F32)
        be8 = sb("b_be8", [128, 8], F32)
        nm8 = sb("b_nm8", [128, 8], F32)
        sxs = sb("b_sxs", [128, 4], F32)

        @block.gpsimd
        def _(g):
            self.minc(g.dma_start(out=rsb[:], in_=d["router"]), sg, 16)
        r_ld = sg.n

        @block.sync
        def _(sync):
            self.minc(sync.dma_start(out=self.fgw[:], in_=d["fgw_rep"]), si, 16)
            self.minc(sync.dma_start(out=self.fgb[:], in_=d["fgb_rep"]), si, 16)
        fg_ld = si.n

        @block.tensor
        def _(t):
            t.wait_ge(sg.h, r_ld)
            t.wait_ge(si.h, 16)
            self.bank_wait(t, 3)
            for dc in range(8):
                self.minc(t.matmul(bank[3][:, dc * 8:(dc + 1) * 8],
                                   rsb[0:8, dc * 128:(dc + 1) * 128],
                                   self.ident[0:8, 0:8], start=True, stop=True), sp, 1)
        rt_mm = sp.n

        @block.scalar
        def _(s):
            s.wait_ge(sp.h, rt_mm)
            s.copy(rT[:], bank[3][:, 0:64])
            self.dinc(s, sa, 1)
        self.bank_set(3, sa)
        rt_done = sa.n

        spec_done = {}
        for hf in range(2):
            for tc in range(8):
                row0 = hf * 1024 + tc * 128

                @block.sync
                def _(sync, row0=row0, tc=tc, hf=hf):
                    if hf > 0 or tc > 0:
                        sync.wait_ge(sv.h, sv.n)
                    self.minc(sync.dma_start(out=xph[:], in_=d["xp_s"][row0:row0 + 128, :]), si, 16)
                ld = si.n

                @block.scalar
                def _(s, tc=tc):
                    s.wait_ge(si.h, ld)
                    s.activation(scrB[:], xph[:], AF.Square, accum_out=sq8[:, tc:tc + 1])
                    s.activation(scrB[:], xph[:], AF.Identity, accum_out=st8[:, tc:tc + 1])
                    self.dinc(s, sa, 1)
                std = sa.n

                @block.vector
                def _(v, tc=tc):
                    v.wait_ge(sa.h, std)
                    cl = slice(tc, tc + 1)
                    v.tensor_scalar(mu8[:, cl], st8[:, cl], 1.0 / D, None, ALU.mult)
                    v.drain()
                    v.tensor_tensor(tA8[:, cl], mu8[:, cl], mu8[:, cl], ALU.mult)
                    v.drain()
                    v.scalar_tensor_tensor(va8[:, cl], sq8[:, cl], 1.0 / D, tA8[:, cl],
                                           ALU.mult, ALU.subtract)
                    v.drain()
                    v.tensor_scalar(tA8[:, cl], va8[:, cl], EPS, None, ALU.add)
                    self.dinc(v, sv, 1)
                ve = sv.n

                @block.scalar
                def _(s, tc=tc):
                    s.wait_ge(sv.h, ve)
                    cl = slice(tc, tc + 1)
                    s.activation(tB8[:, cl], tA8[:, cl], AF.Ln)
                    s.drain()
                    s.activation(tB8[:, cl], tB8[:, cl], AF.Exp, scale=-0.5)
                    self.dinc(s, sa, 1)
                a1d = sa.n

                @block.vector
                def _(v, tc=tc):
                    v.wait_ge(sa.h, a1d)
                    cl = slice(tc, tc + 1)
                    v.reciprocal(be8[:, cl], tA8[:, cl])
                    v.drain()
                    v.tensor_tensor(be8[:, cl], va8[:, cl], be8[:, cl], ALU.mult)
                    v.drain()
                    v.tensor_scalar(be8[:, cl], be8[:, cl], EPS, None, ALU.add)
                    self.dinc(v, sv, 1)
                v1d = sv.n

                @block.scalar
                def _(s, tc=tc):
                    s.wait_ge(sv.h, v1d)
                    cl = slice(tc, tc + 1)
                    s.activation(nm8[:, cl], be8[:, cl], AF.Ln)
                    s.drain()
                    s.activation(nm8[:, cl], nm8[:, cl], AF.Exp, scale=-0.5)
                    self.dinc(s, sa, 1)
                a2d = sa.n

                @block.vector
                def _(v, tc=tc):
                    v.wait_ge(sa.h, a2d)
                    v.wait_ge(si.h, fg_ld)
                    cl = slice(tc, tc + 1)
                    v.tensor_tensor(be8[:, cl], tB8[:, cl], nm8[:, cl], ALU.mult)
                    v.drain()
                    v.scalar_tensor_tensor(nm8[:, cl], mu8[:, cl], -1.0, be8[:, cl],
                                           ALU.mult, ALU.mult)
                    v.drain()
                    v.tensor_scalar(u1[:], xph[:], be8[:, cl], nm8[:, cl], ALU.mult, ALU.add)
                    v.drain()
                    v.tensor_tensor(u1[:], u1[:], self.fgw[:], ALU.mult)
                    v.drain()
                    v.tensor_tensor(hb[:], u1[:], self.fgb[:], ALU.add)
                    self.dinc(v, sv, 1)
                hbd = sv.n

                for grp in range(2):
                    bi = grp % 2
                    @block.tensor
                    def _(t, grp=grp, bi=bi, tc=tc):
                        t.wait_ge(sv.h, hbd)
                        self.bank_wait(t, bi)
                        for j in range(4):
                            dc = grp * 4 + j
                            self.minc(t.matmul(bank[bi][:, j * 128:(j + 1) * 128],
                                               hb[:, dc * 128:(dc + 1) * 128], self.ident[:],
                                               start=True, stop=True), sp, 1)
                    pe_cnt = sp.n

                    @block.scalar
                    def _(s, grp=grp, bi=bi, tc=tc, pe_cnt=pe_cnt, hf=hf):
                        s.wait_ge(sp.h, pe_cnt)
                        if hf > 0 and tc == 0 and grp == 0:
                            s.wait_ge(sp.h, sp.n)
                        for j in range(4):
                            dc = grp * 4 + j
                            s.copy(hT[:, dc * D + tc * 128:dc * D + (tc + 1) * 128],
                                   bank[bi][:, j * 128:(j + 1) * 128])
                        self.dinc(s, sa, 1)
                    self.bank_set(bi, sa)
                hT_tc = sa.n

                @block.tensor
                def _(t, tc=tc):
                    t.wait_ge(sa.h, max(hT_tc, rt_done))
                    self.bank_wait(t, 2)
                    for dc in range(8):
                        self.minc(t.matmul(bank[2][:, 0:8],
                                           hT[:, dc * D + tc * 128:dc * D + (tc + 1) * 128],
                                           rT[:, dc * 8:(dc + 1) * 8],
                                           start=(dc == 0), stop=(dc == 7)), sp, 1)
                lg_mm = sp.n

                @block.vector
                def _(v, tc=tc):
                    v.wait_ge(sp.h, lg_mm)
                    v.tensor_copy(lgs[:], bank[2][:, 0:8])
                    v.drain()
                    v.tensor_reduce(sxs[:, 0:1], lgs[:], AX.X, ALU.max)
                    v.drain()
                    v.tensor_scalar(sxs[:, 1:2], sxs[:, 0:1], -1.0, None, ALU.mult)
                    self.dinc(v, sv, 1)
                self.bank_set(2, sv)
                lmx = sv.n

                @block.scalar
                def _(s, tc=tc):
                    s.wait_ge(sv.h, lmx)
                    s.activation(scrB[:, 0:8], lgs[:], AF.Exp, bias=sxs[:, 1:2],
                                 accum_out=sxs[:, 2:3])
                    self.dinc(s, sa, 1)
                expd = sa.n

                @block.vector
                def _(v, tc=tc):
                    v.wait_ge(sa.h, expd)
                    v.reciprocal(sxs[:, 3:4], sxs[:, 2:3])
                    v.tensor_scalar(lgs[:], lgs[:], sxs[:, 0:1], None, ALU.is_equal)
                    v.drain()
                    v.tensor_scalar(gates[:, tc * 8:(tc + 1) * 8], lgs[:], sxs[:, 3:4],
                                    None, ALU.mult)
                    self.dinc(v, sv, 1)
                gates_done = sv.n

            for s_i in range(B_SPECIES):
                p = 0 if B_SINGLEBUF else s_i % 2
                w1buf = w1T[:, p * 8 * 2048:(p + 1) * 8 * 2048]
                w2buf = w2T[:, p * 16 * D:(p + 1) * 16 * D]

                @block.sync
                def _(sync, s_i=s_i, p=p, w1buf=w1buf, hf=hf):
                    if B_SINGLEBUF:
                        sync.wait_ge(sp.h, sp.n)
                    elif (hf, s_i) != (0, 0) and not (hf == 0 and s_i == 1):
                        key = (hf, s_i - 2) if s_i >= 2 else (hf - 1, 8 - s_i)
                        if key in spec_done:
                            sync.wait_ge(sp.h, spec_done[key])
                    for dc in range(8):
                        self.minc(sync.dma_start(
                            out=w1buf[:, dc * 2048:(dc + 1) * 2048],
                            in_=d["w1n_s"][s_i * 2048:(s_i + 1) * 2048, dc * 128:(dc + 1) * 128],
                            transpose=True), si, 16)
                w1_ld = si.n

                @block.sync
                def _(sync, s_i=s_i, p=p, w2buf=w2buf, hf=hf):
                    for fc in range(16):
                        self.minc(sync.dma_start(
                            out=w2buf[:, fc * D:(fc + 1) * D],
                            in_=d["w2n_s"][s_i * D:(s_i + 1) * D, fc * 128:(fc + 1) * 128],
                            transpose=True), sc, 16)
                w2_ld = sc.n

                for tg in range(4):
                    for fqp in range(2):
                        @block.tensor
                        def _(t, tg=tg, fqp=fqp, s_i=s_i):
                            if tg == 0 and fqp == 0:
                                t.wait_ge(si.h, w1_ld)
                                t.wait_ge(sa.h, hT_tc)
                            for fq in (2 * fqp, 2 * fqp + 1):
                                b0 = (fq % 2) * 2
                                self.bank_wait(t, b0)
                                self.bank_wait(t, b0 + 1)
                                for j in range(4):
                                    fc = fq * 4 + j
                                    dst = bank[b0 + j // 2][:, (j % 2) * 256:(j % 2) * 256 + 256]
                                    for dc in range(8):
                                        self.minc(t.matmul(
                                            dst, w1buf[:, dc * 2048 + fc * 128:dc * 2048 + (fc + 1) * 128],
                                            hT[:, dc * D + tg * 256:dc * D + (tg + 1) * 256],
                                            start=(dc == 0), stop=(dc == 7)), sp, 1)
                        mm_cnt = sp.n

                        @block.scalar
                        def _(s, tg=tg, fqp=fqp, mm_cnt=mm_cnt):
                            s.wait_ge(sp.h, mm_cnt)
                            for fq in (2 * fqp, 2 * fqp + 1):
                                b0 = (fq % 2) * 2
                                s.activation(mid[:, (fq * 4) * 256:(fq * 4) * 256 + 512],
                                             bank[b0][:], AF.Silu)
                                s.activation(mid[:, (fq * 4 + 2) * 256:(fq * 4 + 2) * 256 + 512],
                                             bank[b0 + 1][:], AF.Silu)
                            self.dinc(s, sa, 1)
                        for bi in range(4):
                            self.bank_set(bi, sa)
                    silu_all = sa.n

                    for tc2 in range(2):
                        b0 = 4 + tc2 * 2
                        tgc = tg * 2 + tc2

                        @block.tensor
                        def _(t, tc2=tc2, b0=b0, tgc=tgc, s_i=s_i):
                            t.wait_ge(sa.h, silu_all)
                            if tg == 0 and tc2 == 0:
                                t.wait_ge(sc.h, w2_ld)
                            self.bank_wait(t, b0)
                            self.bank_wait(t, b0 + 1)
                            for fc in range(16):
                                lh = mid[:, fc * 256 + tc2 * 128:fc * 256 + (tc2 + 1) * 128]
                                self.minc(t.matmul(bank[b0][:], lh, w2buf[:, fc * D:fc * D + 512],
                                                   start=(fc == 0), stop=(fc == 15)), sp, 1)
                                self.minc(t.matmul(bank[b0 + 1][:], lh, w2buf[:, fc * D + 512:(fc + 1) * D],
                                                   start=(fc == 0), stop=(fc == 15)), sp, 1)
                        eo_mm = sp.n

                        @block.vector
                        def _(v, tc2=tc2, b0=b0, tgc=tgc, s_i=s_i):
                            v.wait_ge(sp.h, eo_mm)
                            if s_i == 0:
                                v.wait_ge(sv.h, gates_done)
                            asl = acc[:, tgc * D:(tgc + 1) * D]
                            for half in range(2):
                                pslice = bank[b0 + half][:]
                                osl = acc[:, tgc * D + half * 512:tgc * D + (half + 1) * 512]
                                if s_i == 0:
                                    v.tensor_scalar(osl, pslice, 1.0, None, ALU.mult)
                                else:
                                    v.scalar_tensor_tensor(osl, pslice,
                                                           gates[:, tgc * 8 + s_i - 1:tgc * 8 + s_i],
                                                           osl, ALU.mult, ALU.add)
                            self.dinc(v, sv, 1)
                        self.bank_set(b0, sv)
                        self.bank_set(b0 + 1, sv)
                spec_done[(hf, s_i)] = sp.n

            for tc in range(8):
                row0 = hf * 1024 + tc * 128
                accd = sv.n

                @block.scalar
                def _(s, tc=tc):
                    s.wait_ge(sv.h, accd)
                    asl = acc[:, tc * D:(tc + 1) * D]
                    s.activation(scrB[:], asl, AF.Square, accum_out=sq8[:, tc:tc + 1])
                    s.activation(scrB[:], asl, AF.Identity, accum_out=st8[:, tc:tc + 1])
                    self.dinc(s, sa, 1)
                std = sa.n

                @block.sync
                def _(sync, row0=row0, tc=tc):
                    sync.wait_ge(sv.h, sv.n)
                    sync.wait_ge(self.so.h, self.so.n)
                    self.minc(sync.dma_start(out=xph[:], in_=d["xp_s"][row0:row0 + 128, :]), si, 16)
                xld = si.n

                @block.vector
                def _(v, tc=tc):
                    v.wait_ge(sa.h, std)
                    cl = slice(tc, tc + 1)
                    v.tensor_scalar(mu8[:, cl], st8[:, cl], 1.0 / D, None, ALU.mult)
                    v.drain()
                    v.tensor_tensor(tA8[:, cl], mu8[:, cl], mu8[:, cl], ALU.mult)
                    v.drain()
                    v.scalar_tensor_tensor(va8[:, cl], sq8[:, cl], 1.0 / D, tA8[:, cl],
                                           ALU.mult, ALU.subtract)
                    v.drain()
                    v.tensor_scalar(va8[:, cl], va8[:, cl], EPS, None, ALU.add)
                    self.dinc(v, sv, 1)
                ved = sv.n

                @block.scalar
                def _(s, tc=tc):
                    s.wait_ge(sv.h, ved)
                    cl = slice(tc, tc + 1)
                    s.activation(va8[:, cl], va8[:, cl], AF.Ln)
                    s.drain()
                    s.activation(va8[:, cl], va8[:, cl], AF.Exp, scale=-0.5)
                    self.dinc(s, sa, 1)
                ald = sa.n

                @block.vector
                def _(v, tc=tc):
                    v.wait_ge(sa.h, ald)
                    v.wait_ge(si.h, xld)
                    cl = slice(tc, tc + 1)
                    v.scalar_tensor_tensor(nm8[:, cl], mu8[:, cl], -1.0, va8[:, cl],
                                           ALU.mult, ALU.mult)
                    v.drain()
                    v.tensor_scalar(xph[:], xph[:], 1.0, nm8[:, cl], ALU.mult, ALU.add)
                    v.drain()
                    v.scalar_tensor_tensor(xph[:], acc[:, tc * D:(tc + 1) * D], va8[:, cl],
                                           xph[:], ALU.mult, ALU.add)
                    self.dinc(v, sv, 1)
                outd = sv.n

                @block.sync
                def _(sync, row0=row0, outd=outd):
                    sync.wait_ge(sv.h, outd)
                    self.minc(sync.dma_start(out=d["out"][row0:row0 + 128, :], in_=xph[:]), self.so, 16)
                if hf == 1 and tc == 7:
                    self.dbg("xph_last", xph[:])
            if hf == 0:
                self.dbg("gates0", gates[:])
                self.dbg("acc0", acc[:, 0:D])
                self.dbg("hT0", hT[:, 0:D])
                self.dbg("w1T0", w1T[:, 0:2048])
                self.dbg("mid0", mid[:, 0:1024])
            else:
                self.dbg("accF", acc[:])
                self.dbg("gatesF", gates[:])

        @block.sync
        def _(sync):
            sync.wait_ge(self.so.h, self.so.n)

    def emit_dumps(self):
        d, block = self.d, self.block
        if not self.dump:
            return

        @block.sync
        def _(sync):
            for s in (self.sg, self.sv, self.sa, self.sp, self.sc):
                sync.wait_ge(s.h, s.n)
            for name in self.dump:
                if name.startswith("@"):
                    continue
                self.minc(sync.dma_start(out=d["dump_" + name], in_=d[name]), self.so, 16)
            sync.wait_ge(self.so.h, self.so.n)


def _host_constants():
    g = 1.0 - np.exp2(-5.0 - np.arange(H, dtype=np.float64))
    i = np.arange(C, dtype=np.float64)
    dmask = np.where(i[:, None] >= i[None, :], g[:, None, None] ** (i[:, None] - i[None, :]), 0.0)
    dmaskT = dmask.transpose(0, 2, 1)
    dmaskT = np.ascontiguousarray(dmaskT.transpose(1, 0, 2).reshape(C, 4 * C))
    cross = g[:, None] ** (i + 1.0)
    crossrep = np.broadcast_to(cross.reshape(1, 4 * C), (128, 4 * C))
    kdec = g[:, None] ** (C - 1.0 - i)
    kdecP = np.ascontiguousarray(kdec.T)
    inv_freq = 1.0 / 10000.0 ** (np.arange(0, DK, 2, dtype=np.float64) / DK)
    freqs = np.arange(T, dtype=np.float64)[:, None] * inv_freq[None, :]
    cosT = np.cos(freqs).T
    sinT = np.sin(freqs).T
    ident = np.eye(128, dtype=np.float32)
    f32 = lambda a: np.ascontiguousarray(np.asarray(a, np.float64), dtype=np.float32)
    bf = lambda a: np.ascontiguousarray(np.asarray(a, np.float32).astype(ml_dtypes.bfloat16))
    return {"dmaskT": f32(dmaskT), "crossrep": bf(crossrep), "kdecP": f32(kdecP),
            "cosT": bf(cosT), "sinT": bf(sinT), "ident": bf(ident)}


def _rep128(vec):
    v = np.asarray(vec, np.float32)
    return np.ascontiguousarray(np.broadcast_to(v[None, :], (128, v.shape[0])))


def kernel(**inputs):
    consts = _host_constants()
    gnw = np.asarray(inputs["L0_attn_g_norm_w"], np.float32)
    gnwP = np.ascontiguousarray(np.tile(gnw, 4).reshape(16, 128).T)
    f32c = lambda a: np.ascontiguousarray(np.asarray(a, np.float32))

    shared_in = {
        "Wq": f32c(inputs["L0_attn_Wq"]), "Wk": f32c(inputs["L0_attn_Wk"]),
        "Wv": f32c(inputs["L0_attn_Wv"]), "Wg": f32c(inputs["L0_attn_Wg"]),
        "Wo": f32c(inputs["L0_attn_Wo"]),
        "lnw_rep": _rep128(inputs["L0_attn_ln_w"]),
        "lnb_rep": _rep128(inputs["L0_attn_ln_b"]),
        "fgw_rep": _rep128(inputs["L0_ffn_ln_gamma"]),
        "fgb_rep": _rep128(inputs["L0_ffn_ln_beta"]),
        "gnwP": gnwP,
        "router": f32c(inputs["L0_ffn_router_w"]),
        "sw1": f32c(inputs["L0_ffn_shared_w1"]), "sw2": f32c(inputs["L0_ffn_shared_w2"]),
        "ew1": f32c(inputs["L0_ffn_expert_w1"]).reshape(8 * 2 * D, D),
        "ew2": f32c(inputs["L0_ffn_expert_w2"]).reshape(8 * D, 2 * D),
        **consts,
    }
    x = np.asarray(inputs["L0_attn_x"], np.float32)
    in_maps = [dict(shared_in, x=np.ascontiguousarray(x[c])) for c in range(N_CORES)]

    b = Build(DUMP)
    nc = b.build()
    res = run_bass_kernel_spmd(nc, in_maps, core_ids=list(range(N_CORES)))
    if DUMP:
        kernel.last_results = res.results
    return np.stack([np.asarray(res.results[c]["out"]) for c in range(N_CORES)], axis=0)


# version 10
# speedup vs baseline: 1.2951x; 1.2951x over previous
import contextlib

import numpy as np
import ml_dtypes

import concourse.bass as bass
import concourse.mybir as mybir
from concourse.bass_utils import run_bass_kernel_spmd

F32 = mybir.dt.float32
BF16 = mybir.dt.bfloat16
AF = mybir.ActivationFunctionType
ALU = mybir.AluOpType
AX = mybir.AxisListType

T, D, H, C = 2048, 1024, 4, 128
NCH = T // C
DK, DV = D // H, 2 * D // H
EPS = 1e-5
N_CORES = 8

DUMP = ()
B_SPECIES = 9
B_SINGLEBUF = False


class Sem:
    def __init__(self, h):
        self.h = h
        self.n = 0

    def bump(self, k=1):
        self.n += k
        return self.n


class Build:
    def __init__(self, dump=()):
        self.dump = tuple(dump)
        nc = bass.Bass("TRN2", target_bir_lowering=False)
        self.nc = nc
        d = self.d = {}

        def din(name, shape, dt=F32):
            d[name] = nc.dram_tensor(name, shape, dt, kind="ExternalInput").ap()

        din("x", [T, D])
        din("Wq", [D, D]); din("Wk", [D, D]); din("Wv", [D, 2 * D]); din("Wg", [D, 2 * D])
        din("Wo", [2 * D, D])
        din("lnw_rep", [128, D]); din("lnb_rep", [128, D])
        din("fgw_rep", [128, D]); din("fgb_rep", [128, D])
        din("gnwP", [128, 16])
        din("router", [8, D])
        din("sw1", [2 * D, D]); din("sw2", [D, 2 * D])
        din("ew1", [8 * 2 * D, D]); din("ew2", [8 * D, 2 * D])
        din("cosT", [128, T], BF16); din("sinT", [128, T], BF16)
        din("dmaskT", [128, 4 * 128])
        din("crossrep", [128, 4 * 128], BF16)
        din("kdecP", [128, 4])
        din("ident", [128, 128], BF16)
        d["out"] = nc.dram_tensor("out", [T, D], F32, kind="ExternalOutput").ap()

        def scr(name, shape, dt):
            d[name] = nc.dram_tensor(name, shape, dt, kind="Internal").ap()

        scr("qT_s", [D, T], BF16)
        scr("kT_s", [D, T], BF16)
        scr("kk_s", [T, D], BF16)
        scr("v_s", [T, 2 * D], BF16)
        scr("g_s", [T, 2 * D], BF16)
        scr("xp_s", [T, D], F32)
        scr("w1n_s", [9 * 2 * D, D], BF16)
        scr("w2n_s", [9 * D, 2 * D], BF16)

        for name in self.dump:
            if name.startswith("@"):
                continue
            d["dump_" + name] = nc.dram_tensor(
                "dump_" + name, list(d[name].shape), d[name].dtype, kind="ExternalOutput").ap()

        g64 = 1.0 - np.exp2(-5.0 - np.arange(H, dtype=np.float64))
        self.gC = [float(np.float32(v)) for v in g64 ** C]

    def dinc(self, eng, sem, k=1):
        eng.maybe_drain_then_inc((sem.h, k))
        sem.bump(k)

    def minc(self, inst, sem, k=1):
        inst.then_inc(sem.h, k)
        sem.bump(k)

    def dbg(self, name, ap):
        if "@" + name not in self.dump:
            return
        nc, block = self.nc, self.block
        shape = [int(s) for s in ap.shape]
        out = nc.dram_tensor("dump_@" + name, shape, ap.dtype, kind="ExternalOutput").ap()

        @block.gpsimd
        def _(g):
            for s in (self.sv, self.sa, self.sp, self.si, self.sg, self.sc):
                g.wait_ge(s.h, s.n)
            self.minc(g.dma_start(out=out, in_=ap), self.so, 16)

    def bank_wait(self, eng, i):
        g = self.bfree.get(i)
        if g is not None:
            eng.wait_ge(g[0].h, g[1])

    def bank_set(self, i, sem):
        self.bfree[i] = (sem, sem.n)

    def barrier(self):
        block = self.block
        for name, sem in (("tensor", self.sp), ("vector", self.sv),
                          ("scalar", self.sa), ("gpsimd", self.sg)):
            def _f(eng, sem=sem):
                self.dinc(eng, sem, 1)
            getattr(block, name)(_f)
        tot = [(s.h, s.n) for s in (self.si, self.sg, self.sc, self.sp, self.sv, self.sa)]
        for name in ("tensor", "vector", "scalar", "gpsimd", "sync"):
            def _f(eng, tot=tot):
                for h, n in tot:
                    eng.wait_ge(h, n)
            getattr(block, name)(_f)

    def build(self):
        nc, d = self.nc, self.d
        with contextlib.ExitStack() as ctx:
            E = ctx.enter_context
            sb = lambda name, shape, dt: E(nc.sbuf_tensor(name, shape, dt))
            self.bank = [E(nc.psum_tensor(f"bank{i}", [128, 512], F32)) for i in range(8)]
            self.bfree = {}

            self.si = Sem(E(nc.semaphore("si")))
            self.sg = Sem(E(nc.semaphore("sg")))
            self.sc = Sem(E(nc.semaphore("sc")))
            self.sp = Sem(E(nc.semaphore("sp")))
            self.sv = Sem(E(nc.semaphore("sv")))
            self.sa = Sem(E(nc.semaphore("sa")))
            self.so = Sem(E(nc.semaphore("so")))
            self.sq = Sem(E(nc.semaphore("sq")))
            block = self.block = E(nc.Block())

            self.ident = sb("s_ident", [128, 128], BF16)
            self.a1v = sb("s_a1v", [128, NCH], F32)
            self.nma1 = sb("s_nma1", [128, NCH], F32)

            si = self.si

            @block.sync
            def _(sync):
                self.minc(sync.dma_start(out=self.ident[:], in_=d["ident"]), si, 16)
            self.consts_loaded = si.n

            with contextlib.ExitStack() as pctx:
                self.phase_a1(pctx)
            self.barrier()
            with contextlib.ExitStack() as pctx:
                self.phase_a2(pctx)
            self.barrier()
            with contextlib.ExitStack() as qctx:
                self.init_q(qctx)
                self.emit_q(10 ** 9)
            self.barrier()
            with contextlib.ExitStack() as pctx:
                self.phase_b(pctx)
            self.emit_dumps()
        return nc

    def phase_a1(self, ctx):
        nc, d, block = self.nc, self.d, self.block
        si, sg, sp, sv, sa = self.si, self.sg, self.sp, self.sv, self.sa
        bank = self.bank
        sb = lambda name, shape, dt: ctx.enter_context(nc.sbuf_tensor(name, shape, dt))

        wq = sb("a_wq", [128, 8 * D], BF16)
        wk = sb("a_wk", [128, 8 * D], BF16)
        wv = sb("a_wv", [128, 8 * 2 * D], BF16)
        wg = sb("a_wg", [128, 8 * 2 * D], BF16)
        cosT = sb("a_cosT", [128, 512], BF16)
        sinT = sb("a_sinT", [128, 512], BF16)
        xin = sb("a_xin", [128, 4 * D], F32)
        s1 = sb("a_s1", [128, NCH], F32)
        ssq = sb("a_ssq", [128, NCH], F32)
        mu = sb("a_mu", [128, NCH], F32)
        var = sb("a_var", [128, NCH], F32)
        tA = sb("a_tA", [128, NCH], F32)
        tB = sb("a_tB", [128, NCH], F32)
        beta = sb("a_beta", [128, NCH], F32)
        nmb = sb("a_nmb", [128, NCH], F32)
        u1 = sb("a_u1", [128, 4 * D], F32)
        scr = u1[:, 0:D]
        nrm = sb("a_nrm", [128, 4 * D], BF16)
        nrmT = sb("a_nrmT", [128, 8 * 512], BF16)
        praw = sb("a_praw", [128, 8 * 512], BF16)
        ra = sb("a_ra", [128, 1024], BF16)
        rb = sb("a_rb", [128, 1024], BF16)
        pT = praw
        kkb = sb("a_kkb", [128, 4 * D], BF16)
        vgb = sb("a_vgb", [128, 4 * 512], BF16)

        kdecP = sb("a_kdecP", [128, 4], F32)
        self.lnw = sb("a_lnw", [128, D], F32)
        self.lnb = sb("a_lnb", [128, D], F32)

        @block.sync
        def _(sync):
            self.minc(sync.dma_start(out=kdecP[:], in_=d["kdecP"]), si, 16)
            self.minc(sync.dma_start(out=self.lnw[:], in_=d["lnw_rep"]), si, 16)
            self.minc(sync.dma_start(out=self.lnb[:], in_=d["lnb_rep"]), si, 16)
        consts2 = si.n

        xfree = {}
        trig_free = {}

        @block.gpsimd
        def _(g):
            for name, dst, cn in (("Wq", wq, 8), ("Wk", wk, 8), ("Wv", wv, 8), ("Wg", wg, 8)):
                self.minc(g.dma_start(out=dst[:].rearrange("p (c f) -> p c f", c=cn),
                                      in_=d[name].rearrange("(c p) f -> p c f", p=128)), sg, 16)
        w_loaded = sg.n

        for tt in range(4):
            xbuf = xin
            cols = slice(tt * 4, tt * 4 + 4)

            @block.sync
            def _(sync, tt=tt, xbuf=xbuf):
                if tt >= 1:
                    sync.wait_ge(sv.h, xfree[tt - 1])
                    sync.wait_ge(sv.h, trig_free[tt - 1])
                src = d["x"].rearrange("(c p) f -> p c f", p=128)[:, tt * 4:tt * 4 + 4, :]
                self.minc(sync.dma_start(out=xbuf.rearrange("p (c f) -> p c f", c=4), in_=src), si, 16)
                self.minc(sync.dma_start(out=cosT[:], in_=d["cosT"][:, tt * 512:(tt + 1) * 512]), si, 16)
                self.minc(sync.dma_start(out=sinT[:], in_=d["sinT"][:, tt * 512:(tt + 1) * 512]), si, 16)
            x_loaded = si.n
            trig_loaded = si.n

            @block.scalar
            def _(s, tt=tt, xbuf=xbuf):
                s.wait_ge(si.h, x_loaded)
                for c in range(4):
                    tc = tt * 4 + c
                    xc = xbuf[:, c * D:(c + 1) * D]
                    s.activation(scr[:], xc, AF.Square, accum_out=ssq[:, tc:tc + 1])
                    s.activation(scr[:], xc, AF.Identity, accum_out=s1[:, tc:tc + 1])
                self.dinc(s, sa, 1)
            stats_done = sa.n
            if tt == 0:
                self.dbg("xin", xin[:])
                self.dbg("s1", s1[:])
                self.dbg("ssq", ssq[:])

            @block.vector
            def _(v, cols=cols):
                v.wait_ge(sa.h, stats_done)
                cl = cols
                v.tensor_scalar(mu[:, cl], s1[:, cl], 1.0 / D, None, ALU.mult)
                v.drain()
                v.tensor_tensor(tA[:, cl], mu[:, cl], mu[:, cl], ALU.mult)
                v.drain()
                v.scalar_tensor_tensor(var[:, cl], ssq[:, cl], 1.0 / D, tA[:, cl], ALU.mult, ALU.subtract)
                v.drain()
                v.tensor_scalar(tA[:, cl], var[:, cl], EPS, None, ALU.add)
                self.dinc(v, sv, 1)
            ve_done = sv.n

            @block.scalar
            def _(s, cols=cols):
                s.wait_ge(sv.h, ve_done)
                s.activation(tB[:, cols], tA[:, cols], AF.Ln)
                s.drain()
                s.activation(tB[:, cols], tB[:, cols], AF.Exp, scale=0.5)
                self.dinc(s, sa, 1)
            sd1_done = sa.n

            @block.vector
            def _(v, cols=cols):
                cl = cols
                v.wait_ge(sa.h, sd1_done)
                v.reciprocal(self.a1v[:, cl], tB[:, cl])
                v.reciprocal(tB[:, cl], tA[:, cl])
                v.drain()
                v.tensor_tensor(tB[:, cl], var[:, cl], tB[:, cl], ALU.mult)
                v.drain()
                v.tensor_scalar(tB[:, cl], tB[:, cl], EPS, None, ALU.add)
                self.dinc(v, sv, 1)
            v1_done = sv.n

            @block.scalar
            def _(s, cols=cols):
                s.wait_ge(sv.h, v1_done)
                s.activation(tA[:, cols], tB[:, cols], AF.Ln)
                s.drain()
                s.activation(tA[:, cols], tA[:, cols], AF.Exp, scale=0.5)
                self.dinc(s, sa, 1)
            sd2_done = sa.n

            @block.vector
            def _(v, cols=cols, tt=tt, xbuf=xbuf):
                cl = cols
                v.wait_ge(sa.h, sd2_done)
                if tt > 0:
                    v.wait_ge(sp.h, sp.n)
                v.wait_ge(si.h, self.consts_loaded)
                v.reciprocal(tB[:, cl], tA[:, cl])
                v.drain()
                v.tensor_tensor(beta[:, cl], self.a1v[:, cl], tB[:, cl], ALU.mult)
                v.drain()
                v.scalar_tensor_tensor(nmb[:, cl], mu[:, cl], -1.0, beta[:, cl], ALU.mult, ALU.mult)
                v.scalar_tensor_tensor(self.nma1[:, cl], mu[:, cl], -1.0, self.a1v[:, cl], ALU.mult, ALU.mult)
                v.drain()
                for c in range(4):
                    tc = tt * 4 + c
                    v.tensor_scalar(u1[:, c * D:(c + 1) * D], xbuf[:, c * D:(c + 1) * D],
                                    beta[:, tc:tc + 1], nmb[:, tc:tc + 1], ALU.mult, ALU.add)
                v.drain()
                for c in range(4):
                    v.tensor_tensor(u1[:, c * D:(c + 1) * D], u1[:, c * D:(c + 1) * D],
                                    self.lnw[:], ALU.mult)
                v.drain()
                for c in range(4):
                    v.tensor_tensor(nrm[:, c * D:(c + 1) * D], u1[:, c * D:(c + 1) * D],
                                    self.lnb[:], ALU.add)
                self.dinc(v, sv, 1)
            nrm_done = sv.n
            xfree[tt] = nrm_done
            if tt == 0:
                self.dbg("beta", beta[:])
                self.dbg("nmb", nmb[:])
                self.dbg("nrm", nrm[:])
                self.dbg("mu", mu[:])
                self.dbg("var", var[:])
                self.dbg("a1v", self.a1v[:])

            for grp in range(8):
                bi = grp % 2
                bk = bank[bi]

                @block.tensor
                def _(t, grp=grp, bk=bk, bi=bi):
                    t.wait_ge(sv.h, nrm_done)
                    if grp == 0:
                        t.wait_ge(si.h, 16)
                    self.bank_wait(t, bi)
                    for j in range(4):
                        i = grp * 4 + j
                        c, dc = i // 8, i % 8
                        self.minc(t.matmul(bk[:, j * 128:(j + 1) * 128],
                                           nrm[:, c * D + dc * 128:c * D + (dc + 1) * 128],
                                           self.ident[:], start=True, stop=True), sp, 1)
                pe_cnt = sp.n

                @block.scalar
                def _(s, grp=grp, bk=bk):
                    s.wait_ge(sp.h, pe_cnt)
                    for j in range(4):
                        i = grp * 4 + j
                        c, dc = i // 8, i % 8
                        s.copy(nrmT[:, dc * 512 + c * 128:dc * 512 + (c + 1) * 128],
                               bk[:, j * 128:(j + 1) * 128])
                    self.dinc(s, sa, 1)
                self.bank_set(bi, sa)
            nrmT_ready = sa.n

            for which, wmat, dst_s, scale in (("q", wq, "qT_s", DK ** -0.5),
                                              ("k", wk, "kT_s", 1.0)):
                for fp in range(4):
                    @block.tensor
                    def _(t, fp=fp, wmat=wmat, which=which, tt=tt):
                        if which == "q" and fp == 0:
                            t.wait_ge(sa.h, nrmT_ready)
                        if which == "q" and tt == 0 and fp == 0:
                            t.wait_ge(sg.h, w_loaded)
                        for half in range(2):
                            fc = fp * 2 + half
                            self.bank_wait(t, 2 + half)
                            for kc in range(8):
                                self.minc(t.matmul(bank[2 + half][:],
                                                   wmat[:, kc * D + fc * 128:kc * D + (fc + 1) * 128],
                                                   nrmT[:, kc * 512:(kc + 1) * 512],
                                                   start=(kc == 0), stop=(kc == 7)), sp, 1)
                    pe_cnt = sp.n

                    @block.scalar
                    def _(s, fp=fp, scale=scale, pe_cnt=pe_cnt):
                        s.wait_ge(sp.h, pe_cnt)
                        s.wait_ge(sv.h, sv.n)
                        s.wait_ge(sg.h, sg.n)
                        for half in range(2):
                            fc = fp * 2 + half
                            s.activation(praw[:, fc * 512:(fc + 1) * 512], bank[2 + half][:],
                                         AF.Copy, scale=float(scale))
                        self.dinc(s, sa, 1)
                    self.bank_set(2, sa)
                    self.bank_set(3, sa)
                praw_done = sa.n

                @block.vector
                def _(v, tt=tt):
                    v.wait_ge(sa.h, praw_done)
                    v.wait_ge(sg.h, sg.n)
                    v.wait_ge(si.h, trig_loaded)
                    cs = cosT[:]
                    sn = sinT[:]
                    for h in range(H):
                        t1 = praw[:, (2 * h) * 512:(2 * h + 1) * 512]
                        t2 = praw[:, (2 * h + 1) * 512:(2 * h + 2) * 512]
                        v.tensor_tensor(ra[:, 0:512], t1, cs, ALU.mult)
                        v.tensor_tensor(ra[:, 512:1024], t2, sn, ALU.mult)
                        v.tensor_tensor(rb[:, 0:512], t2, cs, ALU.mult)
                        v.tensor_tensor(rb[:, 512:1024], t1, sn, ALU.mult)
                        v.drain()
                        v.tensor_tensor(pT[:, (2 * h) * 512:(2 * h + 1) * 512],
                                        ra[:, 0:512], ra[:, 512:1024], ALU.subtract)
                        v.tensor_tensor(pT[:, (2 * h + 1) * 512:(2 * h + 2) * 512],
                                        rb[:, 0:512], rb[:, 512:1024], ALU.add)
                    self.dinc(v, sv, 1)
                rope_done = sv.n

                @block.gpsimd
                def _(g, tt=tt, dst_s=dst_s):
                    g.wait_ge(sv.h, rope_done)
                    dst = d[dst_s].rearrange("(fc p) t -> p fc t", p=128)[:, :, tt * 512:(tt + 1) * 512]
                    self.minc(g.dma_start(out=dst, in_=pT[:].rearrange("p (fc t) -> p fc t", fc=8)), sg, 16)

                if which == "k":
                    for grp in range(8):
                        bi = 4 + grp % 2
                        bk = bank[bi]

                        @block.tensor
                        def _(t, grp=grp, bk=bk, bi=bi):
                            t.wait_ge(sv.h, rope_done)
                            self.bank_wait(t, bi)
                            for j in range(4):
                                i = grp * 4 + j
                                dc, c = i // 4, i % 4
                                self.minc(t.matmul(bk[:, j * 128:(j + 1) * 128],
                                                   pT[:, dc * 512 + c * 128:dc * 512 + (c + 1) * 128],
                                                   self.ident[:], start=True, stop=True), sp, 1)
                        pe_cnt = sp.n

                        @block.vector
                        def _(v, grp=grp, bk=bk):
                            v.wait_ge(sp.h, pe_cnt)
                            if grp == 0:
                                v.wait_ge(sg.h, sg.n)
                            for j in range(4):
                                i = grp * 4 + j
                                dc, c = i // 4, i % 4
                                h = dc // 2
                                v.tensor_scalar(kkb[:, c * D + dc * 128:c * D + (dc + 1) * 128],
                                                bk[:, j * 128:(j + 1) * 128],
                                                kdecP[:, h:h + 1], None, ALU.mult)
                            self.dinc(v, sv, 1)
                        self.bank_set(bi, sv)
                    kk_done = sv.n

                    @block.gpsimd
                    def _(g, tt=tt):
                        g.wait_ge(sv.h, kk_done)
                        dst = d["kk_s"][tt * 512:(tt + 1) * 512, :].rearrange("(c p) f -> p c f", p=128)
                        self.minc(g.dma_start(out=dst, in_=kkb[:].rearrange("p (c f) -> p c f", c=4)), sg, 16)

            for which, wmat, dst_s in (("v", wv, "v_s"), ("g", wg, "g_s")):
                for c in range(4):
                    @block.tensor
                    def _(t, c=c, wmat=wmat):
                        for vc in range(4):
                            self.bank_wait(t, 4 + vc)
                            for kc in range(8):
                                self.minc(t.matmul(bank[4 + vc][:],
                                                   nrmT[:, kc * 512 + c * 128:kc * 512 + (c + 1) * 128],
                                                   wmat[:, kc * 2 * D + vc * 512:kc * 2 * D + (vc + 1) * 512],
                                                   start=(kc == 0), stop=(kc == 7)), sp, 1)
                    pe_cnt = sp.n

                    @block.scalar
                    def _(s, which=which, pe_cnt=pe_cnt):
                        s.wait_ge(sp.h, pe_cnt)
                        s.wait_ge(sg.h, sg.n)
                        for vc in range(4):
                            if which == "g":
                                s.activation(vgb[:, vc * 512:(vc + 1) * 512], bank[4 + vc][:], AF.Silu)
                            else:
                                s.copy(vgb[:, vc * 512:(vc + 1) * 512], bank[4 + vc][:])
                        self.dinc(s, sa, 1)
                    for vc in range(4):
                        self.bank_set(4 + vc, sa)
                    row_done = sa.n

                    @block.gpsimd
                    def _(g, tt=tt, c=c, dst_s=dst_s):
                        g.wait_ge(sa.h, row_done)
                        dst = d[dst_s][tt * 512 + c * 128:tt * 512 + (c + 1) * 128, :]
                        self.minc(g.dma_start(out=dst, in_=vgb[:]), sg, 16)

            trig_free[tt] = sv.n

        self.a1_spills = sg.n

    def phase_a2(self, ctx):
        nc, d, block = self.nc, self.d, self.block
        si, sg, sp, sv, sa = self.si, self.sg, self.sp, self.sv, self.sa
        bank = self.bank
        sb = lambda name, shape, dt: ctx.enter_context(nc.sbuf_tensor(name, shape, dt))
        gCs = self.gC

        dmaskT = sb("c_dmaskT", [128, 512], F32)
        crossrep = sb("c_crossrep", [128, 512], BF16)
        gnwP = sb("c_gnwP", [128, 16], F32)
        self.S = sb("c_S", [128, 8 * 512], F32)
        self.Sb = sb("c_Sb", [128, 8 * 512], BF16)
        wo = sb("c_wo", [128, 16 * D], BF16)
        wostg = sb("c_wostg", [128, 2 * D], F32)
        qgrp = sb("c_qgrp", [128, 2 * 2048], BF16)
        kgrp = sb("c_kgrp", [128, 2 * 2048], BF16)
        kkgrp = sb("c_kkgrp", [128, 2 * 2048], BF16)
        vgrp = sb("c_vgrp", [128, 2 * 4096], BF16)
        ggrp = sb("c_ggrp", [128, 2 * 4096], BF16)
        xgrp = sb("c_xgrp", [128, 2 * 2048], F32)
        qx = sb("c_qx", [128, 8 * 128], BF16)
        at = sb("c_at", [128, 4 * 128], BF16)
        og = sb("c_og", [128, 2048], BF16)
        ogT = sb("c_ogT", [128, 16 * 128], BF16)
        scr5 = sb("c_scr5", [128, 512], F32)
        ssqo = sb("c_ssqo", [128, 4], F32)
        rstd = sb("c_rstd", [128, 4], F32)
        yv = sb("c_y", [128, D], F32)
        af = sb("c_af", [128, D], F32)
        xp = sb("c_xp", [128, D], F32)
        sm1 = sb("c_sm1", [128, 6], F32)

        @block.sync
        def _(sync):
            self.minc(sync.dma_start(out=dmaskT[:], in_=d["dmaskT"]), si, 16)
            self.minc(sync.dma_start(out=crossrep[:], in_=d["crossrep"]), si, 16)
            self.minc(sync.dma_start(out=gnwP[:], in_=d["gnwP"]), si, 16)
        a2c_loaded = si.n

        @block.vector
        def _(v):
            v.memset(self.S[:], 0.0)
            self.dinc(v, sv, 1)
        S_zeroed = sv.n

        wo_ready = []
        for kc in range(16):
            @block.sync
            def _(sync, kc=kc):
                if kc >= 2:
                    sync.wait_ge(sv.h, wo_ready[kc - 2])
                self.minc(sync.dma_start(out=wostg[:, (kc % 2) * D:((kc % 2) + 1) * D],
                                         in_=d["Wo"][kc * 128:(kc + 1) * 128, :]), si, 16)
            ld = si.n

            @block.vector
            def _(v, kc=kc):
                v.wait_ge(si.h, max(ld, a2c_loaded))
                v.tensor_scalar(wo[:, kc * D:(kc + 1) * D], wostg[:, (kc % 2) * D:((kc % 2) + 1) * D],
                                gnwP[:, kc:kc + 1], None, ALU.mult)
                self.dinc(v, sv, 1)
            wo_ready.append(sv.n)
        wo_done = sv.n

        group_done = {}
        group_loaded = {}
        prev = {}
        for n in range(NCH):
            gidx, par = n // 2, n % 2
            gb = (gidx % 2)

            if par == 0:
                @block.sync
                def _(sync, n=n, gidx=gidx, gb=gb):
                    if gidx >= 2:
                        sync.wait_ge(sv.h, group_done[gidx - 2])
                    r0 = n * 128
                    self.minc(sync.dma_start(
                        out=qgrp[:, gb * 2048:(gb + 1) * 2048].rearrange("p (fc t) -> p fc t", fc=8),
                        in_=d["qT_s"].rearrange("(fc p) t -> p fc t", p=128)[:, :, r0:r0 + 256]), si, 16)
                    self.minc(sync.dma_start(
                        out=kgrp[:, gb * 2048:(gb + 1) * 2048].rearrange("p (fc t) -> p fc t", fc=8),
                        in_=d["kT_s"].rearrange("(fc p) t -> p fc t", p=128)[:, :, r0:r0 + 256]), si, 16)
                    self.minc(sync.dma_start(
                        out=kkgrp[:, gb * 2048:(gb + 1) * 2048].rearrange("p (c f) -> p c f", c=2),
                        in_=d["kk_s"][r0:r0 + 256, :].rearrange("(c p) f -> p c f", p=128)), si, 16)
                    self.minc(sync.dma_start(
                        out=vgrp[:, gb * 4096:(gb + 1) * 4096].rearrange("p (c f) -> p c f", c=2),
                        in_=d["v_s"][r0:r0 + 256, :].rearrange("(c p) f -> p c f", p=128)), si, 16)
                    self.minc(sync.dma_start(
                        out=ggrp[:, gb * 4096:(gb + 1) * 4096].rearrange("p (c f) -> p c f", c=2),
                        in_=d["g_s"][r0:r0 + 256, :].rearrange("(c p) f -> p c f", p=128)), si, 16)
                    self.minc(sync.dma_start(
                        out=xgrp[:, gb * 2048:(gb + 1) * 2048].rearrange("p (c f) -> p c f", c=2),
                        in_=d["x"][r0:r0 + 256, :].rearrange("(c p) f -> p c f", p=128)), si, 16)
                group_loaded[gidx] = si.n

            ld = group_loaded[gidx]
            qch = lambda i, gb=gb, par=par: qgrp[:, gb * 2048 + i * 256 + par * 128:gb * 2048 + i * 256 + (par + 1) * 128]
            kch = lambda i, gb=gb, par=par: kgrp[:, gb * 2048 + i * 256 + par * 128:gb * 2048 + i * 256 + (par + 1) * 128]
            kkch = lambda i, gb=gb, par=par: kkgrp[:, gb * 2048 + par * 1024 + i * 128:gb * 2048 + par * 1024 + (i + 1) * 128]
            vch = lambda h, gb=gb, par=par: vgrp[:, gb * 4096 + par * 2048 + h * 512:gb * 4096 + par * 2048 + (h + 1) * 512]
            gch = lambda h, gb=gb, par=par: ggrp[:, gb * 4096 + par * 2048 + h * 512:gb * 4096 + par * 2048 + (h + 1) * 512]
            xch = lambda gb=gb, par=par: xgrp[:, gb * 2048 + par * 1024:gb * 2048 + (par + 1) * 1024]

            @block.vector
            def _(v, n=n):
                v.wait_ge(sv.h, S_zeroed if n == 0 else prev["supd"])
                if n > 0:
                    v.wait_ge(sp.h, prev["o_mm"])
                for i in range(8):
                    v.tensor_copy(self.Sb[:, i * 512:(i + 1) * 512], self.S[:, i * 512:(i + 1) * 512])
                self.dinc(v, sv, 1)
            cast_done = sv.n

            @block.vector
            def _(v, n=n):
                v.wait_ge(si.h, max(ld, a2c_loaded))
                if n > 0:
                    v.wait_ge(sp.h, prev["o_mm"])
                for i in range(8):
                    h = i // 2
                    v.tensor_tensor(qx[:, i * 128:(i + 1) * 128], qch(i),
                                    crossrep[:, h * 128:(h + 1) * 128], ALU.mult)
                self.dinc(v, sv, 1)
            qx_done = sv.n

            @block.tensor
            def _(t, n=n):
                t.wait_ge(si.h, ld)
                self.bank_wait(t, 0)
                for h in range(H):
                    for dc in range(2):
                        self.minc(t.matmul(bank[0][:, h * 128:(h + 1) * 128],
                                           kch(2 * h + dc), qch(2 * h + dc),
                                           start=(dc == 0), stop=(dc == 1)), sp, 1)
            attnT_mm = sp.n

            @block.vector
            def _(v, n=n):
                v.wait_ge(sp.h, attnT_mm)
                if n > 0:
                    v.wait_ge(sp.h, prev["o_mm"])
                for h in range(H):
                    v.tensor_tensor(at[:, h * 128:(h + 1) * 128], bank[0][:, h * 128:(h + 1) * 128],
                                    dmaskT[:, h * 128:(h + 1) * 128], ALU.mult)
                self.dinc(v, sv, 1)
            mask_done = sv.n
            self.bank_set(0, sv)

            @block.tensor
            def _(t, n=n):
                t.wait_ge(sv.h, mask_done)
                t.wait_ge(sv.h, cast_done)
                t.wait_ge(sv.h, qx_done)
                for h in range(H):
                    self.bank_wait(t, 2 + h)
                    self.minc(t.matmul(bank[2 + h][:], at[:, h * 128:(h + 1) * 128], vch(h),
                                       start=True, stop=False), sp, 1)
                    for dc in range(2):
                        self.minc(t.matmul(bank[2 + h][:], qx[:, (2 * h + dc) * 128:(2 * h + dc + 1) * 128],
                                           self.Sb[:, (2 * h + dc) * 512:(2 * h + dc + 1) * 512],
                                           start=False, stop=(dc == 1)), sp, 1)
            o_mm = sp.n

            supd_marks = []
            for h in range(H):
                @block.tensor
                def _(t, h=h, n=n):
                    for dc in range(2):
                        bi = 6 + dc
                        self.bank_wait(t, bi)
                        self.minc(t.matmul(bank[bi][:], kkch(2 * h + dc), vch(h),
                                           start=True, stop=True), sp, 1)
                ds_mm = sp.n

                @block.vector
                def _(v, h=h, n=n, ds_mm=ds_mm):
                    v.wait_ge(sp.h, ds_mm)
                    if n == 0:
                        v.wait_ge(sv.h, S_zeroed)
                    for dc in range(2):
                        sl = slice((2 * h + dc) * 512, (2 * h + dc + 1) * 512)
                        v.scalar_tensor_tensor(self.S[:, sl], self.S[:, sl], gCs[h],
                                               bank[6 + dc][:], ALU.mult, ALU.add)
                    self.dinc(v, sv, 1)
                self.bank_set(6, sv)
                self.bank_set(7, sv)
            supd = sv.n

            @block.scalar
            def _(s, n=n):
                s.wait_ge(sp.h, o_mm)
                for h in range(H):
                    s.activation(scr5[:], bank[2 + h][:], AF.Square, accum_out=ssqo[:, h:h + 1])
                self.dinc(s, sa, 1)
            sq_done = sa.n

            @block.vector
            def _(v):
                v.wait_ge(sa.h, sq_done)
                v.tensor_scalar(rstd[:], ssqo[:], 1.0 / DV, EPS, ALU.mult, ALU.add)
                self.dinc(v, sv, 1)
            ve2 = sv.n

            @block.scalar
            def _(s):
                s.wait_ge(sv.h, ve2)
                s.activation(rstd[:], rstd[:], AF.Ln)
                s.drain()
                s.activation(rstd[:], rstd[:], AF.Exp, scale=-0.5)
                self.dinc(s, sa, 1)
            rstd_done = sa.n

            @block.vector
            def _(v, n=n):
                v.wait_ge(sa.h, rstd_done)
                if n > 0:
                    v.wait_ge(sp.h, prev["ogT_mm"])
                for h in range(H):
                    v.scalar_tensor_tensor(og[:, h * 512:(h + 1) * 512], bank[2 + h][:],
                                           rstd[:, h:h + 1], gch(h),
                                           ALU.mult, ALU.mult)
                self.dinc(v, sv, 1)
            og_done = sv.n
            for h in range(H):
                self.bank_set(2 + h, sv)

            for grp in range(4):
                bi = grp % 2
                @block.tensor
                def _(t, grp=grp, bi=bi):
                    t.wait_ge(sv.h, og_done)
                    self.bank_wait(t, bi)
                    for j in range(4):
                        kc = grp * 4 + j
                        self.minc(t.matmul(bank[bi][:, j * 128:(j + 1) * 128],
                                           og[:, kc * 128:(kc + 1) * 128], self.ident[:],
                                           start=True, stop=True), sp, 1)
                pe_cnt = sp.n

                @block.scalar
                def _(s, grp=grp, bi=bi, pe_cnt=pe_cnt):
                    s.wait_ge(sp.h, pe_cnt)
                    for j in range(4):
                        kc = grp * 4 + j
                        s.copy(ogT[:, kc * 128:(kc + 1) * 128], bank[bi][:, j * 128:(j + 1) * 128])
                    self.dinc(s, sa, 1)
                self.bank_set(bi, sa)
            ogT_mm = sp.n
            ogT_done = sa.n

            @block.tensor
            def _(t, n=n):
                t.wait_ge(sa.h, ogT_done)
                t.wait_ge(sv.h, wo_done)
                self.bank_wait(t, 2)
                self.bank_wait(t, 3)
                for half in range(2):
                    for kc in range(16):
                        self.minc(t.matmul(bank[2 + half][:], ogT[:, kc * 128:(kc + 1) * 128],
                                           wo[:, kc * D + half * 512:kc * D + (half + 1) * 512],
                                           start=(kc == 0), stop=(kc == 15)), sp, 1)
            wo_mm = sp.n

            @block.vector
            def _(v, n=n):
                v.wait_ge(sp.h, wo_mm)
                v.tensor_scalar(yv[:], xch(), self.a1v[:, n:n + 1], self.nma1[:, n:n + 1],
                                ALU.mult, ALU.add)
                v.drain()
                for half in range(2):
                    v.scalar_tensor_tensor(af[:, half * 512:(half + 1) * 512], bank[2 + half][:],
                                           1.0, yv[:, half * 512:(half + 1) * 512], ALU.mult, ALU.add)
                self.dinc(v, sv, 1)
            af_done = sv.n
            self.bank_set(2, sv)
            self.bank_set(3, sv)

            @block.scalar
            def _(s):
                s.wait_ge(sv.h, af_done)
                s.activation(scr5[:], af[:, 0:512], AF.Square, accum_out=sm1[:, 0:1])
                s.activation(scr5[:], af[:, 512:1024], AF.Square, accum_out=sm1[:, 1:2])
                s.activation(scr5[:], af[:, 0:512], AF.Identity, accum_out=sm1[:, 2:3])
                s.activation(scr5[:], af[:, 512:1024], AF.Identity, accum_out=sm1[:, 3:4])
                self.dinc(s, sa, 1)
            ost_done = sa.n

            @block.vector
            def _(v):
                v.wait_ge(sa.h, ost_done)
                v.tensor_tensor(sm1[:, 4:5], sm1[:, 2:3], sm1[:, 3:4], ALU.add)
                v.drain()
                v.tensor_scalar(sm1[:, 5:6], sm1[:, 4:5], 1.0 / D, None, ALU.mult)
                v.drain()
                v.tensor_tensor(sm1[:, 4:5], sm1[:, 0:1], sm1[:, 1:2], ALU.add)
                v.drain()
                v.tensor_tensor(sm1[:, 2:3], sm1[:, 5:6], sm1[:, 5:6], ALU.mult)
                v.drain()
                v.scalar_tensor_tensor(sm1[:, 4:5], sm1[:, 4:5], 1.0 / D, sm1[:, 2:3],
                                       ALU.mult, ALU.subtract)
                v.drain()
                v.tensor_scalar(sm1[:, 4:5], sm1[:, 4:5], EPS, None, ALU.add)
                self.dinc(v, sv, 1)
            ve3 = sv.n

            @block.scalar
            def _(s):
                s.wait_ge(sv.h, ve3)
                s.activation(sm1[:, 4:5], sm1[:, 4:5], AF.Ln)
                s.drain()
                s.activation(sm1[:, 4:5], sm1[:, 4:5], AF.Exp, scale=-0.5)
                self.dinc(s, sa, 1)
            al_done = sa.n

            @block.vector
            def _(v, n=n):
                v.wait_ge(sa.h, al_done)
                if n > 0:
                    v.wait_ge(sg.h, prev["spill"])
                v.scalar_tensor_tensor(sm1[:, 2:3], sm1[:, 5:6], -1.0, sm1[:, 4:5],
                                       ALU.mult, ALU.mult)
                v.drain()
                v.tensor_scalar(xp[:], xch(), 1.0, sm1[:, 2:3], ALU.mult, ALU.add)
                v.drain()
                v.scalar_tensor_tensor(xp[:], af[:], sm1[:, 4:5], xp[:], ALU.mult, ALU.add)
                self.dinc(v, sv, 1)
            xp_done = sv.n

            @block.gpsimd
            def _(g, n=n):
                g.wait_ge(sv.h, xp_done)
                self.minc(g.dma_start(out=d["xp_s"][n * 128:(n + 1) * 128, :], in_=xp[:]), sg, 16)
            prev = dict(supd=supd, o_mm=o_mm, og=og_done, ogT_mm=ogT_mm, spill=sg.n)
            group_done[gidx] = xp_done

    def init_q(self, ctx):
        nc, d = self.nc, self.d
        sb = lambda name, shape, dt: ctx.enter_context(nc.sbuf_tensor(name, shape, dt))
        self.q_wraw = sb("q_wraw", [128, 8 * 1024], F32)
        self.q_wab = sb("q_wab", [128, 8 * 1024], F32)
        self.q_wsgn = sb("q_wsgn", [128, 8 * 1024], BF16)
        self.q_wout = sb("q_wout", [128, 8 * 1024], BF16)
        self.q_red = sb("q_red", [128, 8], F32)
        self.q_thr = sb("q_thr", [128, 8], F32)
        self.q_mcol = sb("q_mcol", [128, 8], F32)

        units = []
        for s_i in range(9):
            w1src = d["sw1"] if s_i == 0 else d["ew1"][(s_i - 1) * 2048:s_i * 2048, :]
            w2src = d["sw2"] if s_i == 0 else d["ew2"][(s_i - 1) * 1024:s_i * 1024, :]
            for it in range(16):
                units.append((w1src[it * 128:(it + 1) * 128, :],
                              d["w1n_s"][s_i * 2048 + it * 128:s_i * 2048 + (it + 1) * 128, :], 0))
            for it in range(8):
                r0 = s_i * 1024 + it * 128
                units.append((w2src[it * 128:(it + 1) * 128, 0:1024],
                              d["w2n_s"][r0:r0 + 128, 0:1024], 1))
                units.append((w2src[it * 128:(it + 1) * 128, 1024:2048],
                              d["w2n_s"][r0:r0 + 128, 1024:2048], 2))
        self.q_units = units
        self.q_pos = 0
        self.q_gmark = {}
        self.q_stmark = {}

    def emit_q(self, kmax):
        block = self.block
        si, sg, sv, sa = self.si, self.sg, self.sv, self.sa
        units = self.q_units
        red, thr, mcol = self.q_red, self.q_thr, self.q_mcol
        ngrp = len(units) // 4
        assert len(units) % 4 == 0
        for gi in range(ngrp):
            grp = units[gi * 4:gi * 4 + 4]
            sb0 = (gi % 2) * 4
            sl = lambda j, t, sb0=sb0: t[:, (sb0 + j) * 1024:(sb0 + j + 1) * 1024]
            cidx = lambda j, sb0=sb0: slice(sb0 + j, sb0 + j + 1)

            @block.sync
            def _(sync, gi=gi, grp=grp, sl=sl):
                if gi - 2 in self.q_gmark:
                    sync.wait_ge(sv.h, self.q_gmark[gi - 2])
                for j, (srcap, dstap, kind) in enumerate(grp):
                    self.minc(sync.dma_start(out=sl(j, self.q_wraw), in_=srcap), si, 16)
            ld = si.n

            @block.scalar
            def _(s, gi=gi, grp=grp, sl=sl, cidx=cidx):
                s.wait_ge(si.h, ld)
                if gi - 2 in self.q_gmark:
                    s.wait_ge(sv.h, self.q_gmark[gi - 2])
                for j in range(4):
                    s.activation(sl(j, self.q_wab), sl(j, self.q_wraw), AF.Abs,
                                 accum_out=red[:, cidx(j)])
                    s.activation(sl(j, self.q_wsgn), sl(j, self.q_wraw), AF.Sign)
                self.dinc(s, sa, 1)
            absd = sa.n

            @block.vector
            def _(v, gi=gi, grp=grp, sl=sl, cidx=cidx):
                v.wait_ge(sa.h, absd)
                if gi - 2 in self.q_stmark:
                    v.wait_ge(sg.h, self.q_stmark[gi - 2])
                for j, (srcap, dstap, kind) in enumerate(grp):
                    if kind == 2:
                        v.tensor_tensor(red[:, cidx(j)], red[:, cidx(j - 1)], red[:, cidx(j)], ALU.add)
                v.drain()
                for j, (srcap, dstap, kind) in enumerate(grp):
                    if kind == 1:
                        continue
                    FD = 1024 if kind == 0 else 2048
                    v.tensor_scalar(thr[:, cidx(j)], red[:, cidx(j)], 0.5 / FD, 5e-6, ALU.mult, ALU.max)
                    v.tensor_scalar(mcol[:, cidx(j)], red[:, cidx(j)], 1.0 / FD, 1e-5, ALU.mult, ALU.max)
                v.drain()
                for j, (srcap, dstap, kind) in enumerate(grp):
                    jj = j + 1 if kind == 1 else j
                    v.tensor_scalar(sl(j, self.q_wab), sl(j, self.q_wab),
                                    thr[:, cidx(jj)], None, ALU.is_gt)
                v.drain()
                for j, (srcap, dstap, kind) in enumerate(grp):
                    jj = j + 1 if kind == 1 else j
                    v.scalar_tensor_tensor(sl(j, self.q_wout), sl(j, self.q_wab),
                                           mcol[:, cidx(jj)], sl(j, self.q_wsgn),
                                           ALU.mult, ALU.mult)
                self.dinc(v, sv, 1)
            self.q_gmark[gi] = sv.n

            @block.gpsimd
            def _(g, gi=gi, grp=grp, sl=sl):
                g.wait_ge(sv.h, self.q_gmark[gi])
                for j, (srcap, dstap, kind) in enumerate(grp):
                    self.minc(g.dma_start(out=dstap, in_=sl(j, self.q_wout)), sg, 16)
            self.q_stmark[gi] = sg.n

    def phase_b(self, ctx):
        nc, d, block = self.nc, self.d, self.block
        si, sg, sc, sp, sv, sa = self.si, self.sg, self.sc, self.sp, self.sv, self.sa
        bank = self.bank
        sb = lambda name, shape, dt: ctx.enter_context(nc.sbuf_tensor(name, shape, dt))

        hT = sb("b_hT", [128, 8 * D], BF16)
        w1T = sb("b_w1T", [128, 2 * 8 * 2048], BF16)
        w2T = sb("b_w2T", [128, 2 * 16 * D], BF16)
        acc = sb("b_acc", [128, 8 * D], F32)
        mid = sb("b_mid", [128, 16 * 256], BF16)
        xph = sb("b_xph", [128, D], F32)
        scrB = sb("b_scrB", [128, D], F32)
        hb = sb("b_hb", [128, D], BF16)
        u1 = scrB
        rsb = sb("b_rsb", [8, D], BF16)
        rT = sb("b_rT", [128, 64], BF16)
        self.fgw = sb("b_fgw", [128, D], F32)
        self.fgb = sb("b_fgb", [128, D], F32)
        lgs = sb("b_lgs", [128, 8], F32)
        gates = sb("b_gates", [128, 8 * 8], F32)
        st8 = sb("b_st8", [128, 8], F32)
        sq8 = sb("b_sq8", [128, 8], F32)
        mu8 = sb("b_mu8", [128, 8], F32)
        va8 = sb("b_va8", [128, 8], F32)
        tA8 = sb("b_tA8", [128, 8], F32)
        tB8 = sb("b_tB8", [128, 8], F32)
        be8 = sb("b_be8", [128, 8], F32)
        nm8 = sb("b_nm8", [128, 8], F32)
        sxs = sb("b_sxs", [128, 4], F32)

        @block.gpsimd
        def _(g):
            self.minc(g.dma_start(out=rsb[:], in_=d["router"]), sg, 16)
        r_ld = sg.n

        @block.sync
        def _(sync):
            self.minc(sync.dma_start(out=self.fgw[:], in_=d["fgw_rep"]), si, 16)
            self.minc(sync.dma_start(out=self.fgb[:], in_=d["fgb_rep"]), si, 16)
        fg_ld = si.n

        @block.tensor
        def _(t):
            t.wait_ge(sg.h, r_ld)
            t.wait_ge(si.h, 16)
            self.bank_wait(t, 3)
            for dc in range(8):
                self.minc(t.matmul(bank[3][:, dc * 8:(dc + 1) * 8],
                                   rsb[0:8, dc * 128:(dc + 1) * 128],
                                   self.ident[0:8, 0:8], start=True, stop=True), sp, 1)
        rt_mm = sp.n

        @block.scalar
        def _(s):
            s.wait_ge(sp.h, rt_mm)
            s.copy(rT[:], bank[3][:, 0:64])
            self.dinc(s, sa, 1)
        self.bank_set(3, sa)
        rt_done = sa.n

        spec_done = {}
        for hf in range(2):
            for tc in range(8):
                row0 = hf * 1024 + tc * 128

                @block.sync
                def _(sync, row0=row0, tc=tc, hf=hf):
                    if hf > 0 or tc > 0:
                        sync.wait_ge(sv.h, sv.n)
                    self.minc(sync.dma_start(out=xph[:], in_=d["xp_s"][row0:row0 + 128, :]), si, 16)
                ld = si.n

                @block.scalar
                def _(s, tc=tc):
                    s.wait_ge(si.h, ld)
                    s.activation(scrB[:], xph[:], AF.Square, accum_out=sq8[:, tc:tc + 1])
                    s.activation(scrB[:], xph[:], AF.Identity, accum_out=st8[:, tc:tc + 1])
                    self.dinc(s, sa, 1)
                std = sa.n

                @block.vector
                def _(v, tc=tc):
                    v.wait_ge(sa.h, std)
                    cl = slice(tc, tc + 1)
                    v.tensor_scalar(mu8[:, cl], st8[:, cl], 1.0 / D, None, ALU.mult)
                    v.drain()
                    v.tensor_tensor(tA8[:, cl], mu8[:, cl], mu8[:, cl], ALU.mult)
                    v.drain()
                    v.scalar_tensor_tensor(va8[:, cl], sq8[:, cl], 1.0 / D, tA8[:, cl],
                                           ALU.mult, ALU.subtract)
                    v.drain()
                    v.tensor_scalar(tA8[:, cl], va8[:, cl], EPS, None, ALU.add)
                    self.dinc(v, sv, 1)
                ve = sv.n

                @block.scalar
                def _(s, tc=tc):
                    s.wait_ge(sv.h, ve)
                    cl = slice(tc, tc + 1)
                    s.activation(tB8[:, cl], tA8[:, cl], AF.Ln)
                    s.drain()
                    s.activation(tB8[:, cl], tB8[:, cl], AF.Exp, scale=-0.5)
                    self.dinc(s, sa, 1)
                a1d = sa.n

                @block.vector
                def _(v, tc=tc):
                    v.wait_ge(sa.h, a1d)
                    cl = slice(tc, tc + 1)
                    v.reciprocal(be8[:, cl], tA8[:, cl])
                    v.drain()
                    v.tensor_tensor(be8[:, cl], va8[:, cl], be8[:, cl], ALU.mult)
                    v.drain()
                    v.tensor_scalar(be8[:, cl], be8[:, cl], EPS, None, ALU.add)
                    self.dinc(v, sv, 1)
                v1d = sv.n

                @block.scalar
                def _(s, tc=tc):
                    s.wait_ge(sv.h, v1d)
                    cl = slice(tc, tc + 1)
                    s.activation(nm8[:, cl], be8[:, cl], AF.Ln)
                    s.drain()
                    s.activation(nm8[:, cl], nm8[:, cl], AF.Exp, scale=-0.5)
                    self.dinc(s, sa, 1)
                a2d = sa.n

                @block.vector
                def _(v, tc=tc):
                    v.wait_ge(sa.h, a2d)
                    v.wait_ge(si.h, fg_ld)
                    cl = slice(tc, tc + 1)
                    v.tensor_tensor(be8[:, cl], tB8[:, cl], nm8[:, cl], ALU.mult)
                    v.drain()
                    v.scalar_tensor_tensor(nm8[:, cl], mu8[:, cl], -1.0, be8[:, cl],
                                           ALU.mult, ALU.mult)
                    v.drain()
                    v.tensor_scalar(u1[:], xph[:], be8[:, cl], nm8[:, cl], ALU.mult, ALU.add)
                    v.drain()
                    v.tensor_tensor(u1[:], u1[:], self.fgw[:], ALU.mult)
                    v.drain()
                    v.tensor_tensor(hb[:], u1[:], self.fgb[:], ALU.add)
                    self.dinc(v, sv, 1)
                hbd = sv.n

                for grp in range(2):
                    bi = grp % 2
                    @block.tensor
                    def _(t, grp=grp, bi=bi, tc=tc):
                        t.wait_ge(sv.h, hbd)
                        self.bank_wait(t, bi)
                        for j in range(4):
                            dc = grp * 4 + j
                            self.minc(t.matmul(bank[bi][:, j * 128:(j + 1) * 128],
                                               hb[:, dc * 128:(dc + 1) * 128], self.ident[:],
                                               start=True, stop=True), sp, 1)
                    pe_cnt = sp.n

                    @block.scalar
                    def _(s, grp=grp, bi=bi, tc=tc, pe_cnt=pe_cnt, hf=hf):
                        s.wait_ge(sp.h, pe_cnt)
                        if hf > 0 and tc == 0 and grp == 0:
                            s.wait_ge(sp.h, sp.n)
                        for j in range(4):
                            dc = grp * 4 + j
                            s.copy(hT[:, dc * D + tc * 128:dc * D + (tc + 1) * 128],
                                   bank[bi][:, j * 128:(j + 1) * 128])
                        self.dinc(s, sa, 1)
                    self.bank_set(bi, sa)
                hT_tc = sa.n

                @block.tensor
                def _(t, tc=tc):
                    t.wait_ge(sa.h, max(hT_tc, rt_done))
                    self.bank_wait(t, 2)
                    for dc in range(8):
                        self.minc(t.matmul(bank[2][:, 0:8],
                                           hT[:, dc * D + tc * 128:dc * D + (tc + 1) * 128],
                                           rT[:, dc * 8:(dc + 1) * 8],
                                           start=(dc == 0), stop=(dc == 7)), sp, 1)
                lg_mm = sp.n

                @block.vector
                def _(v, tc=tc):
                    v.wait_ge(sp.h, lg_mm)
                    v.tensor_copy(lgs[:], bank[2][:, 0:8])
                    v.drain()
                    v.tensor_reduce(sxs[:, 0:1], lgs[:], AX.X, ALU.max)
                    v.drain()
                    v.tensor_scalar(sxs[:, 1:2], sxs[:, 0:1], -1.0, None, ALU.mult)
                    self.dinc(v, sv, 1)
                self.bank_set(2, sv)
                lmx = sv.n

                @block.scalar
                def _(s, tc=tc):
                    s.wait_ge(sv.h, lmx)
                    s.activation(scrB[:, 0:8], lgs[:], AF.Exp, bias=sxs[:, 1:2],
                                 accum_out=sxs[:, 2:3])
                    self.dinc(s, sa, 1)
                expd = sa.n

                @block.vector
                def _(v, tc=tc):
                    v.wait_ge(sa.h, expd)
                    v.reciprocal(sxs[:, 3:4], sxs[:, 2:3])
                    v.tensor_scalar(lgs[:], lgs[:], sxs[:, 0:1], None, ALU.is_equal)
                    v.drain()
                    v.tensor_scalar(gates[:, tc * 8:(tc + 1) * 8], lgs[:], sxs[:, 3:4],
                                    None, ALU.mult)
                    self.dinc(v, sv, 1)
                gates_done = sv.n

            for s_i in range(B_SPECIES):
                p = 0 if B_SINGLEBUF else s_i % 2
                w1buf = w1T[:, p * 8 * 2048:(p + 1) * 8 * 2048]
                w2buf = w2T[:, p * 16 * D:(p + 1) * 16 * D]

                @block.sync
                def _(sync, s_i=s_i, p=p, w1buf=w1buf, hf=hf):
                    if B_SINGLEBUF:
                        sync.wait_ge(sp.h, sp.n)
                    elif (hf, s_i) != (0, 0) and not (hf == 0 and s_i == 1):
                        key = (hf, s_i - 2) if s_i >= 2 else (hf - 1, 8 - s_i)
                        if key in spec_done:
                            sync.wait_ge(sp.h, spec_done[key])
                    for dc in range(8):
                        self.minc(sync.dma_start(
                            out=w1buf[:, dc * 2048:(dc + 1) * 2048],
                            in_=d["w1n_s"][s_i * 2048:(s_i + 1) * 2048, dc * 128:(dc + 1) * 128],
                            transpose=True), si, 16)
                w1_ld = si.n

                @block.sync
                def _(sync, s_i=s_i, p=p, w2buf=w2buf, hf=hf):
                    for fc in range(16):
                        self.minc(sync.dma_start(
                            out=w2buf[:, fc * D:(fc + 1) * D],
                            in_=d["w2n_s"][s_i * D:(s_i + 1) * D, fc * 128:(fc + 1) * 128],
                            transpose=True), sc, 16)
                w2_ld = sc.n

                for tg in range(4):
                    for fqp in range(2):
                        @block.tensor
                        def _(t, tg=tg, fqp=fqp, s_i=s_i):
                            if tg == 0 and fqp == 0:
                                t.wait_ge(si.h, w1_ld)
                                t.wait_ge(sa.h, hT_tc)
                            for fq in (2 * fqp, 2 * fqp + 1):
                                b0 = (fq % 2) * 2
                                self.bank_wait(t, b0)
                                self.bank_wait(t, b0 + 1)
                                for j in range(4):
                                    fc = fq * 4 + j
                                    dst = bank[b0 + j // 2][:, (j % 2) * 256:(j % 2) * 256 + 256]
                                    for dc in range(8):
                                        self.minc(t.matmul(
                                            dst, w1buf[:, dc * 2048 + fc * 128:dc * 2048 + (fc + 1) * 128],
                                            hT[:, dc * D + tg * 256:dc * D + (tg + 1) * 256],
                                            start=(dc == 0), stop=(dc == 7)), sp, 1)
                        mm_cnt = sp.n

                        @block.scalar
                        def _(s, tg=tg, fqp=fqp, mm_cnt=mm_cnt):
                            s.wait_ge(sp.h, mm_cnt)
                            for fq in (2 * fqp, 2 * fqp + 1):
                                b0 = (fq % 2) * 2
                                s.activation(mid[:, (fq * 4) * 256:(fq * 4) * 256 + 512],
                                             bank[b0][:], AF.Silu)
                                s.activation(mid[:, (fq * 4 + 2) * 256:(fq * 4 + 2) * 256 + 512],
                                             bank[b0 + 1][:], AF.Silu)
                            self.dinc(s, sa, 1)
                        for bi in range(4):
                            self.bank_set(bi, sa)
                    silu_all = sa.n

                    for tc2 in range(2):
                        b0 = 4 + tc2 * 2
                        tgc = tg * 2 + tc2

                        @block.tensor
                        def _(t, tc2=tc2, b0=b0, tgc=tgc, s_i=s_i):
                            t.wait_ge(sa.h, silu_all)
                            if tg == 0 and tc2 == 0:
                                t.wait_ge(sc.h, w2_ld)
                            self.bank_wait(t, b0)
                            self.bank_wait(t, b0 + 1)
                            for fc in range(16):
                                lh = mid[:, fc * 256 + tc2 * 128:fc * 256 + (tc2 + 1) * 128]
                                self.minc(t.matmul(bank[b0][:], lh, w2buf[:, fc * D:fc * D + 512],
                                                   start=(fc == 0), stop=(fc == 15)), sp, 1)
                                self.minc(t.matmul(bank[b0 + 1][:], lh, w2buf[:, fc * D + 512:(fc + 1) * D],
                                                   start=(fc == 0), stop=(fc == 15)), sp, 1)
                        eo_mm = sp.n

                        @block.vector
                        def _(v, tc2=tc2, b0=b0, tgc=tgc, s_i=s_i):
                            v.wait_ge(sp.h, eo_mm)
                            if s_i == 0:
                                v.wait_ge(sv.h, gates_done)
                            asl = acc[:, tgc * D:(tgc + 1) * D]
                            for half in range(2):
                                pslice = bank[b0 + half][:]
                                osl = acc[:, tgc * D + half * 512:tgc * D + (half + 1) * 512]
                                if s_i == 0:
                                    v.tensor_scalar(osl, pslice, 1.0, None, ALU.mult)
                                else:
                                    v.scalar_tensor_tensor(osl, pslice,
                                                           gates[:, tgc * 8 + s_i - 1:tgc * 8 + s_i],
                                                           osl, ALU.mult, ALU.add)
                            self.dinc(v, sv, 1)
                        self.bank_set(b0, sv)
                        self.bank_set(b0 + 1, sv)
                spec_done[(hf, s_i)] = sp.n

            for tc in range(8):
                row0 = hf * 1024 + tc * 128
                accd = sv.n

                @block.scalar
                def _(s, tc=tc):
                    s.wait_ge(sv.h, accd)
                    asl = acc[:, tc * D:(tc + 1) * D]
                    s.activation(scrB[:], asl, AF.Square, accum_out=sq8[:, tc:tc + 1])
                    s.activation(scrB[:], asl, AF.Identity, accum_out=st8[:, tc:tc + 1])
                    self.dinc(s, sa, 1)
                std = sa.n

                @block.sync
                def _(sync, row0=row0, tc=tc):
                    sync.wait_ge(sv.h, sv.n)
                    sync.wait_ge(self.so.h, self.so.n)
                    self.minc(sync.dma_start(out=xph[:], in_=d["xp_s"][row0:row0 + 128, :]), si, 16)
                xld = si.n

                @block.vector
                def _(v, tc=tc):
                    v.wait_ge(sa.h, std)
                    cl = slice(tc, tc + 1)
                    v.tensor_scalar(mu8[:, cl], st8[:, cl], 1.0 / D, None, ALU.mult)
                    v.drain()
                    v.tensor_tensor(tA8[:, cl], mu8[:, cl], mu8[:, cl], ALU.mult)
                    v.drain()
                    v.scalar_tensor_tensor(va8[:, cl], sq8[:, cl], 1.0 / D, tA8[:, cl],
                                           ALU.mult, ALU.subtract)
                    v.drain()
                    v.tensor_scalar(va8[:, cl], va8[:, cl], EPS, None, ALU.add)
                    self.dinc(v, sv, 1)
                ved = sv.n

                @block.scalar
                def _(s, tc=tc):
                    s.wait_ge(sv.h, ved)
                    cl = slice(tc, tc + 1)
                    s.activation(va8[:, cl], va8[:, cl], AF.Ln)
                    s.drain()
                    s.activation(va8[:, cl], va8[:, cl], AF.Exp, scale=-0.5)
                    self.dinc(s, sa, 1)
                ald = sa.n

                @block.vector
                def _(v, tc=tc):
                    v.wait_ge(sa.h, ald)
                    v.wait_ge(si.h, xld)
                    cl = slice(tc, tc + 1)
                    v.scalar_tensor_tensor(nm8[:, cl], mu8[:, cl], -1.0, va8[:, cl],
                                           ALU.mult, ALU.mult)
                    v.drain()
                    v.tensor_scalar(xph[:], xph[:], 1.0, nm8[:, cl], ALU.mult, ALU.add)
                    v.drain()
                    v.scalar_tensor_tensor(xph[:], acc[:, tc * D:(tc + 1) * D], va8[:, cl],
                                           xph[:], ALU.mult, ALU.add)
                    self.dinc(v, sv, 1)
                outd = sv.n

                @block.sync
                def _(sync, row0=row0, outd=outd):
                    sync.wait_ge(sv.h, outd)
                    self.minc(sync.dma_start(out=d["out"][row0:row0 + 128, :], in_=xph[:]), self.so, 16)
                if hf == 1 and tc == 7:
                    self.dbg("xph_last", xph[:])
            if hf == 0:
                self.dbg("gates0", gates[:])
                self.dbg("acc0", acc[:, 0:D])
                self.dbg("hT0", hT[:, 0:D])
                self.dbg("w1T0", w1T[:, 0:2048])
                self.dbg("mid0", mid[:, 0:1024])
            else:
                self.dbg("accF", acc[:])
                self.dbg("gatesF", gates[:])

        @block.sync
        def _(sync):
            sync.wait_ge(self.so.h, self.so.n)

    def emit_dumps(self):
        d, block = self.d, self.block
        if not self.dump:
            return

        @block.sync
        def _(sync):
            for s in (self.sg, self.sv, self.sa, self.sp, self.sc):
                sync.wait_ge(s.h, s.n)
            for name in self.dump:
                if name.startswith("@"):
                    continue
                self.minc(sync.dma_start(out=d["dump_" + name], in_=d[name]), self.so, 16)
            sync.wait_ge(self.so.h, self.so.n)


def _host_constants():
    g = 1.0 - np.exp2(-5.0 - np.arange(H, dtype=np.float64))
    i = np.arange(C, dtype=np.float64)
    dmask = np.where(i[:, None] >= i[None, :], g[:, None, None] ** (i[:, None] - i[None, :]), 0.0)
    dmaskT = dmask.transpose(0, 2, 1)
    dmaskT = np.ascontiguousarray(dmaskT.transpose(1, 0, 2).reshape(C, 4 * C))
    cross = g[:, None] ** (i + 1.0)
    crossrep = np.broadcast_to(cross.reshape(1, 4 * C), (128, 4 * C))
    kdec = g[:, None] ** (C - 1.0 - i)
    kdecP = np.ascontiguousarray(kdec.T)
    inv_freq = 1.0 / 10000.0 ** (np.arange(0, DK, 2, dtype=np.float64) / DK)
    freqs = np.arange(T, dtype=np.float64)[:, None] * inv_freq[None, :]
    cosT = np.cos(freqs).T
    sinT = np.sin(freqs).T
    ident = np.eye(128, dtype=np.float32)
    f32 = lambda a: np.ascontiguousarray(np.asarray(a, np.float64), dtype=np.float32)
    bf = lambda a: np.ascontiguousarray(np.asarray(a, np.float32).astype(ml_dtypes.bfloat16))
    return {"dmaskT": f32(dmaskT), "crossrep": bf(crossrep), "kdecP": f32(kdecP),
            "cosT": bf(cosT), "sinT": bf(sinT), "ident": bf(ident)}


def _rep128(vec):
    v = np.asarray(vec, np.float32)
    return np.ascontiguousarray(np.broadcast_to(v[None, :], (128, v.shape[0])))


def kernel(**inputs):
    consts = _host_constants()
    gnw = np.asarray(inputs["L0_attn_g_norm_w"], np.float32)
    gnwP = np.ascontiguousarray(np.tile(gnw, 4).reshape(16, 128).T)
    f32c = lambda a: np.ascontiguousarray(np.asarray(a, np.float32))

    shared_in = {
        "Wq": f32c(inputs["L0_attn_Wq"]), "Wk": f32c(inputs["L0_attn_Wk"]),
        "Wv": f32c(inputs["L0_attn_Wv"]), "Wg": f32c(inputs["L0_attn_Wg"]),
        "Wo": f32c(inputs["L0_attn_Wo"]),
        "lnw_rep": _rep128(inputs["L0_attn_ln_w"]),
        "lnb_rep": _rep128(inputs["L0_attn_ln_b"]),
        "fgw_rep": _rep128(inputs["L0_ffn_ln_gamma"]),
        "fgb_rep": _rep128(inputs["L0_ffn_ln_beta"]),
        "gnwP": gnwP,
        "router": f32c(inputs["L0_ffn_router_w"]),
        "sw1": f32c(inputs["L0_ffn_shared_w1"]), "sw2": f32c(inputs["L0_ffn_shared_w2"]),
        "ew1": f32c(inputs["L0_ffn_expert_w1"]).reshape(8 * 2 * D, D),
        "ew2": f32c(inputs["L0_ffn_expert_w2"]).reshape(8 * D, 2 * D),
        **consts,
    }
    x = np.asarray(inputs["L0_attn_x"], np.float32)
    in_maps = [dict(shared_in, x=np.ascontiguousarray(x[c])) for c in range(N_CORES)]

    b = Build(DUMP)
    nc = b.build()
    res = run_bass_kernel_spmd(nc, in_maps, core_ids=list(range(N_CORES)))
    if DUMP:
        kernel.last_results = res.results
    return np.stack([np.asarray(res.results[c]["out"]) for c in range(N_CORES)], axis=0)


# version 11
# speedup vs baseline: 1.4289x; 1.1033x over previous
import contextlib

import numpy as np
import ml_dtypes

import concourse.bass as bass
import concourse.mybir as mybir
from concourse.bass_utils import run_bass_kernel_spmd

F32 = mybir.dt.float32
BF16 = mybir.dt.bfloat16
AF = mybir.ActivationFunctionType
ALU = mybir.AluOpType
AX = mybir.AxisListType

T, D, H, C = 2048, 1024, 4, 128
NCH = T // C
DK, DV = D // H, 2 * D // H
EPS = 1e-5
N_CORES = 8

DUMP = ()
B_SPECIES = 9
B_SINGLEBUF = False


class Sem:
    def __init__(self, h):
        self.h = h
        self.n = 0

    def bump(self, k=1):
        self.n += k
        return self.n


class Build:
    def __init__(self, dump=()):
        self.dump = tuple(dump)
        nc = bass.Bass("TRN2", target_bir_lowering=False)
        self.nc = nc
        d = self.d = {}

        def din(name, shape, dt=F32):
            d[name] = nc.dram_tensor(name, shape, dt, kind="ExternalInput").ap()

        din("x", [T, D])
        din("Wq", [D, D]); din("Wk", [D, D]); din("Wv", [D, 2 * D]); din("Wg", [D, 2 * D])
        din("Wo", [2 * D, D])
        din("lnw_rep", [128, D]); din("lnb_rep", [128, D])
        din("fgw_rep", [128, D]); din("fgb_rep", [128, D])
        din("gnwP", [128, 16])
        din("router", [8, D])
        din("sw1", [2 * D, D]); din("sw2", [D, 2 * D])
        din("ew1", [8 * 2 * D, D]); din("ew2", [8 * D, 2 * D])
        din("cosT", [128, T], BF16); din("sinT", [128, T], BF16)
        din("dmaskT", [128, 4 * 128])
        din("crossrep", [128, 4 * 128], BF16)
        din("kdecP", [128, 4])
        din("ident", [128, 128], BF16)
        d["out"] = nc.dram_tensor("out", [T, D], F32, kind="ExternalOutput").ap()

        def scr(name, shape, dt):
            d[name] = nc.dram_tensor(name, shape, dt, kind="Internal").ap()

        scr("qT_s", [D, T], BF16)
        scr("kT_s", [D, T], BF16)
        scr("kk_s", [T, D], BF16)
        scr("v_s", [T, 2 * D], BF16)
        scr("g_s", [T, 2 * D], BF16)
        scr("xp_s", [T, D], F32)
        scr("w1n_s", [9 * 2 * D, D], BF16)
        scr("w2n_s", [9 * D, 2 * D], BF16)

        for name in self.dump:
            if name.startswith("@"):
                continue
            d["dump_" + name] = nc.dram_tensor(
                "dump_" + name, list(d[name].shape), d[name].dtype, kind="ExternalOutput").ap()

        g64 = 1.0 - np.exp2(-5.0 - np.arange(H, dtype=np.float64))
        self.gC = [float(np.float32(v)) for v in g64 ** C]

    def dinc(self, eng, sem, k=1):
        eng.maybe_drain_then_inc((sem.h, k))
        sem.bump(k)

    def minc(self, inst, sem, k=1):
        inst.then_inc(sem.h, k)
        sem.bump(k)

    def dbg(self, name, ap):
        if "@" + name not in self.dump:
            return
        nc, block = self.nc, self.block
        shape = [int(s) for s in ap.shape]
        out = nc.dram_tensor("dump_@" + name, shape, ap.dtype, kind="ExternalOutput").ap()

        @block.gpsimd
        def _(g):
            for s in (self.sv, self.sa, self.sp, self.si, self.sg, self.sc):
                g.wait_ge(s.h, s.n)
            self.minc(g.dma_start(out=out, in_=ap), self.so, 16)

    def bank_wait(self, eng, i):
        g = self.bfree.get(i)
        if g is not None:
            eng.wait_ge(g[0].h, g[1])

    def bank_set(self, i, sem):
        self.bfree[i] = (sem, sem.n)

    def barrier(self):
        block = self.block
        for name, sem in (("tensor", self.sp), ("vector", self.sv),
                          ("scalar", self.sa), ("gpsimd", self.sg)):
            def _f(eng, sem=sem):
                self.dinc(eng, sem, 1)
            getattr(block, name)(_f)
        tot = [(s.h, s.n) for s in (self.si, self.sg, self.sc, self.sp, self.sv, self.sa)]
        for name in ("tensor", "vector", "scalar", "gpsimd", "sync"):
            def _f(eng, tot=tot):
                for h, n in tot:
                    eng.wait_ge(h, n)
            getattr(block, name)(_f)

    def build(self):
        nc, d = self.nc, self.d
        with contextlib.ExitStack() as ctx:
            E = ctx.enter_context
            sb = lambda name, shape, dt: E(nc.sbuf_tensor(name, shape, dt))
            self.bank = [E(nc.psum_tensor(f"bank{i}", [128, 512], F32)) for i in range(8)]
            self.bfree = {}

            self.si = Sem(E(nc.semaphore("si")))
            self.sg = Sem(E(nc.semaphore("sg")))
            self.sc = Sem(E(nc.semaphore("sc")))
            self.sp = Sem(E(nc.semaphore("sp")))
            self.sv = Sem(E(nc.semaphore("sv")))
            self.sa = Sem(E(nc.semaphore("sa")))
            self.so = Sem(E(nc.semaphore("so")))
            self.sq = Sem(E(nc.semaphore("sq")))
            block = self.block = E(nc.Block())

            self.ident = sb("s_ident", [128, 128], BF16)
            self.a1v = sb("s_a1v", [128, NCH], F32)
            self.nma1 = sb("s_nma1", [128, NCH], F32)

            si = self.si

            @block.sync
            def _(sync):
                self.minc(sync.dma_start(out=self.ident[:], in_=d["ident"]), si, 16)
            self.consts_loaded = si.n

            with contextlib.ExitStack() as pctx:
                self.phase_a1(pctx)
            self.barrier()
            with contextlib.ExitStack() as pctx:
                self.phase_a2(pctx)
            self.barrier()
            with contextlib.ExitStack() as qctx:
                self.init_q(qctx)
                self.emit_q(10 ** 9)
            self.barrier()
            with contextlib.ExitStack() as pctx:
                self.phase_b(pctx)
            self.emit_dumps()
        return nc

    def phase_a1(self, ctx):
        nc, d, block = self.nc, self.d, self.block
        si, sg, sp, sv, sa = self.si, self.sg, self.sp, self.sv, self.sa
        bank = self.bank
        sb = lambda name, shape, dt: ctx.enter_context(nc.sbuf_tensor(name, shape, dt))

        wq = sb("a_wq", [128, 8 * D], BF16)
        wk = sb("a_wk", [128, 8 * D], BF16)
        wv = sb("a_wv", [128, 8 * 2 * D], BF16)
        wg = sb("a_wg", [128, 8 * 2 * D], BF16)
        cosT = sb("a_cosT", [128, T], BF16)
        sinT = sb("a_sinT", [128, T], BF16)
        xin = sb("a_xin", [128, 4 * D], F32)
        s1 = sb("a_s1", [128, NCH], F32)
        ssq = sb("a_ssq", [128, NCH], F32)
        mu = sb("a_mu", [128, NCH], F32)
        var = sb("a_var", [128, NCH], F32)
        tA = sb("a_tA", [128, NCH], F32)
        tB = sb("a_tB", [128, NCH], F32)
        beta = sb("a_beta", [128, NCH], F32)
        nmb = sb("a_nmb", [128, NCH], F32)
        u1 = sb("a_u1", [128, 4 * D], F32)
        scr = u1[:, 0:D]
        nrm = sb("a_nrm", [128, 4 * D], BF16)
        nrmT = sb("a_nrmT", [128, 8 * 512], BF16)
        praw = sb("a_praw", [128, 8 * 512], BF16)
        ra = sb("a_ra", [128, 1024], BF16)
        rb = sb("a_rb", [128, 1024], BF16)
        pT = praw
        kkb = sb("a_kkb", [128, 4 * D], BF16)
        vgb = sb("a_vgb", [128, 4 * 512], BF16)

        kdecP = sb("a_kdecP", [128, 4], F32)
        self.lnw = sb("a_lnw", [128, D], F32)
        self.lnb = sb("a_lnb", [128, D], F32)

        @block.sync
        def _(sync):
            self.minc(sync.dma_start(out=kdecP[:], in_=d["kdecP"]), si, 16)
            self.minc(sync.dma_start(out=self.lnw[:], in_=d["lnw_rep"]), si, 16)
            self.minc(sync.dma_start(out=self.lnb[:], in_=d["lnb_rep"]), si, 16)
            self.minc(sync.dma_start(out=cosT[:], in_=d["cosT"]), si, 16)
            self.minc(sync.dma_start(out=sinT[:], in_=d["sinT"]), si, 16)
        consts2 = si.n
        trig_loaded = consts2

        xfree = {}

        @block.gpsimd
        def _(g):
            for name, dst, cn in (("Wq", wq, 8), ("Wk", wk, 8), ("Wv", wv, 8), ("Wg", wg, 8)):
                self.minc(g.dma_start(out=dst[:].rearrange("p (c f) -> p c f", c=cn),
                                      in_=d[name].rearrange("(c p) f -> p c f", p=128)), sg, 16)
        w_loaded = sg.n

        for tt in range(4):
            xbuf = xin
            cols = slice(tt * 4, tt * 4 + 4)

            @block.sync
            def _(sync, tt=tt, xbuf=xbuf):
                if tt >= 1:
                    sync.wait_ge(sv.h, xfree[tt - 1])
                src = d["x"].rearrange("(c p) f -> p c f", p=128)[:, tt * 4:tt * 4 + 4, :]
                self.minc(sync.dma_start(out=xbuf.rearrange("p (c f) -> p c f", c=4), in_=src), si, 16)
            x_loaded = si.n

            @block.scalar
            def _(s, tt=tt, xbuf=xbuf):
                s.wait_ge(si.h, x_loaded)
                for c in range(4):
                    tc = tt * 4 + c
                    xc = xbuf[:, c * D:(c + 1) * D]
                    s.activation(scr[:], xc, AF.Square, accum_out=ssq[:, tc:tc + 1])
                    s.activation(scr[:], xc, AF.Identity, accum_out=s1[:, tc:tc + 1])
                self.dinc(s, sa, 1)
            stats_done = sa.n
            if tt == 0:
                self.dbg("xin", xin[:])
                self.dbg("s1", s1[:])
                self.dbg("ssq", ssq[:])

            @block.vector
            def _(v, cols=cols):
                v.wait_ge(sa.h, stats_done)
                cl = cols
                v.tensor_scalar(mu[:, cl], s1[:, cl], 1.0 / D, None, ALU.mult)
                v.drain()
                v.tensor_tensor(tA[:, cl], mu[:, cl], mu[:, cl], ALU.mult)
                v.drain()
                v.scalar_tensor_tensor(var[:, cl], ssq[:, cl], 1.0 / D, tA[:, cl], ALU.mult, ALU.subtract)
                v.drain()
                v.tensor_scalar(tA[:, cl], var[:, cl], EPS, None, ALU.add)
                self.dinc(v, sv, 1)
            ve_done = sv.n

            @block.scalar
            def _(s, cols=cols):
                s.wait_ge(sv.h, ve_done)
                s.activation(tB[:, cols], tA[:, cols], AF.Ln)
                s.drain()
                s.activation(tB[:, cols], tB[:, cols], AF.Exp, scale=0.5)
                self.dinc(s, sa, 1)
            sd1_done = sa.n

            @block.vector
            def _(v, cols=cols):
                cl = cols
                v.wait_ge(sa.h, sd1_done)
                v.reciprocal(self.a1v[:, cl], tB[:, cl])
                v.reciprocal(tB[:, cl], tA[:, cl])
                v.drain()
                v.tensor_tensor(tB[:, cl], var[:, cl], tB[:, cl], ALU.mult)
                v.drain()
                v.tensor_scalar(tB[:, cl], tB[:, cl], EPS, None, ALU.add)
                self.dinc(v, sv, 1)
            v1_done = sv.n

            @block.scalar
            def _(s, cols=cols):
                s.wait_ge(sv.h, v1_done)
                s.activation(tA[:, cols], tB[:, cols], AF.Ln)
                s.drain()
                s.activation(tA[:, cols], tA[:, cols], AF.Exp, scale=0.5)
                self.dinc(s, sa, 1)
            sd2_done = sa.n

            @block.vector
            def _(v, cols=cols, tt=tt, xbuf=xbuf):
                cl = cols
                v.wait_ge(sa.h, sd2_done)
                if tt > 0:
                    v.wait_ge(sp.h, sp.n)
                v.wait_ge(si.h, self.consts_loaded)
                v.reciprocal(tB[:, cl], tA[:, cl])
                v.drain()
                v.tensor_tensor(beta[:, cl], self.a1v[:, cl], tB[:, cl], ALU.mult)
                v.drain()
                v.scalar_tensor_tensor(nmb[:, cl], mu[:, cl], -1.0, beta[:, cl], ALU.mult, ALU.mult)
                v.scalar_tensor_tensor(self.nma1[:, cl], mu[:, cl], -1.0, self.a1v[:, cl], ALU.mult, ALU.mult)
                v.drain()
                for c in range(4):
                    tc = tt * 4 + c
                    v.tensor_scalar(u1[:, c * D:(c + 1) * D], xbuf[:, c * D:(c + 1) * D],
                                    beta[:, tc:tc + 1], nmb[:, tc:tc + 1], ALU.mult, ALU.add)
                v.drain()
                for c in range(4):
                    v.tensor_tensor(u1[:, c * D:(c + 1) * D], u1[:, c * D:(c + 1) * D],
                                    self.lnw[:], ALU.mult)
                v.drain()
                for c in range(4):
                    v.tensor_tensor(nrm[:, c * D:(c + 1) * D], u1[:, c * D:(c + 1) * D],
                                    self.lnb[:], ALU.add)
                self.dinc(v, sv, 1)
            nrm_done = sv.n
            xfree[tt] = nrm_done
            if tt == 0:
                self.dbg("beta", beta[:])
                self.dbg("nmb", nmb[:])
                self.dbg("nrm", nrm[:])
                self.dbg("mu", mu[:])
                self.dbg("var", var[:])
                self.dbg("a1v", self.a1v[:])

            for grp in range(8):
                bi = grp % 2
                bk = bank[bi]

                @block.tensor
                def _(t, grp=grp, bk=bk, bi=bi):
                    t.wait_ge(sv.h, nrm_done)
                    if grp == 0:
                        t.wait_ge(si.h, 16)
                    self.bank_wait(t, bi)
                    for j in range(4):
                        i = grp * 4 + j
                        c, dc = i // 8, i % 8
                        self.minc(t.matmul(bk[:, j * 128:(j + 1) * 128],
                                           nrm[:, c * D + dc * 128:c * D + (dc + 1) * 128],
                                           self.ident[:], start=True, stop=True), sp, 1)
                pe_cnt = sp.n

                @block.scalar
                def _(s, grp=grp, bk=bk):
                    s.wait_ge(sp.h, pe_cnt)
                    for j in range(4):
                        i = grp * 4 + j
                        c, dc = i // 8, i % 8
                        s.copy(nrmT[:, dc * 512 + c * 128:dc * 512 + (c + 1) * 128],
                               bk[:, j * 128:(j + 1) * 128])
                    self.dinc(s, sa, 1)
                self.bank_set(bi, sa)
            nrmT_ready = sa.n

            for which, wmat, dst_s, scale in (("q", wq, "qT_s", DK ** -0.5),
                                              ("k", wk, "kT_s", 1.0)):
                for fp in range(4):
                    @block.tensor
                    def _(t, fp=fp, wmat=wmat, which=which, tt=tt):
                        if which == "q" and fp == 0:
                            t.wait_ge(sa.h, nrmT_ready)
                        if which == "q" and tt == 0 and fp == 0:
                            t.wait_ge(sg.h, w_loaded)
                        for half in range(2):
                            fc = fp * 2 + half
                            self.bank_wait(t, 2 + half)
                            for kc in range(8):
                                self.minc(t.matmul(bank[2 + half][:],
                                                   wmat[:, kc * D + fc * 128:kc * D + (fc + 1) * 128],
                                                   nrmT[:, kc * 512:(kc + 1) * 512],
                                                   start=(kc == 0), stop=(kc == 7)), sp, 1)
                    pe_cnt = sp.n

                    @block.scalar
                    def _(s, fp=fp, scale=scale, pe_cnt=pe_cnt):
                        s.wait_ge(sp.h, pe_cnt)
                        s.wait_ge(sv.h, sv.n)
                        s.wait_ge(sg.h, sg.n)
                        for half in range(2):
                            fc = fp * 2 + half
                            s.activation(praw[:, fc * 512:(fc + 1) * 512], bank[2 + half][:],
                                         AF.Copy, scale=float(scale))
                        self.dinc(s, sa, 1)
                    self.bank_set(2, sa)
                    self.bank_set(3, sa)
                praw_done = sa.n

                @block.vector
                def _(v, tt=tt):
                    v.wait_ge(sa.h, praw_done)
                    v.wait_ge(sg.h, sg.n)
                    v.wait_ge(si.h, trig_loaded)
                    cs = cosT[:, tt * 512:(tt + 1) * 512]
                    sn = sinT[:, tt * 512:(tt + 1) * 512]
                    for h in range(H):
                        t1 = praw[:, (2 * h) * 512:(2 * h + 1) * 512]
                        t2 = praw[:, (2 * h + 1) * 512:(2 * h + 2) * 512]
                        v.tensor_tensor(ra[:, 0:512], t1, cs, ALU.mult)
                        v.tensor_tensor(ra[:, 512:1024], t2, sn, ALU.mult)
                        v.tensor_tensor(rb[:, 0:512], t2, cs, ALU.mult)
                        v.tensor_tensor(rb[:, 512:1024], t1, sn, ALU.mult)
                        v.drain()
                        v.tensor_tensor(pT[:, (2 * h) * 512:(2 * h + 1) * 512],
                                        ra[:, 0:512], ra[:, 512:1024], ALU.subtract)
                        v.tensor_tensor(pT[:, (2 * h + 1) * 512:(2 * h + 2) * 512],
                                        rb[:, 0:512], rb[:, 512:1024], ALU.add)
                    self.dinc(v, sv, 1)
                rope_done = sv.n

                @block.gpsimd
                def _(g, tt=tt, dst_s=dst_s):
                    g.wait_ge(sv.h, rope_done)
                    dst = d[dst_s].rearrange("(fc p) t -> p fc t", p=128)[:, :, tt * 512:(tt + 1) * 512]
                    self.minc(g.dma_start(out=dst, in_=pT[:].rearrange("p (fc t) -> p fc t", fc=8)), sg, 16)

                if which == "k":
                    for grp in range(8):
                        bi = 4 + grp % 2
                        bk = bank[bi]

                        @block.tensor
                        def _(t, grp=grp, bk=bk, bi=bi):
                            t.wait_ge(sv.h, rope_done)
                            self.bank_wait(t, bi)
                            for j in range(4):
                                i = grp * 4 + j
                                dc, c = i // 4, i % 4
                                self.minc(t.matmul(bk[:, j * 128:(j + 1) * 128],
                                                   pT[:, dc * 512 + c * 128:dc * 512 + (c + 1) * 128],
                                                   self.ident[:], start=True, stop=True), sp, 1)
                        pe_cnt = sp.n

                        @block.vector
                        def _(v, grp=grp, bk=bk):
                            v.wait_ge(sp.h, pe_cnt)
                            if grp == 0:
                                v.wait_ge(sg.h, sg.n)
                            for j in range(4):
                                i = grp * 4 + j
                                dc, c = i // 4, i % 4
                                h = dc // 2
                                v.tensor_scalar(kkb[:, c * D + dc * 128:c * D + (dc + 1) * 128],
                                                bk[:, j * 128:(j + 1) * 128],
                                                kdecP[:, h:h + 1], None, ALU.mult)
                            self.dinc(v, sv, 1)
                        self.bank_set(bi, sv)
                    kk_done = sv.n

                    @block.gpsimd
                    def _(g, tt=tt):
                        g.wait_ge(sv.h, kk_done)
                        dst = d["kk_s"][tt * 512:(tt + 1) * 512, :].rearrange("(c p) f -> p c f", p=128)
                        self.minc(g.dma_start(out=dst, in_=kkb[:].rearrange("p (c f) -> p c f", c=4)), sg, 16)

            for which, wmat, dst_s in (("v", wv, "v_s"), ("g", wg, "g_s")):
                for c in range(4):
                    vb = (c % 2) * 4
                    @block.tensor
                    def _(t, c=c, vb=vb, wmat=wmat):
                        for vc in range(4):
                            self.bank_wait(t, vb + vc)
                            for kc in range(8):
                                self.minc(t.matmul(bank[vb + vc][:],
                                                   nrmT[:, kc * 512 + c * 128:kc * 512 + (c + 1) * 128],
                                                   wmat[:, kc * 2 * D + vc * 512:kc * 2 * D + (vc + 1) * 512],
                                                   start=(kc == 0), stop=(kc == 7)), sp, 1)
                    pe_cnt = sp.n

                    @block.scalar
                    def _(s, which=which, vb=vb, pe_cnt=pe_cnt):
                        s.wait_ge(sp.h, pe_cnt)
                        s.wait_ge(sg.h, sg.n)
                        for vc in range(4):
                            if which == "g":
                                s.activation(vgb[:, vc * 512:(vc + 1) * 512], bank[vb + vc][:], AF.Silu)
                            else:
                                s.copy(vgb[:, vc * 512:(vc + 1) * 512], bank[vb + vc][:])
                        self.dinc(s, sa, 1)
                    for vc in range(4):
                        self.bank_set(vb + vc, sa)
                    row_done = sa.n

                    @block.gpsimd
                    def _(g, tt=tt, c=c, dst_s=dst_s):
                        g.wait_ge(sa.h, row_done)
                        dst = d[dst_s][tt * 512 + c * 128:tt * 512 + (c + 1) * 128, :]
                        self.minc(g.dma_start(out=dst, in_=vgb[:]), sg, 16)


        self.a1_spills = sg.n

    def phase_a2(self, ctx):
        nc, d, block = self.nc, self.d, self.block
        si, sg, sp, sv, sa = self.si, self.sg, self.sp, self.sv, self.sa
        bank = self.bank
        sb = lambda name, shape, dt: ctx.enter_context(nc.sbuf_tensor(name, shape, dt))
        gCs = self.gC

        dmaskT = sb("c_dmaskT", [128, 512], F32)
        crossrep = sb("c_crossrep", [128, 512], BF16)
        gnwP = sb("c_gnwP", [128, 16], F32)
        self.S = sb("c_S", [128, 8 * 512], F32)
        self.Sb = sb("c_Sb", [128, 8 * 512], BF16)
        wo = sb("c_wo", [128, 16 * D], BF16)
        wostg = sb("c_wostg", [128, 2 * D], F32)
        qgrp = sb("c_qgrp", [128, 2 * 2048], BF16)
        kgrp = sb("c_kgrp", [128, 2 * 2048], BF16)
        kkgrp = sb("c_kkgrp", [128, 2 * 2048], BF16)
        vgrp = sb("c_vgrp", [128, 2 * 4096], BF16)
        ggrp = sb("c_ggrp", [128, 2 * 4096], BF16)
        xgrp = sb("c_xgrp", [128, 2 * 2048], F32)
        qx = sb("c_qx", [128, 8 * 128], BF16)
        at = sb("c_at", [128, 4 * 128], BF16)
        og = sb("c_og", [128, 2048], BF16)
        ogT = sb("c_ogT", [128, 16 * 128], BF16)
        scr5 = sb("c_scr5", [128, 512], F32)
        ssqo = sb("c_ssqo", [128, 4], F32)
        rstd = sb("c_rstd", [128, 4], F32)
        yv = sb("c_y", [128, D], F32)
        af = sb("c_af", [128, D], F32)
        xp = sb("c_xp", [128, D], F32)
        sm1 = sb("c_sm1", [128, 6], F32)

        @block.sync
        def _(sync):
            self.minc(sync.dma_start(out=dmaskT[:], in_=d["dmaskT"]), si, 16)
            self.minc(sync.dma_start(out=crossrep[:], in_=d["crossrep"]), si, 16)
            self.minc(sync.dma_start(out=gnwP[:], in_=d["gnwP"]), si, 16)
        a2c_loaded = si.n

        @block.vector
        def _(v):
            v.memset(self.S[:], 0.0)
            self.dinc(v, sv, 1)
        S_zeroed = sv.n

        wo_ready = []
        for kc in range(16):
            @block.sync
            def _(sync, kc=kc):
                if kc >= 2:
                    sync.wait_ge(sv.h, wo_ready[kc - 2])
                self.minc(sync.dma_start(out=wostg[:, (kc % 2) * D:((kc % 2) + 1) * D],
                                         in_=d["Wo"][kc * 128:(kc + 1) * 128, :]), si, 16)
            ld = si.n

            @block.vector
            def _(v, kc=kc):
                v.wait_ge(si.h, max(ld, a2c_loaded))
                v.tensor_scalar(wo[:, kc * D:(kc + 1) * D], wostg[:, (kc % 2) * D:((kc % 2) + 1) * D],
                                gnwP[:, kc:kc + 1], None, ALU.mult)
                self.dinc(v, sv, 1)
            wo_ready.append(sv.n)
        wo_done = sv.n

        group_done = {}
        group_loaded = {}
        prev = {}
        for n in range(NCH):
            gidx, par = n // 2, n % 2
            gb = (gidx % 2)

            if par == 0:
                @block.sync
                def _(sync, n=n, gidx=gidx, gb=gb):
                    if gidx >= 2:
                        sync.wait_ge(sv.h, group_done[gidx - 2])
                    r0 = n * 128
                    self.minc(sync.dma_start(
                        out=qgrp[:, gb * 2048:(gb + 1) * 2048].rearrange("p (fc t) -> p fc t", fc=8),
                        in_=d["qT_s"].rearrange("(fc p) t -> p fc t", p=128)[:, :, r0:r0 + 256]), si, 16)
                    self.minc(sync.dma_start(
                        out=kgrp[:, gb * 2048:(gb + 1) * 2048].rearrange("p (fc t) -> p fc t", fc=8),
                        in_=d["kT_s"].rearrange("(fc p) t -> p fc t", p=128)[:, :, r0:r0 + 256]), si, 16)
                    self.minc(sync.dma_start(
                        out=kkgrp[:, gb * 2048:(gb + 1) * 2048].rearrange("p (c f) -> p c f", c=2),
                        in_=d["kk_s"][r0:r0 + 256, :].rearrange("(c p) f -> p c f", p=128)), si, 16)
                    self.minc(sync.dma_start(
                        out=vgrp[:, gb * 4096:(gb + 1) * 4096].rearrange("p (c f) -> p c f", c=2),
                        in_=d["v_s"][r0:r0 + 256, :].rearrange("(c p) f -> p c f", p=128)), si, 16)
                    self.minc(sync.dma_start(
                        out=ggrp[:, gb * 4096:(gb + 1) * 4096].rearrange("p (c f) -> p c f", c=2),
                        in_=d["g_s"][r0:r0 + 256, :].rearrange("(c p) f -> p c f", p=128)), si, 16)
                    self.minc(sync.dma_start(
                        out=xgrp[:, gb * 2048:(gb + 1) * 2048].rearrange("p (c f) -> p c f", c=2),
                        in_=d["x"][r0:r0 + 256, :].rearrange("(c p) f -> p c f", p=128)), si, 16)
                group_loaded[gidx] = si.n

            ld = group_loaded[gidx]
            qch = lambda i, gb=gb, par=par: qgrp[:, gb * 2048 + i * 256 + par * 128:gb * 2048 + i * 256 + (par + 1) * 128]
            kch = lambda i, gb=gb, par=par: kgrp[:, gb * 2048 + i * 256 + par * 128:gb * 2048 + i * 256 + (par + 1) * 128]
            kkch = lambda i, gb=gb, par=par: kkgrp[:, gb * 2048 + par * 1024 + i * 128:gb * 2048 + par * 1024 + (i + 1) * 128]
            vch = lambda h, gb=gb, par=par: vgrp[:, gb * 4096 + par * 2048 + h * 512:gb * 4096 + par * 2048 + (h + 1) * 512]
            gch = lambda h, gb=gb, par=par: ggrp[:, gb * 4096 + par * 2048 + h * 512:gb * 4096 + par * 2048 + (h + 1) * 512]
            xch = lambda gb=gb, par=par: xgrp[:, gb * 2048 + par * 1024:gb * 2048 + (par + 1) * 1024]

            @block.vector
            def _(v, n=n):
                v.wait_ge(sv.h, S_zeroed if n == 0 else prev["supd"])
                if n > 0:
                    v.wait_ge(sp.h, prev["o_mm"])
                for i in range(8):
                    v.tensor_copy(self.Sb[:, i * 512:(i + 1) * 512], self.S[:, i * 512:(i + 1) * 512])
                self.dinc(v, sv, 1)
            cast_done = sv.n

            @block.vector
            def _(v, n=n):
                v.wait_ge(si.h, max(ld, a2c_loaded))
                if n > 0:
                    v.wait_ge(sp.h, prev["o_mm"])
                for i in range(8):
                    h = i // 2
                    v.tensor_tensor(qx[:, i * 128:(i + 1) * 128], qch(i),
                                    crossrep[:, h * 128:(h + 1) * 128], ALU.mult)
                self.dinc(v, sv, 1)
            qx_done = sv.n

            @block.tensor
            def _(t, n=n):
                t.wait_ge(si.h, ld)
                self.bank_wait(t, 0)
                for h in range(H):
                    for dc in range(2):
                        self.minc(t.matmul(bank[0][:, h * 128:(h + 1) * 128],
                                           kch(2 * h + dc), qch(2 * h + dc),
                                           start=(dc == 0), stop=(dc == 1)), sp, 1)
            attnT_mm = sp.n

            @block.vector
            def _(v, n=n):
                v.wait_ge(sp.h, attnT_mm)
                if n > 0:
                    v.wait_ge(sp.h, prev["o_mm"])
                for h in range(H):
                    v.tensor_tensor(at[:, h * 128:(h + 1) * 128], bank[0][:, h * 128:(h + 1) * 128],
                                    dmaskT[:, h * 128:(h + 1) * 128], ALU.mult)
                self.dinc(v, sv, 1)
            mask_done = sv.n
            self.bank_set(0, sv)

            @block.tensor
            def _(t, n=n):
                t.wait_ge(sv.h, mask_done)
                t.wait_ge(sv.h, cast_done)
                t.wait_ge(sv.h, qx_done)
                for h in range(H):
                    self.bank_wait(t, 2 + h)
                    self.minc(t.matmul(bank[2 + h][:], at[:, h * 128:(h + 1) * 128], vch(h),
                                       start=True, stop=False), sp, 1)
                    for dc in range(2):
                        self.minc(t.matmul(bank[2 + h][:], qx[:, (2 * h + dc) * 128:(2 * h + dc + 1) * 128],
                                           self.Sb[:, (2 * h + dc) * 512:(2 * h + dc + 1) * 512],
                                           start=False, stop=(dc == 1)), sp, 1)
            o_mm = sp.n

            supd_marks = []
            for h in range(H):
                @block.tensor
                def _(t, h=h, n=n):
                    for dc in range(2):
                        bi = 6 + dc
                        self.bank_wait(t, bi)
                        self.minc(t.matmul(bank[bi][:], kkch(2 * h + dc), vch(h),
                                           start=True, stop=True), sp, 1)
                ds_mm = sp.n

                @block.vector
                def _(v, h=h, n=n, ds_mm=ds_mm):
                    v.wait_ge(sp.h, ds_mm)
                    if n == 0:
                        v.wait_ge(sv.h, S_zeroed)
                    for dc in range(2):
                        sl = slice((2 * h + dc) * 512, (2 * h + dc + 1) * 512)
                        v.scalar_tensor_tensor(self.S[:, sl], self.S[:, sl], gCs[h],
                                               bank[6 + dc][:], ALU.mult, ALU.add)
                    self.dinc(v, sv, 1)
                self.bank_set(6, sv)
                self.bank_set(7, sv)
            supd = sv.n

            @block.scalar
            def _(s, n=n):
                s.wait_ge(sp.h, o_mm)
                for h in range(H):
                    s.activation(scr5[:], bank[2 + h][:], AF.Square, accum_out=ssqo[:, h:h + 1])
                self.dinc(s, sa, 1)
            sq_done = sa.n

            @block.vector
            def _(v):
                v.wait_ge(sa.h, sq_done)
                v.tensor_scalar(rstd[:], ssqo[:], 1.0 / DV, EPS, ALU.mult, ALU.add)
                self.dinc(v, sv, 1)
            ve2 = sv.n

            @block.scalar
            def _(s):
                s.wait_ge(sv.h, ve2)
                s.activation(rstd[:], rstd[:], AF.Ln)
                s.drain()
                s.activation(rstd[:], rstd[:], AF.Exp, scale=-0.5)
                self.dinc(s, sa, 1)
            rstd_done = sa.n

            @block.vector
            def _(v, n=n):
                v.wait_ge(sa.h, rstd_done)
                if n > 0:
                    v.wait_ge(sp.h, prev["ogT_mm"])
                for h in range(H):
                    v.scalar_tensor_tensor(og[:, h * 512:(h + 1) * 512], bank[2 + h][:],
                                           rstd[:, h:h + 1], gch(h),
                                           ALU.mult, ALU.mult)
                self.dinc(v, sv, 1)
            og_done = sv.n
            for h in range(H):
                self.bank_set(2 + h, sv)

            for grp in range(4):
                bi = grp % 2
                @block.tensor
                def _(t, grp=grp, bi=bi):
                    t.wait_ge(sv.h, og_done)
                    self.bank_wait(t, bi)
                    for j in range(4):
                        kc = grp * 4 + j
                        self.minc(t.matmul(bank[bi][:, j * 128:(j + 1) * 128],
                                           og[:, kc * 128:(kc + 1) * 128], self.ident[:],
                                           start=True, stop=True), sp, 1)
                pe_cnt = sp.n

                @block.scalar
                def _(s, grp=grp, bi=bi, pe_cnt=pe_cnt):
                    s.wait_ge(sp.h, pe_cnt)
                    for j in range(4):
                        kc = grp * 4 + j
                        s.copy(ogT[:, kc * 128:(kc + 1) * 128], bank[bi][:, j * 128:(j + 1) * 128])
                    self.dinc(s, sa, 1)
                self.bank_set(bi, sa)
            ogT_mm = sp.n
            ogT_done = sa.n

            @block.tensor
            def _(t, n=n):
                t.wait_ge(sa.h, ogT_done)
                t.wait_ge(sv.h, wo_done)
                self.bank_wait(t, 2)
                self.bank_wait(t, 3)
                for half in range(2):
                    for kc in range(16):
                        self.minc(t.matmul(bank[2 + half][:], ogT[:, kc * 128:(kc + 1) * 128],
                                           wo[:, kc * D + half * 512:kc * D + (half + 1) * 512],
                                           start=(kc == 0), stop=(kc == 15)), sp, 1)
            wo_mm = sp.n

            @block.vector
            def _(v, n=n):
                v.wait_ge(sp.h, wo_mm)
                v.tensor_scalar(yv[:], xch(), self.a1v[:, n:n + 1], self.nma1[:, n:n + 1],
                                ALU.mult, ALU.add)
                v.drain()
                for half in range(2):
                    v.scalar_tensor_tensor(af[:, half * 512:(half + 1) * 512], bank[2 + half][:],
                                           1.0, yv[:, half * 512:(half + 1) * 512], ALU.mult, ALU.add)
                self.dinc(v, sv, 1)
            af_done = sv.n
            self.bank_set(2, sv)
            self.bank_set(3, sv)

            @block.scalar
            def _(s):
                s.wait_ge(sv.h, af_done)
                s.activation(scr5[:], af[:, 0:512], AF.Square, accum_out=sm1[:, 0:1])
                s.activation(scr5[:], af[:, 512:1024], AF.Square, accum_out=sm1[:, 1:2])
                s.activation(scr5[:], af[:, 0:512], AF.Identity, accum_out=sm1[:, 2:3])
                s.activation(scr5[:], af[:, 512:1024], AF.Identity, accum_out=sm1[:, 3:4])
                self.dinc(s, sa, 1)
            ost_done = sa.n

            @block.vector
            def _(v):
                v.wait_ge(sa.h, ost_done)
                v.tensor_tensor(sm1[:, 4:5], sm1[:, 2:3], sm1[:, 3:4], ALU.add)
                v.drain()
                v.tensor_scalar(sm1[:, 5:6], sm1[:, 4:5], 1.0 / D, None, ALU.mult)
                v.drain()
                v.tensor_tensor(sm1[:, 4:5], sm1[:, 0:1], sm1[:, 1:2], ALU.add)
                v.drain()
                v.tensor_tensor(sm1[:, 2:3], sm1[:, 5:6], sm1[:, 5:6], ALU.mult)
                v.drain()
                v.scalar_tensor_tensor(sm1[:, 4:5], sm1[:, 4:5], 1.0 / D, sm1[:, 2:3],
                                       ALU.mult, ALU.subtract)
                v.drain()
                v.tensor_scalar(sm1[:, 4:5], sm1[:, 4:5], EPS, None, ALU.add)
                self.dinc(v, sv, 1)
            ve3 = sv.n

            @block.scalar
            def _(s):
                s.wait_ge(sv.h, ve3)
                s.activation(sm1[:, 4:5], sm1[:, 4:5], AF.Ln)
                s.drain()
                s.activation(sm1[:, 4:5], sm1[:, 4:5], AF.Exp, scale=-0.5)
                self.dinc(s, sa, 1)
            al_done = sa.n

            @block.vector
            def _(v, n=n):
                v.wait_ge(sa.h, al_done)
                if n > 0:
                    v.wait_ge(sg.h, prev["spill"])
                v.scalar_tensor_tensor(sm1[:, 2:3], sm1[:, 5:6], -1.0, sm1[:, 4:5],
                                       ALU.mult, ALU.mult)
                v.drain()
                v.tensor_scalar(xp[:], xch(), 1.0, sm1[:, 2:3], ALU.mult, ALU.add)
                v.drain()
                v.scalar_tensor_tensor(xp[:], af[:], sm1[:, 4:5], xp[:], ALU.mult, ALU.add)
                self.dinc(v, sv, 1)
            xp_done = sv.n

            @block.gpsimd
            def _(g, n=n):
                g.wait_ge(sv.h, xp_done)
                self.minc(g.dma_start(out=d["xp_s"][n * 128:(n + 1) * 128, :], in_=xp[:]), sg, 16)
            prev = dict(supd=supd, o_mm=o_mm, og=og_done, ogT_mm=ogT_mm, spill=sg.n)
            group_done[gidx] = xp_done

    def init_q(self, ctx):
        nc, d = self.nc, self.d
        sb = lambda name, shape, dt: ctx.enter_context(nc.sbuf_tensor(name, shape, dt))
        self.q_wraw = sb("q_wraw", [128, 8 * 1024], F32)
        self.q_wab = sb("q_wab", [128, 8 * 1024], F32)
        self.q_wsgn = sb("q_wsgn", [128, 8 * 1024], BF16)
        self.q_wout = sb("q_wout", [128, 8 * 1024], BF16)
        self.q_red = sb("q_red", [128, 8], F32)
        self.q_thr = sb("q_thr", [128, 8], F32)
        self.q_mcol = sb("q_mcol", [128, 8], F32)

        units = []
        for s_i in range(9):
            w1src = d["sw1"] if s_i == 0 else d["ew1"][(s_i - 1) * 2048:s_i * 2048, :]
            w2src = d["sw2"] if s_i == 0 else d["ew2"][(s_i - 1) * 1024:s_i * 1024, :]
            for it in range(16):
                units.append((w1src[it * 128:(it + 1) * 128, :],
                              d["w1n_s"][s_i * 2048 + it * 128:s_i * 2048 + (it + 1) * 128, :], 0))
            for it in range(8):
                r0 = s_i * 1024 + it * 128
                units.append((w2src[it * 128:(it + 1) * 128, 0:1024],
                              d["w2n_s"][r0:r0 + 128, 0:1024], 1))
                units.append((w2src[it * 128:(it + 1) * 128, 1024:2048],
                              d["w2n_s"][r0:r0 + 128, 1024:2048], 2))
        self.q_units = units
        self.q_pos = 0
        self.q_gmark = {}
        self.q_stmark = {}

    def emit_q(self, kmax):
        block = self.block
        si, sg, sv, sa = self.si, self.sg, self.sv, self.sa
        units = self.q_units
        red, thr, mcol = self.q_red, self.q_thr, self.q_mcol
        ngrp = len(units) // 4
        assert len(units) % 4 == 0
        for gi in range(ngrp):
            grp = units[gi * 4:gi * 4 + 4]
            sb0 = (gi % 2) * 4
            sl = lambda j, t, sb0=sb0: t[:, (sb0 + j) * 1024:(sb0 + j + 1) * 1024]
            cidx = lambda j, sb0=sb0: slice(sb0 + j, sb0 + j + 1)

            @block.sync
            def _(sync, gi=gi, grp=grp, sl=sl):
                if gi - 2 in self.q_gmark:
                    sync.wait_ge(sv.h, self.q_gmark[gi - 2])
                for j, (srcap, dstap, kind) in enumerate(grp):
                    self.minc(sync.dma_start(out=sl(j, self.q_wraw), in_=srcap), si, 16)
            ld = si.n

            @block.scalar
            def _(s, gi=gi, grp=grp, sl=sl, cidx=cidx):
                s.wait_ge(si.h, ld)
                if gi - 2 in self.q_gmark:
                    s.wait_ge(sv.h, self.q_gmark[gi - 2])
                for j in range(4):
                    s.activation(sl(j, self.q_wab), sl(j, self.q_wraw), AF.Abs,
                                 accum_out=red[:, cidx(j)])
                    s.activation(sl(j, self.q_wsgn), sl(j, self.q_wraw), AF.Sign)
                self.dinc(s, sa, 1)
            absd = sa.n

            @block.vector
            def _(v, gi=gi, grp=grp, sl=sl, cidx=cidx):
                v.wait_ge(sa.h, absd)
                if gi - 2 in self.q_stmark:
                    v.wait_ge(sg.h, self.q_stmark[gi - 2])
                for j, (srcap, dstap, kind) in enumerate(grp):
                    if kind == 2:
                        v.tensor_tensor(red[:, cidx(j)], red[:, cidx(j - 1)], red[:, cidx(j)], ALU.add)
                v.drain()
                for j, (srcap, dstap, kind) in enumerate(grp):
                    if kind == 1:
                        continue
                    FD = 1024 if kind == 0 else 2048
                    v.tensor_scalar(thr[:, cidx(j)], red[:, cidx(j)], 0.5 / FD, 5e-6, ALU.mult, ALU.max)
                    v.tensor_scalar(mcol[:, cidx(j)], red[:, cidx(j)], 1.0 / FD, 1e-5, ALU.mult, ALU.max)
                v.drain()
                for j, (srcap, dstap, kind) in enumerate(grp):
                    jj = j + 1 if kind == 1 else j
                    v.tensor_scalar(sl(j, self.q_wab), sl(j, self.q_wab),
                                    thr[:, cidx(jj)], None, ALU.is_gt)
                v.drain()
                for j, (srcap, dstap, kind) in enumerate(grp):
                    jj = j + 1 if kind == 1 else j
                    v.scalar_tensor_tensor(sl(j, self.q_wout), sl(j, self.q_wab),
                                           mcol[:, cidx(jj)], sl(j, self.q_wsgn),
                                           ALU.mult, ALU.mult)
                self.dinc(v, sv, 1)
            self.q_gmark[gi] = sv.n

            @block.gpsimd
            def _(g, gi=gi, grp=grp, sl=sl):
                g.wait_ge(sv.h, self.q_gmark[gi])
                for j, (srcap, dstap, kind) in enumerate(grp):
                    self.minc(g.dma_start(out=dstap, in_=sl(j, self.q_wout)), sg, 16)
            self.q_stmark[gi] = sg.n

    def phase_b(self, ctx):
        nc, d, block = self.nc, self.d, self.block
        si, sg, sc, sp, sv, sa = self.si, self.sg, self.sc, self.sp, self.sv, self.sa
        bank = self.bank
        sb = lambda name, shape, dt: ctx.enter_context(nc.sbuf_tensor(name, shape, dt))

        hT = sb("b_hT", [128, 8 * D], BF16)
        w1T = sb("b_w1T", [128, 2 * 8 * 2048], BF16)
        w2T = sb("b_w2T", [128, 2 * 16 * D], BF16)
        acc = sb("b_acc", [128, 8 * D], F32)
        mid = sb("b_mid", [128, 16 * 256], BF16)
        xph = sb("b_xph", [128, D], F32)
        scrB = sb("b_scrB", [128, D], F32)
        hb = sb("b_hb", [128, D], BF16)
        u1 = scrB
        rsb = sb("b_rsb", [8, D], BF16)
        rT = sb("b_rT", [128, 64], BF16)
        self.fgw = sb("b_fgw", [128, D], F32)
        self.fgb = sb("b_fgb", [128, D], F32)
        lgs = sb("b_lgs", [128, 8], F32)
        gates = sb("b_gates", [128, 8 * 8], F32)
        st8 = sb("b_st8", [128, 8], F32)
        sq8 = sb("b_sq8", [128, 8], F32)
        mu8 = sb("b_mu8", [128, 8], F32)
        va8 = sb("b_va8", [128, 8], F32)
        tA8 = sb("b_tA8", [128, 8], F32)
        tB8 = sb("b_tB8", [128, 8], F32)
        be8 = sb("b_be8", [128, 8], F32)
        nm8 = sb("b_nm8", [128, 8], F32)
        sxs = sb("b_sxs", [128, 4], F32)

        @block.gpsimd
        def _(g):
            self.minc(g.dma_start(out=rsb[:], in_=d["router"]), sg, 16)
        r_ld = sg.n

        @block.sync
        def _(sync):
            self.minc(sync.dma_start(out=self.fgw[:], in_=d["fgw_rep"]), si, 16)
            self.minc(sync.dma_start(out=self.fgb[:], in_=d["fgb_rep"]), si, 16)
        fg_ld = si.n

        @block.tensor
        def _(t):
            t.wait_ge(sg.h, r_ld)
            t.wait_ge(si.h, 16)
            self.bank_wait(t, 3)
            for dc in range(8):
                self.minc(t.matmul(bank[3][:, dc * 8:(dc + 1) * 8],
                                   rsb[0:8, dc * 128:(dc + 1) * 128],
                                   self.ident[0:8, 0:8], start=True, stop=True), sp, 1)
        rt_mm = sp.n

        @block.scalar
        def _(s):
            s.wait_ge(sp.h, rt_mm)
            s.copy(rT[:], bank[3][:, 0:64])
            self.dinc(s, sa, 1)
        self.bank_set(3, sa)
        rt_done = sa.n

        spec_done = {}
        for hf in range(2):
            for tc in range(8):
                row0 = hf * 1024 + tc * 128

                @block.sync
                def _(sync, row0=row0, tc=tc, hf=hf):
                    if hf > 0 or tc > 0:
                        sync.wait_ge(sv.h, sv.n)
                    self.minc(sync.dma_start(out=xph[:], in_=d["xp_s"][row0:row0 + 128, :]), si, 16)
                ld = si.n

                @block.scalar
                def _(s, tc=tc):
                    s.wait_ge(si.h, ld)
                    s.activation(scrB[:], xph[:], AF.Square, accum_out=sq8[:, tc:tc + 1])
                    s.activation(scrB[:], xph[:], AF.Identity, accum_out=st8[:, tc:tc + 1])
                    self.dinc(s, sa, 1)
                std = sa.n

                @block.vector
                def _(v, tc=tc):
                    v.wait_ge(sa.h, std)
                    cl = slice(tc, tc + 1)
                    v.tensor_scalar(mu8[:, cl], st8[:, cl], 1.0 / D, None, ALU.mult)
                    v.drain()
                    v.tensor_tensor(tA8[:, cl], mu8[:, cl], mu8[:, cl], ALU.mult)
                    v.drain()
                    v.scalar_tensor_tensor(va8[:, cl], sq8[:, cl], 1.0 / D, tA8[:, cl],
                                           ALU.mult, ALU.subtract)
                    v.drain()
                    v.tensor_scalar(tA8[:, cl], va8[:, cl], EPS, None, ALU.add)
                    self.dinc(v, sv, 1)
                ve = sv.n

                @block.scalar
                def _(s, tc=tc):
                    s.wait_ge(sv.h, ve)
                    cl = slice(tc, tc + 1)
                    s.activation(tB8[:, cl], tA8[:, cl], AF.Ln)
                    s.drain()
                    s.activation(tB8[:, cl], tB8[:, cl], AF.Exp, scale=-0.5)
                    self.dinc(s, sa, 1)
                a1d = sa.n

                @block.vector
                def _(v, tc=tc):
                    v.wait_ge(sa.h, a1d)
                    cl = slice(tc, tc + 1)
                    v.reciprocal(be8[:, cl], tA8[:, cl])
                    v.drain()
                    v.tensor_tensor(be8[:, cl], va8[:, cl], be8[:, cl], ALU.mult)
                    v.drain()
                    v.tensor_scalar(be8[:, cl], be8[:, cl], EPS, None, ALU.add)
                    self.dinc(v, sv, 1)
                v1d = sv.n

                @block.scalar
                def _(s, tc=tc):
                    s.wait_ge(sv.h, v1d)
                    cl = slice(tc, tc + 1)
                    s.activation(nm8[:, cl], be8[:, cl], AF.Ln)
                    s.drain()
                    s.activation(nm8[:, cl], nm8[:, cl], AF.Exp, scale=-0.5)
                    self.dinc(s, sa, 1)
                a2d = sa.n

                @block.vector
                def _(v, tc=tc):
                    v.wait_ge(sa.h, a2d)
                    v.wait_ge(si.h, fg_ld)
                    cl = slice(tc, tc + 1)
                    v.tensor_tensor(be8[:, cl], tB8[:, cl], nm8[:, cl], ALU.mult)
                    v.drain()
                    v.scalar_tensor_tensor(nm8[:, cl], mu8[:, cl], -1.0, be8[:, cl],
                                           ALU.mult, ALU.mult)
                    v.drain()
                    v.tensor_scalar(u1[:], xph[:], be8[:, cl], nm8[:, cl], ALU.mult, ALU.add)
                    v.drain()
                    v.tensor_tensor(u1[:], u1[:], self.fgw[:], ALU.mult)
                    v.drain()
                    v.tensor_tensor(hb[:], u1[:], self.fgb[:], ALU.add)
                    self.dinc(v, sv, 1)
                hbd = sv.n

                for grp in range(2):
                    bi = grp % 2
                    @block.tensor
                    def _(t, grp=grp, bi=bi, tc=tc):
                        t.wait_ge(sv.h, hbd)
                        self.bank_wait(t, bi)
                        for j in range(4):
                            dc = grp * 4 + j
                            self.minc(t.matmul(bank[bi][:, j * 128:(j + 1) * 128],
                                               hb[:, dc * 128:(dc + 1) * 128], self.ident[:],
                                               start=True, stop=True), sp, 1)
                    pe_cnt = sp.n

                    @block.scalar
                    def _(s, grp=grp, bi=bi, tc=tc, pe_cnt=pe_cnt, hf=hf):
                        s.wait_ge(sp.h, pe_cnt)
                        if hf > 0 and tc == 0 and grp == 0:
                            s.wait_ge(sp.h, sp.n)
                        for j in range(4):
                            dc = grp * 4 + j
                            s.copy(hT[:, dc * D + tc * 128:dc * D + (tc + 1) * 128],
                                   bank[bi][:, j * 128:(j + 1) * 128])
                        self.dinc(s, sa, 1)
                    self.bank_set(bi, sa)
                hT_tc = sa.n

                @block.tensor
                def _(t, tc=tc):
                    t.wait_ge(sa.h, max(hT_tc, rt_done))
                    self.bank_wait(t, 2)
                    for dc in range(8):
                        self.minc(t.matmul(bank[2][:, 0:8],
                                           hT[:, dc * D + tc * 128:dc * D + (tc + 1) * 128],
                                           rT[:, dc * 8:(dc + 1) * 8],
                                           start=(dc == 0), stop=(dc == 7)), sp, 1)
                lg_mm = sp.n

                @block.vector
                def _(v, tc=tc):
                    v.wait_ge(sp.h, lg_mm)
                    v.tensor_copy(lgs[:], bank[2][:, 0:8])
                    v.drain()
                    v.tensor_reduce(sxs[:, 0:1], lgs[:], AX.X, ALU.max)
                    v.drain()
                    v.tensor_scalar(sxs[:, 1:2], sxs[:, 0:1], -1.0, None, ALU.mult)
                    self.dinc(v, sv, 1)
                self.bank_set(2, sv)
                lmx = sv.n

                @block.scalar
                def _(s, tc=tc):
                    s.wait_ge(sv.h, lmx)
                    s.activation(scrB[:, 0:8], lgs[:], AF.Exp, bias=sxs[:, 1:2],
                                 accum_out=sxs[:, 2:3])
                    self.dinc(s, sa, 1)
                expd = sa.n

                @block.vector
                def _(v, tc=tc):
                    v.wait_ge(sa.h, expd)
                    v.reciprocal(sxs[:, 3:4], sxs[:, 2:3])
                    v.tensor_scalar(lgs[:], lgs[:], sxs[:, 0:1], None, ALU.is_equal)
                    v.drain()
                    v.tensor_scalar(gates[:, tc * 8:(tc + 1) * 8], lgs[:], sxs[:, 3:4],
                                    None, ALU.mult)
                    self.dinc(v, sv, 1)
                gates_done = sv.n

            for s_i in range(B_SPECIES):
                p = 0 if B_SINGLEBUF else s_i % 2
                w1buf = w1T[:, p * 8 * 2048:(p + 1) * 8 * 2048]
                w2buf = w2T[:, p * 16 * D:(p + 1) * 16 * D]

                @block.sync
                def _(sync, s_i=s_i, p=p, w1buf=w1buf, hf=hf):
                    if B_SINGLEBUF:
                        sync.wait_ge(sp.h, sp.n)
                    elif (hf, s_i) != (0, 0) and not (hf == 0 and s_i == 1):
                        key = (hf, s_i - 2) if s_i >= 2 else (hf - 1, 8 - s_i)
                        if key in spec_done:
                            sync.wait_ge(sp.h, spec_done[key])
                    for dc in range(8):
                        self.minc(sync.dma_start(
                            out=w1buf[:, dc * 2048:(dc + 1) * 2048],
                            in_=d["w1n_s"][s_i * 2048:(s_i + 1) * 2048, dc * 128:(dc + 1) * 128],
                            transpose=True), si, 16)
                w1_ld = si.n

                @block.sync
                def _(sync, s_i=s_i, p=p, w2buf=w2buf, hf=hf):
                    for fc in range(16):
                        self.minc(sync.dma_start(
                            out=w2buf[:, fc * D:(fc + 1) * D],
                            in_=d["w2n_s"][s_i * D:(s_i + 1) * D, fc * 128:(fc + 1) * 128],
                            transpose=True), sc, 16)
                w2_ld = sc.n

                for tg in range(4):
                    for fq in range(4):
                        b0 = (fq % 2) * 2
                        @block.tensor
                        def _(t, tg=tg, fq=fq, b0=b0, s_i=s_i):
                            if tg == 0 and fq == 0:
                                t.wait_ge(si.h, w1_ld)
                                t.wait_ge(sa.h, hT_tc)
                            self.bank_wait(t, b0)
                            self.bank_wait(t, b0 + 1)
                            for j in range(4):
                                fc = fq * 4 + j
                                dst = bank[b0 + j // 2][:, (j % 2) * 256:(j % 2) * 256 + 256]
                                for dc in range(8):
                                    self.minc(t.matmul(
                                        dst, w1buf[:, dc * 2048 + fc * 128:dc * 2048 + (fc + 1) * 128],
                                        hT[:, dc * D + tg * 256:dc * D + (tg + 1) * 256],
                                        start=(dc == 0), stop=(dc == 7)), sp, 1)
                        mm_cnt = sp.n

                        @block.scalar
                        def _(s, tg=tg, fq=fq, b0=b0, mm_cnt=mm_cnt):
                            s.wait_ge(sp.h, mm_cnt)
                            s.activation(mid[:, (fq * 4) * 256:(fq * 4) * 256 + 512],
                                         bank[b0][:], AF.Silu)
                            s.activation(mid[:, (fq * 4 + 2) * 256:(fq * 4 + 2) * 256 + 512],
                                         bank[b0 + 1][:], AF.Silu)
                            self.dinc(s, sa, 1)
                        self.bank_set(b0, sa)
                        self.bank_set(b0 + 1, sa)
                    silu_all = sa.n

                    for tc2 in range(2):
                        b0 = 4 + tc2 * 2
                        tgc = tg * 2 + tc2

                        @block.tensor
                        def _(t, tc2=tc2, b0=b0, tgc=tgc, s_i=s_i):
                            t.wait_ge(sa.h, silu_all)
                            if tg == 0 and tc2 == 0:
                                t.wait_ge(sc.h, w2_ld)
                            self.bank_wait(t, b0)
                            self.bank_wait(t, b0 + 1)
                            for fc in range(16):
                                lh = mid[:, fc * 256 + tc2 * 128:fc * 256 + (tc2 + 1) * 128]
                                self.minc(t.matmul(bank[b0][:], lh, w2buf[:, fc * D:fc * D + 512],
                                                   start=(fc == 0), stop=(fc == 15)), sp, 1)
                                self.minc(t.matmul(bank[b0 + 1][:], lh, w2buf[:, fc * D + 512:(fc + 1) * D],
                                                   start=(fc == 0), stop=(fc == 15)), sp, 1)
                        eo_mm = sp.n

                        @block.vector
                        def _(v, tc2=tc2, b0=b0, tgc=tgc, s_i=s_i):
                            v.wait_ge(sp.h, eo_mm)
                            if s_i == 0:
                                v.wait_ge(sv.h, gates_done)
                            asl = acc[:, tgc * D:(tgc + 1) * D]
                            for half in range(2):
                                pslice = bank[b0 + half][:]
                                osl = acc[:, tgc * D + half * 512:tgc * D + (half + 1) * 512]
                                if s_i == 0:
                                    v.tensor_scalar(osl, pslice, 1.0, None, ALU.mult)
                                else:
                                    v.scalar_tensor_tensor(osl, pslice,
                                                           gates[:, tgc * 8 + s_i - 1:tgc * 8 + s_i],
                                                           osl, ALU.mult, ALU.add)
                            self.dinc(v, sv, 1)
                        self.bank_set(b0, sv)
                        self.bank_set(b0 + 1, sv)
                spec_done[(hf, s_i)] = sp.n

            for tc in range(8):
                row0 = hf * 1024 + tc * 128
                accd = sv.n

                @block.scalar
                def _(s, tc=tc):
                    s.wait_ge(sv.h, accd)
                    asl = acc[:, tc * D:(tc + 1) * D]
                    s.activation(scrB[:], asl, AF.Square, accum_out=sq8[:, tc:tc + 1])
                    s.activation(scrB[:], asl, AF.Identity, accum_out=st8[:, tc:tc + 1])
                    self.dinc(s, sa, 1)
                std = sa.n

                @block.sync
                def _(sync, row0=row0, tc=tc):
                    sync.wait_ge(sv.h, sv.n)
                    sync.wait_ge(self.so.h, self.so.n)
                    self.minc(sync.dma_start(out=xph[:], in_=d["xp_s"][row0:row0 + 128, :]), si, 16)
                xld = si.n

                @block.vector
                def _(v, tc=tc):
                    v.wait_ge(sa.h, std)
                    cl = slice(tc, tc + 1)
                    v.tensor_scalar(mu8[:, cl], st8[:, cl], 1.0 / D, None, ALU.mult)
                    v.drain()
                    v.tensor_tensor(tA8[:, cl], mu8[:, cl], mu8[:, cl], ALU.mult)
                    v.drain()
                    v.scalar_tensor_tensor(va8[:, cl], sq8[:, cl], 1.0 / D, tA8[:, cl],
                                           ALU.mult, ALU.subtract)
                    v.drain()
                    v.tensor_scalar(va8[:, cl], va8[:, cl], EPS, None, ALU.add)
                    self.dinc(v, sv, 1)
                ved = sv.n

                @block.scalar
                def _(s, tc=tc):
                    s.wait_ge(sv.h, ved)
                    cl = slice(tc, tc + 1)
                    s.activation(va8[:, cl], va8[:, cl], AF.Ln)
                    s.drain()
                    s.activation(va8[:, cl], va8[:, cl], AF.Exp, scale=-0.5)
                    self.dinc(s, sa, 1)
                ald = sa.n

                @block.vector
                def _(v, tc=tc):
                    v.wait_ge(sa.h, ald)
                    v.wait_ge(si.h, xld)
                    cl = slice(tc, tc + 1)
                    v.scalar_tensor_tensor(nm8[:, cl], mu8[:, cl], -1.0, va8[:, cl],
                                           ALU.mult, ALU.mult)
                    v.drain()
                    v.tensor_scalar(xph[:], xph[:], 1.0, nm8[:, cl], ALU.mult, ALU.add)
                    v.drain()
                    v.scalar_tensor_tensor(xph[:], acc[:, tc * D:(tc + 1) * D], va8[:, cl],
                                           xph[:], ALU.mult, ALU.add)
                    self.dinc(v, sv, 1)
                outd = sv.n

                @block.sync
                def _(sync, row0=row0, outd=outd):
                    sync.wait_ge(sv.h, outd)
                    self.minc(sync.dma_start(out=d["out"][row0:row0 + 128, :], in_=xph[:]), self.so, 16)
                if hf == 1 and tc == 7:
                    self.dbg("xph_last", xph[:])
            if hf == 0:
                self.dbg("gates0", gates[:])
                self.dbg("acc0", acc[:, 0:D])
                self.dbg("hT0", hT[:, 0:D])
                self.dbg("w1T0", w1T[:, 0:2048])
                self.dbg("mid0", mid[:, 0:1024])
            else:
                self.dbg("accF", acc[:])
                self.dbg("gatesF", gates[:])

        @block.sync
        def _(sync):
            sync.wait_ge(self.so.h, self.so.n)

    def emit_dumps(self):
        d, block = self.d, self.block
        if not self.dump:
            return

        @block.sync
        def _(sync):
            for s in (self.sg, self.sv, self.sa, self.sp, self.sc):
                sync.wait_ge(s.h, s.n)
            for name in self.dump:
                if name.startswith("@"):
                    continue
                self.minc(sync.dma_start(out=d["dump_" + name], in_=d[name]), self.so, 16)
            sync.wait_ge(self.so.h, self.so.n)


def _host_constants():
    g = 1.0 - np.exp2(-5.0 - np.arange(H, dtype=np.float64))
    i = np.arange(C, dtype=np.float64)
    dmask = np.where(i[:, None] >= i[None, :], g[:, None, None] ** (i[:, None] - i[None, :]), 0.0)
    dmaskT = dmask.transpose(0, 2, 1)
    dmaskT = np.ascontiguousarray(dmaskT.transpose(1, 0, 2).reshape(C, 4 * C))
    cross = g[:, None] ** (i + 1.0)
    crossrep = np.broadcast_to(cross.reshape(1, 4 * C), (128, 4 * C))
    kdec = g[:, None] ** (C - 1.0 - i)
    kdecP = np.ascontiguousarray(kdec.T)
    inv_freq = 1.0 / 10000.0 ** (np.arange(0, DK, 2, dtype=np.float64) / DK)
    freqs = np.arange(T, dtype=np.float64)[:, None] * inv_freq[None, :]
    cosT = np.cos(freqs).T
    sinT = np.sin(freqs).T
    ident = np.eye(128, dtype=np.float32)
    f32 = lambda a: np.ascontiguousarray(np.asarray(a, np.float64), dtype=np.float32)
    bf = lambda a: np.ascontiguousarray(np.asarray(a, np.float32).astype(ml_dtypes.bfloat16))
    return {"dmaskT": f32(dmaskT), "crossrep": bf(crossrep), "kdecP": f32(kdecP),
            "cosT": bf(cosT), "sinT": bf(sinT), "ident": bf(ident)}


def _rep128(vec):
    v = np.asarray(vec, np.float32)
    return np.ascontiguousarray(np.broadcast_to(v[None, :], (128, v.shape[0])))


def kernel(**inputs):
    consts = _host_constants()
    gnw = np.asarray(inputs["L0_attn_g_norm_w"], np.float32)
    gnwP = np.ascontiguousarray(np.tile(gnw, 4).reshape(16, 128).T)
    f32c = lambda a: np.ascontiguousarray(np.asarray(a, np.float32))

    shared_in = {
        "Wq": f32c(inputs["L0_attn_Wq"]), "Wk": f32c(inputs["L0_attn_Wk"]),
        "Wv": f32c(inputs["L0_attn_Wv"]), "Wg": f32c(inputs["L0_attn_Wg"]),
        "Wo": f32c(inputs["L0_attn_Wo"]),
        "lnw_rep": _rep128(inputs["L0_attn_ln_w"]),
        "lnb_rep": _rep128(inputs["L0_attn_ln_b"]),
        "fgw_rep": _rep128(inputs["L0_ffn_ln_gamma"]),
        "fgb_rep": _rep128(inputs["L0_ffn_ln_beta"]),
        "gnwP": gnwP,
        "router": f32c(inputs["L0_ffn_router_w"]),
        "sw1": f32c(inputs["L0_ffn_shared_w1"]), "sw2": f32c(inputs["L0_ffn_shared_w2"]),
        "ew1": f32c(inputs["L0_ffn_expert_w1"]).reshape(8 * 2 * D, D),
        "ew2": f32c(inputs["L0_ffn_expert_w2"]).reshape(8 * D, 2 * D),
        **consts,
    }
    x = np.asarray(inputs["L0_attn_x"], np.float32)
    in_maps = [dict(shared_in, x=np.ascontiguousarray(x[c])) for c in range(N_CORES)]

    b = Build(DUMP)
    nc = b.build()
    res = run_bass_kernel_spmd(nc, in_maps, core_ids=list(range(N_CORES)))
    if DUMP:
        kernel.last_results = res.results
    return np.stack([np.asarray(res.results[c]["out"]) for c in range(N_CORES)], axis=0)


# version 12 (final)
# speedup vs baseline: 1.4716x; 1.0299x over previous
import contextlib

import numpy as np
import ml_dtypes

import concourse.bass as bass
import concourse.mybir as mybir
from concourse.bass_utils import run_bass_kernel_spmd

F32 = mybir.dt.float32
BF16 = mybir.dt.bfloat16
AF = mybir.ActivationFunctionType
ALU = mybir.AluOpType
AX = mybir.AxisListType

T, D, H, C = 2048, 1024, 4, 128
NCH = T // C
DK, DV = D // H, 2 * D // H
EPS = 1e-5
N_CORES = 8

DUMP = ()
B_SPECIES = 9
B_SINGLEBUF = False


class Sem:
    def __init__(self, h):
        self.h = h
        self.n = 0

    def bump(self, k=1):
        self.n += k
        return self.n


class Build:
    def __init__(self, dump=()):
        self.dump = tuple(dump)
        nc = bass.Bass("TRN2", target_bir_lowering=False)
        self.nc = nc
        d = self.d = {}

        def din(name, shape, dt=F32):
            d[name] = nc.dram_tensor(name, shape, dt, kind="ExternalInput").ap()

        din("x", [T, D])
        din("Wq", [D, D]); din("Wk", [D, D]); din("Wv", [D, 2 * D]); din("Wg", [D, 2 * D])
        din("Wo", [2 * D, D])
        din("lnw_rep", [128, D]); din("lnb_rep", [128, D])
        din("fgw_rep", [128, D]); din("fgb_rep", [128, D])
        din("gnwP", [128, 16])
        din("router", [8, D])
        din("sw1", [2 * D, D]); din("sw2", [D, 2 * D])
        din("ew1", [8 * 2 * D, D]); din("ew2", [8 * D, 2 * D])
        din("cosT", [128, T], BF16); din("sinT", [128, T], BF16)
        din("dmaskT", [128, 4 * 128])
        din("crossrep", [128, 4 * 128], BF16)
        din("kdecP", [128, 4])
        din("ident", [128, 128], BF16)
        d["out"] = nc.dram_tensor("out", [T, D], F32, kind="ExternalOutput").ap()

        def scr(name, shape, dt):
            d[name] = nc.dram_tensor(name, shape, dt, kind="Internal").ap()

        scr("qT_s", [D, T], BF16)
        scr("kT_s", [D, T], BF16)
        scr("kk_s", [T, D], BF16)
        scr("v_s", [T, 2 * D], BF16)
        scr("g_s", [T, 2 * D], BF16)
        scr("xp_s", [T, D], F32)
        scr("w1n_s", [9 * 2 * D, D], BF16)
        scr("w2n_s", [9 * D, 2 * D], BF16)

        for name in self.dump:
            if name.startswith("@"):
                continue
            d["dump_" + name] = nc.dram_tensor(
                "dump_" + name, list(d[name].shape), d[name].dtype, kind="ExternalOutput").ap()

        g64 = 1.0 - np.exp2(-5.0 - np.arange(H, dtype=np.float64))
        self.gC = [float(np.float32(v)) for v in g64 ** C]

    def dinc(self, eng, sem, k=1):
        eng.maybe_drain_then_inc((sem.h, k))
        sem.bump(k)

    def minc(self, inst, sem, k=1):
        inst.then_inc(sem.h, k)
        sem.bump(k)

    def dbg(self, name, ap):
        if "@" + name not in self.dump:
            return
        nc, block = self.nc, self.block
        shape = [int(s) for s in ap.shape]
        out = nc.dram_tensor("dump_@" + name, shape, ap.dtype, kind="ExternalOutput").ap()

        @block.gpsimd
        def _(g):
            for s in (self.sv, self.sa, self.sp, self.si, self.sg, self.sc):
                g.wait_ge(s.h, s.n)
            self.minc(g.dma_start(out=out, in_=ap), self.so, 16)

    def bank_wait(self, eng, i):
        g = self.bfree.get(i)
        if g is not None:
            eng.wait_ge(g[0].h, g[1])

    def bank_set(self, i, sem):
        self.bfree[i] = (sem, sem.n)

    def barrier(self):
        block = self.block
        for name, sem in (("tensor", self.sp), ("vector", self.sv),
                          ("scalar", self.sa), ("gpsimd", self.sg)):
            def _f(eng, sem=sem):
                self.dinc(eng, sem, 1)
            getattr(block, name)(_f)
        tot = [(s.h, s.n) for s in (self.si, self.sg, self.sc, self.sp, self.sv, self.sa)]
        for name in ("tensor", "vector", "scalar", "gpsimd", "sync"):
            def _f(eng, tot=tot):
                for h, n in tot:
                    eng.wait_ge(h, n)
            getattr(block, name)(_f)

    def build(self):
        nc, d = self.nc, self.d
        with contextlib.ExitStack() as ctx:
            E = ctx.enter_context
            sb = lambda name, shape, dt: E(nc.sbuf_tensor(name, shape, dt))
            self.bank = [E(nc.psum_tensor(f"bank{i}", [128, 512], F32)) for i in range(8)]
            self.bfree = {}

            self.si = Sem(E(nc.semaphore("si")))
            self.sg = Sem(E(nc.semaphore("sg")))
            self.sc = Sem(E(nc.semaphore("sc")))
            self.sp = Sem(E(nc.semaphore("sp")))
            self.sv = Sem(E(nc.semaphore("sv")))
            self.sa = Sem(E(nc.semaphore("sa")))
            self.so = Sem(E(nc.semaphore("so")))
            self.sq = Sem(E(nc.semaphore("sq")))
            block = self.block = E(nc.Block())

            self.ident = sb("s_ident", [128, 128], BF16)
            self.a1v = sb("s_a1v", [128, NCH], F32)
            self.nma1 = sb("s_nma1", [128, NCH], F32)

            si = self.si

            @block.sync
            def _(sync):
                self.minc(sync.dma_start(out=self.ident[:], in_=d["ident"]), si, 16)
            self.consts_loaded = si.n

            with contextlib.ExitStack() as pctx:
                self.phase_a1(pctx)
            self.barrier()
            with contextlib.ExitStack() as pctx:
                self.phase_a2(pctx)
            self.barrier()
            with contextlib.ExitStack() as qctx:
                self.init_q(qctx)
                self.emit_q(10 ** 9)
            self.barrier()
            with contextlib.ExitStack() as pctx:
                self.phase_b(pctx)
            self.emit_dumps()
        return nc

    def phase_a1(self, ctx):
        nc, d, block = self.nc, self.d, self.block
        si, sg, sp, sv, sa = self.si, self.sg, self.sp, self.sv, self.sa
        bank = self.bank
        sb = lambda name, shape, dt: ctx.enter_context(nc.sbuf_tensor(name, shape, dt))

        wq = sb("a_wq", [128, 8 * D], BF16)
        wk = sb("a_wk", [128, 8 * D], BF16)
        wv = sb("a_wv", [128, 8 * 2 * D], BF16)
        wg = sb("a_wg", [128, 8 * 2 * D], BF16)
        cosT = sb("a_cosT", [128, T], BF16)
        sinT = sb("a_sinT", [128, T], BF16)
        xin = sb("a_xin", [128, 4 * D], F32)
        s1 = sb("a_s1", [128, NCH], F32)
        ssq = sb("a_ssq", [128, NCH], F32)
        mu = sb("a_mu", [128, NCH], F32)
        var = sb("a_var", [128, NCH], F32)
        tA = sb("a_tA", [128, NCH], F32)
        tB = sb("a_tB", [128, NCH], F32)
        beta = sb("a_beta", [128, NCH], F32)
        nmb = sb("a_nmb", [128, NCH], F32)
        u1 = sb("a_u1", [128, 4 * D], F32)
        scr = u1[:, 0:D]
        nrm = sb("a_nrm", [128, 4 * D], BF16)
        nrmT = sb("a_nrmT", [128, 8 * 512], BF16)
        praw = sb("a_praw", [128, 8 * 512], BF16)
        ra = sb("a_ra", [128, 1024], BF16)
        rb = sb("a_rb", [128, 1024], BF16)
        pT = praw
        kkb = sb("a_kkb", [128, 4 * D], BF16)
        vgb = sb("a_vgb", [128, 4 * 512], BF16)

        kdecP = sb("a_kdecP", [128, 4], F32)
        self.lnw = sb("a_lnw", [128, D], F32)
        self.lnb = sb("a_lnb", [128, D], F32)

        @block.sync
        def _(sync):
            self.minc(sync.dma_start(out=kdecP[:], in_=d["kdecP"]), si, 16)
            self.minc(sync.dma_start(out=self.lnw[:], in_=d["lnw_rep"]), si, 16)
            self.minc(sync.dma_start(out=self.lnb[:], in_=d["lnb_rep"]), si, 16)
            self.minc(sync.dma_start(out=cosT[:], in_=d["cosT"]), si, 16)
            self.minc(sync.dma_start(out=sinT[:], in_=d["sinT"]), si, 16)
        consts2 = si.n
        trig_loaded = consts2

        xfree = {}

        @block.gpsimd
        def _(g):
            for name, dst, cn in (("Wq", wq, 8), ("Wk", wk, 8), ("Wv", wv, 8), ("Wg", wg, 8)):
                self.minc(g.dma_start(out=dst[:].rearrange("p (c f) -> p c f", c=cn),
                                      in_=d[name].rearrange("(c p) f -> p c f", p=128)), sg, 16)
        w_loaded = sg.n

        for tt in range(4):
            xbuf = xin
            cols = slice(tt * 4, tt * 4 + 4)

            @block.sync
            def _(sync, tt=tt, xbuf=xbuf):
                if tt >= 1:
                    sync.wait_ge(sv.h, xfree[tt - 1])
                src = d["x"].rearrange("(c p) f -> p c f", p=128)[:, tt * 4:tt * 4 + 4, :]
                self.minc(sync.dma_start(out=xbuf.rearrange("p (c f) -> p c f", c=4), in_=src), si, 16)
            x_loaded = si.n

            @block.scalar
            def _(s, tt=tt, xbuf=xbuf):
                s.wait_ge(si.h, x_loaded)
                for c in range(4):
                    tc = tt * 4 + c
                    xc = xbuf[:, c * D:(c + 1) * D]
                    s.activation(scr[:], xc, AF.Square, accum_out=ssq[:, tc:tc + 1])
                    s.activation(scr[:], xc, AF.Identity, accum_out=s1[:, tc:tc + 1])
                self.dinc(s, sa, 1)
            stats_done = sa.n
            if tt == 0:
                self.dbg("xin", xin[:])
                self.dbg("s1", s1[:])
                self.dbg("ssq", ssq[:])

            @block.vector
            def _(v, cols=cols):
                v.wait_ge(sa.h, stats_done)
                cl = cols
                v.tensor_scalar(mu[:, cl], s1[:, cl], 1.0 / D, None, ALU.mult)
                v.drain()
                v.tensor_tensor(tA[:, cl], mu[:, cl], mu[:, cl], ALU.mult)
                v.drain()
                v.scalar_tensor_tensor(var[:, cl], ssq[:, cl], 1.0 / D, tA[:, cl], ALU.mult, ALU.subtract)
                v.drain()
                v.tensor_scalar(tA[:, cl], var[:, cl], EPS, None, ALU.add)
                self.dinc(v, sv, 1)
            ve_done = sv.n

            @block.scalar
            def _(s, cols=cols):
                s.wait_ge(sv.h, ve_done)
                s.activation(tB[:, cols], tA[:, cols], AF.Ln)
                s.drain()
                s.activation(tB[:, cols], tB[:, cols], AF.Exp, scale=0.5)
                self.dinc(s, sa, 1)
            sd1_done = sa.n

            @block.vector
            def _(v, cols=cols):
                cl = cols
                v.wait_ge(sa.h, sd1_done)
                v.reciprocal(self.a1v[:, cl], tB[:, cl])
                v.reciprocal(tB[:, cl], tA[:, cl])
                v.drain()
                v.tensor_tensor(tB[:, cl], var[:, cl], tB[:, cl], ALU.mult)
                v.drain()
                v.tensor_scalar(tB[:, cl], tB[:, cl], EPS, None, ALU.add)
                self.dinc(v, sv, 1)
            v1_done = sv.n

            @block.scalar
            def _(s, cols=cols):
                s.wait_ge(sv.h, v1_done)
                s.activation(tA[:, cols], tB[:, cols], AF.Ln)
                s.drain()
                s.activation(tA[:, cols], tA[:, cols], AF.Exp, scale=0.5)
                self.dinc(s, sa, 1)
            sd2_done = sa.n

            @block.vector
            def _(v, cols=cols, tt=tt, xbuf=xbuf):
                cl = cols
                v.wait_ge(sa.h, sd2_done)
                if tt > 0:
                    v.wait_ge(sp.h, sp.n)
                v.wait_ge(si.h, self.consts_loaded)
                v.reciprocal(tB[:, cl], tA[:, cl])
                v.drain()
                v.tensor_tensor(beta[:, cl], self.a1v[:, cl], tB[:, cl], ALU.mult)
                v.drain()
                v.scalar_tensor_tensor(nmb[:, cl], mu[:, cl], -1.0, beta[:, cl], ALU.mult, ALU.mult)
                v.scalar_tensor_tensor(self.nma1[:, cl], mu[:, cl], -1.0, self.a1v[:, cl], ALU.mult, ALU.mult)
                v.drain()
                for c in range(4):
                    tc = tt * 4 + c
                    v.tensor_scalar(u1[:, c * D:(c + 1) * D], xbuf[:, c * D:(c + 1) * D],
                                    beta[:, tc:tc + 1], nmb[:, tc:tc + 1], ALU.mult, ALU.add)
                v.drain()
                for c in range(4):
                    v.tensor_tensor(u1[:, c * D:(c + 1) * D], u1[:, c * D:(c + 1) * D],
                                    self.lnw[:], ALU.mult)
                v.drain()
                for c in range(4):
                    v.tensor_tensor(nrm[:, c * D:(c + 1) * D], u1[:, c * D:(c + 1) * D],
                                    self.lnb[:], ALU.add)
                self.dinc(v, sv, 1)
            nrm_done = sv.n
            xfree[tt] = nrm_done
            if tt == 0:
                self.dbg("beta", beta[:])
                self.dbg("nmb", nmb[:])
                self.dbg("nrm", nrm[:])
                self.dbg("mu", mu[:])
                self.dbg("var", var[:])
                self.dbg("a1v", self.a1v[:])

            for grp in range(8):
                bi = grp % 2
                bk = bank[bi]

                @block.tensor
                def _(t, grp=grp, bk=bk, bi=bi):
                    t.wait_ge(sv.h, nrm_done)
                    if grp == 0:
                        t.wait_ge(si.h, 16)
                    self.bank_wait(t, bi)
                    for j in range(4):
                        i = grp * 4 + j
                        c, dc = i // 8, i % 8
                        self.minc(t.matmul(bk[:, j * 128:(j + 1) * 128],
                                           nrm[:, c * D + dc * 128:c * D + (dc + 1) * 128],
                                           self.ident[:], start=True, stop=True), sp, 1)
                pe_cnt = sp.n

                @block.scalar
                def _(s, grp=grp, bk=bk):
                    s.wait_ge(sp.h, pe_cnt)
                    for j in range(4):
                        i = grp * 4 + j
                        c, dc = i // 8, i % 8
                        s.copy(nrmT[:, dc * 512 + c * 128:dc * 512 + (c + 1) * 128],
                               bk[:, j * 128:(j + 1) * 128])
                    self.dinc(s, sa, 1)
                self.bank_set(bi, sa)
            nrmT_ready = sa.n

            for which, wmat, dst_s, scale in (("q", wq, "qT_s", DK ** -0.5),
                                              ("k", wk, "kT_s", 1.0)):
                for fp in range(4):
                    @block.tensor
                    def _(t, fp=fp, wmat=wmat, which=which, tt=tt):
                        if which == "q" and fp == 0:
                            t.wait_ge(sa.h, nrmT_ready)
                        if which == "q" and tt == 0 and fp == 0:
                            t.wait_ge(sg.h, w_loaded)
                        for half in range(2):
                            fc = fp * 2 + half
                            self.bank_wait(t, 2 + half)
                            for kc in range(8):
                                self.minc(t.matmul(bank[2 + half][:],
                                                   wmat[:, kc * D + fc * 128:kc * D + (fc + 1) * 128],
                                                   nrmT[:, kc * 512:(kc + 1) * 512],
                                                   start=(kc == 0), stop=(kc == 7)), sp, 1)
                    pe_cnt = sp.n

                    @block.scalar
                    def _(s, fp=fp, scale=scale, pe_cnt=pe_cnt):
                        s.wait_ge(sp.h, pe_cnt)
                        s.wait_ge(sv.h, sv.n)
                        s.wait_ge(sg.h, sg.n)
                        for half in range(2):
                            fc = fp * 2 + half
                            s.activation(praw[:, fc * 512:(fc + 1) * 512], bank[2 + half][:],
                                         AF.Copy, scale=float(scale))
                        self.dinc(s, sa, 1)
                    self.bank_set(2, sa)
                    self.bank_set(3, sa)
                praw_done = sa.n

                @block.vector
                def _(v, tt=tt):
                    v.wait_ge(sa.h, praw_done)
                    v.wait_ge(sg.h, sg.n)
                    v.wait_ge(si.h, trig_loaded)
                    cs = cosT[:, tt * 512:(tt + 1) * 512]
                    sn = sinT[:, tt * 512:(tt + 1) * 512]
                    for h in range(H):
                        t1 = praw[:, (2 * h) * 512:(2 * h + 1) * 512]
                        t2 = praw[:, (2 * h + 1) * 512:(2 * h + 2) * 512]
                        v.tensor_tensor(ra[:, 0:512], t1, cs, ALU.mult)
                        v.tensor_tensor(ra[:, 512:1024], t2, sn, ALU.mult)
                        v.tensor_tensor(rb[:, 0:512], t2, cs, ALU.mult)
                        v.tensor_tensor(rb[:, 512:1024], t1, sn, ALU.mult)
                        v.drain()
                        v.tensor_tensor(pT[:, (2 * h) * 512:(2 * h + 1) * 512],
                                        ra[:, 0:512], ra[:, 512:1024], ALU.subtract)
                        v.tensor_tensor(pT[:, (2 * h + 1) * 512:(2 * h + 2) * 512],
                                        rb[:, 0:512], rb[:, 512:1024], ALU.add)
                    self.dinc(v, sv, 1)
                rope_done = sv.n

                @block.gpsimd
                def _(g, tt=tt, dst_s=dst_s):
                    g.wait_ge(sv.h, rope_done)
                    dst = d[dst_s].rearrange("(fc p) t -> p fc t", p=128)[:, :, tt * 512:(tt + 1) * 512]
                    self.minc(g.dma_start(out=dst, in_=pT[:].rearrange("p (fc t) -> p fc t", fc=8)), sg, 16)

                if which == "k":
                    for grp in range(8):
                        bi = 4 + grp % 2
                        bk = bank[bi]

                        @block.tensor
                        def _(t, grp=grp, bk=bk, bi=bi):
                            t.wait_ge(sv.h, rope_done)
                            self.bank_wait(t, bi)
                            for j in range(4):
                                i = grp * 4 + j
                                dc, c = i // 4, i % 4
                                self.minc(t.matmul(bk[:, j * 128:(j + 1) * 128],
                                                   pT[:, dc * 512 + c * 128:dc * 512 + (c + 1) * 128],
                                                   self.ident[:], start=True, stop=True), sp, 1)
                        pe_cnt = sp.n

                        @block.vector
                        def _(v, grp=grp, bk=bk):
                            v.wait_ge(sp.h, pe_cnt)
                            if grp == 0:
                                v.wait_ge(sg.h, sg.n)
                            for j in range(4):
                                i = grp * 4 + j
                                dc, c = i // 4, i % 4
                                h = dc // 2
                                v.tensor_scalar(kkb[:, c * D + dc * 128:c * D + (dc + 1) * 128],
                                                bk[:, j * 128:(j + 1) * 128],
                                                kdecP[:, h:h + 1], None, ALU.mult)
                            self.dinc(v, sv, 1)
                        self.bank_set(bi, sv)
                    kk_done = sv.n

                    @block.gpsimd
                    def _(g, tt=tt):
                        g.wait_ge(sv.h, kk_done)
                        dst = d["kk_s"][tt * 512:(tt + 1) * 512, :].rearrange("(c p) f -> p c f", p=128)
                        self.minc(g.dma_start(out=dst, in_=kkb[:].rearrange("p (c f) -> p c f", c=4)), sg, 16)

            for which, wmat, dst_s in (("v", wv, "v_s"), ("g", wg, "g_s")):
                for c in range(4):
                    vb = (c % 2) * 4
                    @block.tensor
                    def _(t, c=c, vb=vb, wmat=wmat):
                        for vc in range(4):
                            self.bank_wait(t, vb + vc)
                            for kc in range(8):
                                self.minc(t.matmul(bank[vb + vc][:],
                                                   nrmT[:, kc * 512 + c * 128:kc * 512 + (c + 1) * 128],
                                                   wmat[:, kc * 2 * D + vc * 512:kc * 2 * D + (vc + 1) * 512],
                                                   start=(kc == 0), stop=(kc == 7)), sp, 1)
                    pe_cnt = sp.n

                    @block.scalar
                    def _(s, which=which, vb=vb, pe_cnt=pe_cnt):
                        s.wait_ge(sp.h, pe_cnt)
                        s.wait_ge(sg.h, sg.n)
                        for vc in range(4):
                            if which == "g":
                                s.activation(vgb[:, vc * 512:(vc + 1) * 512], bank[vb + vc][:], AF.Silu)
                            else:
                                s.copy(vgb[:, vc * 512:(vc + 1) * 512], bank[vb + vc][:])
                        self.dinc(s, sa, 1)
                    for vc in range(4):
                        self.bank_set(vb + vc, sa)
                    row_done = sa.n

                    @block.gpsimd
                    def _(g, tt=tt, c=c, dst_s=dst_s):
                        g.wait_ge(sa.h, row_done)
                        dst = d[dst_s][tt * 512 + c * 128:tt * 512 + (c + 1) * 128, :]
                        self.minc(g.dma_start(out=dst, in_=vgb[:]), sg, 16)


        self.a1_spills = sg.n

    def phase_a2(self, ctx):
        nc, d, block = self.nc, self.d, self.block
        si, sg, sp, sv, sa = self.si, self.sg, self.sp, self.sv, self.sa
        bank = self.bank
        sb = lambda name, shape, dt: ctx.enter_context(nc.sbuf_tensor(name, shape, dt))
        gCs = self.gC

        dmaskT = sb("c_dmaskT", [128, 512], F32)
        crossrep = sb("c_crossrep", [128, 512], BF16)
        gnwP = sb("c_gnwP", [128, 16], F32)
        self.S = sb("c_S", [128, 8 * 512], F32)
        self.Sb = sb("c_Sb", [128, 8 * 512], BF16)
        wo = sb("c_wo", [128, 16 * D], BF16)
        wostg = sb("c_wostg", [128, 2 * D], F32)
        qgrp = sb("c_qgrp", [128, 2 * 2048], BF16)
        kgrp = sb("c_kgrp", [128, 2 * 2048], BF16)
        kkgrp = sb("c_kkgrp", [128, 2 * 2048], BF16)
        vgrp = sb("c_vgrp", [128, 2 * 4096], BF16)
        ggrp = sb("c_ggrp", [128, 2 * 4096], BF16)
        xgrp = sb("c_xgrp", [128, 2 * 2048], F32)
        qx = sb("c_qx", [128, 8 * 128], BF16)
        at = sb("c_at", [128, 4 * 128], BF16)
        og = sb("c_og", [128, 2048], BF16)
        ogT = sb("c_ogT", [128, 16 * 128], BF16)
        scr5 = sb("c_scr5", [128, 512], F32)
        ssqo = sb("c_ssqo", [128, 4], F32)
        rstd = sb("c_rstd", [128, 4], F32)
        yv = sb("c_y", [128, D], F32)
        af = sb("c_af", [128, D], F32)
        xp = sb("c_xp", [128, D], F32)
        sm1 = sb("c_sm1", [128, 6], F32)

        @block.sync
        def _(sync):
            self.minc(sync.dma_start(out=dmaskT[:], in_=d["dmaskT"]), si, 16)
            self.minc(sync.dma_start(out=crossrep[:], in_=d["crossrep"]), si, 16)
            self.minc(sync.dma_start(out=gnwP[:], in_=d["gnwP"]), si, 16)
        a2c_loaded = si.n

        @block.vector
        def _(v):
            v.memset(self.S[:], 0.0)
            self.dinc(v, sv, 1)
        S_zeroed = sv.n

        wo_ready = []
        for kc in range(16):
            @block.sync
            def _(sync, kc=kc):
                if kc >= 2:
                    sync.wait_ge(sv.h, wo_ready[kc - 2])
                self.minc(sync.dma_start(out=wostg[:, (kc % 2) * D:((kc % 2) + 1) * D],
                                         in_=d["Wo"][kc * 128:(kc + 1) * 128, :]), si, 16)
            ld = si.n

            @block.vector
            def _(v, kc=kc):
                v.wait_ge(si.h, max(ld, a2c_loaded))
                v.tensor_scalar(wo[:, kc * D:(kc + 1) * D], wostg[:, (kc % 2) * D:((kc % 2) + 1) * D],
                                gnwP[:, kc:kc + 1], None, ALU.mult)
                self.dinc(v, sv, 1)
            wo_ready.append(sv.n)
        wo_done = sv.n

        group_done = {}
        group_loaded = {}
        prev = {}
        for n in range(NCH):
            gidx, par = n // 2, n % 2
            gb = (gidx % 2)

            if par == 0:
                @block.sync
                def _(sync, n=n, gidx=gidx, gb=gb):
                    if gidx >= 2:
                        sync.wait_ge(sv.h, group_done[gidx - 2])
                    r0 = n * 128
                    self.minc(sync.dma_start(
                        out=qgrp[:, gb * 2048:(gb + 1) * 2048].rearrange("p (fc t) -> p fc t", fc=8),
                        in_=d["qT_s"].rearrange("(fc p) t -> p fc t", p=128)[:, :, r0:r0 + 256]), si, 16)
                    self.minc(sync.dma_start(
                        out=kgrp[:, gb * 2048:(gb + 1) * 2048].rearrange("p (fc t) -> p fc t", fc=8),
                        in_=d["kT_s"].rearrange("(fc p) t -> p fc t", p=128)[:, :, r0:r0 + 256]), si, 16)
                    self.minc(sync.dma_start(
                        out=kkgrp[:, gb * 2048:(gb + 1) * 2048].rearrange("p (c f) -> p c f", c=2),
                        in_=d["kk_s"][r0:r0 + 256, :].rearrange("(c p) f -> p c f", p=128)), si, 16)
                    self.minc(sync.dma_start(
                        out=vgrp[:, gb * 4096:(gb + 1) * 4096].rearrange("p (c f) -> p c f", c=2),
                        in_=d["v_s"][r0:r0 + 256, :].rearrange("(c p) f -> p c f", p=128)), si, 16)
                    self.minc(sync.dma_start(
                        out=ggrp[:, gb * 4096:(gb + 1) * 4096].rearrange("p (c f) -> p c f", c=2),
                        in_=d["g_s"][r0:r0 + 256, :].rearrange("(c p) f -> p c f", p=128)), si, 16)
                    self.minc(sync.dma_start(
                        out=xgrp[:, gb * 2048:(gb + 1) * 2048].rearrange("p (c f) -> p c f", c=2),
                        in_=d["x"][r0:r0 + 256, :].rearrange("(c p) f -> p c f", p=128)), si, 16)
                group_loaded[gidx] = si.n

            ld = group_loaded[gidx]
            qch = lambda i, gb=gb, par=par: qgrp[:, gb * 2048 + i * 256 + par * 128:gb * 2048 + i * 256 + (par + 1) * 128]
            kch = lambda i, gb=gb, par=par: kgrp[:, gb * 2048 + i * 256 + par * 128:gb * 2048 + i * 256 + (par + 1) * 128]
            kkch = lambda i, gb=gb, par=par: kkgrp[:, gb * 2048 + par * 1024 + i * 128:gb * 2048 + par * 1024 + (i + 1) * 128]
            vch = lambda h, gb=gb, par=par: vgrp[:, gb * 4096 + par * 2048 + h * 512:gb * 4096 + par * 2048 + (h + 1) * 512]
            gch = lambda h, gb=gb, par=par: ggrp[:, gb * 4096 + par * 2048 + h * 512:gb * 4096 + par * 2048 + (h + 1) * 512]
            xch = lambda gb=gb, par=par: xgrp[:, gb * 2048 + par * 1024:gb * 2048 + (par + 1) * 1024]

            @block.vector
            def _(v, n=n):
                v.wait_ge(sv.h, S_zeroed if n == 0 else prev["supd"])
                if n > 0:
                    v.wait_ge(sp.h, prev["o_mm"])
                for i in range(8):
                    v.tensor_copy(self.Sb[:, i * 512:(i + 1) * 512], self.S[:, i * 512:(i + 1) * 512])
                self.dinc(v, sv, 1)
            cast_done = sv.n

            @block.vector
            def _(v, n=n):
                v.wait_ge(si.h, max(ld, a2c_loaded))
                if n > 0:
                    v.wait_ge(sp.h, prev["o_mm"])
                for i in range(8):
                    h = i // 2
                    v.tensor_tensor(qx[:, i * 128:(i + 1) * 128], qch(i),
                                    crossrep[:, h * 128:(h + 1) * 128], ALU.mult)
                self.dinc(v, sv, 1)
            qx_done = sv.n

            @block.tensor
            def _(t, n=n):
                t.wait_ge(si.h, ld)
                self.bank_wait(t, 0)
                for h in range(H):
                    for dc in range(2):
                        self.minc(t.matmul(bank[0][:, h * 128:(h + 1) * 128],
                                           kch(2 * h + dc), qch(2 * h + dc),
                                           start=(dc == 0), stop=(dc == 1)), sp, 1)
            attnT_mm = sp.n

            @block.vector
            def _(v, n=n):
                v.wait_ge(sp.h, attnT_mm)
                if n > 0:
                    v.wait_ge(sp.h, prev["o_mm"])
                for h in range(H):
                    v.tensor_tensor(at[:, h * 128:(h + 1) * 128], bank[0][:, h * 128:(h + 1) * 128],
                                    dmaskT[:, h * 128:(h + 1) * 128], ALU.mult)
                self.dinc(v, sv, 1)
            mask_done = sv.n
            self.bank_set(0, sv)

            @block.tensor
            def _(t, n=n):
                t.wait_ge(sv.h, mask_done)
                t.wait_ge(sv.h, cast_done)
                t.wait_ge(sv.h, qx_done)
                for h in range(H):
                    self.bank_wait(t, 2 + h)
                    self.minc(t.matmul(bank[2 + h][:], at[:, h * 128:(h + 1) * 128], vch(h),
                                       start=True, stop=False), sp, 1)
                    for dc in range(2):
                        self.minc(t.matmul(bank[2 + h][:], qx[:, (2 * h + dc) * 128:(2 * h + dc + 1) * 128],
                                           self.Sb[:, (2 * h + dc) * 512:(2 * h + dc + 1) * 512],
                                           start=False, stop=(dc == 1)), sp, 1)
            o_mm = sp.n

            supd_marks = []
            for h in range(H):
                @block.tensor
                def _(t, h=h, n=n):
                    for dc in range(2):
                        bi = 6 + dc
                        self.bank_wait(t, bi)
                        self.minc(t.matmul(bank[bi][:], kkch(2 * h + dc), vch(h),
                                           start=True, stop=True), sp, 1)
                ds_mm = sp.n

                @block.vector
                def _(v, h=h, n=n, ds_mm=ds_mm):
                    v.wait_ge(sp.h, ds_mm)
                    if n == 0:
                        v.wait_ge(sv.h, S_zeroed)
                    for dc in range(2):
                        sl = slice((2 * h + dc) * 512, (2 * h + dc + 1) * 512)
                        v.scalar_tensor_tensor(self.S[:, sl], self.S[:, sl], gCs[h],
                                               bank[6 + dc][:], ALU.mult, ALU.add)
                    self.dinc(v, sv, 1)
                self.bank_set(6, sv)
                self.bank_set(7, sv)
            supd = sv.n

            @block.scalar
            def _(s, n=n):
                s.wait_ge(sp.h, o_mm)
                for h in range(H):
                    s.activation(scr5[:], bank[2 + h][:], AF.Square, accum_out=ssqo[:, h:h + 1])
                self.dinc(s, sa, 1)
            sq_done = sa.n

            @block.vector
            def _(v):
                v.wait_ge(sa.h, sq_done)
                v.tensor_scalar(rstd[:], ssqo[:], 1.0 / DV, EPS, ALU.mult, ALU.add)
                self.dinc(v, sv, 1)
            ve2 = sv.n

            @block.scalar
            def _(s):
                s.wait_ge(sv.h, ve2)
                s.activation(rstd[:], rstd[:], AF.Ln)
                s.drain()
                s.activation(rstd[:], rstd[:], AF.Exp, scale=-0.5)
                self.dinc(s, sa, 1)
            rstd_done = sa.n

            @block.vector
            def _(v, n=n):
                v.wait_ge(sa.h, rstd_done)
                if n > 0:
                    v.wait_ge(sp.h, prev["ogT_mm"])
                for h in range(H):
                    v.scalar_tensor_tensor(og[:, h * 512:(h + 1) * 512], bank[2 + h][:],
                                           rstd[:, h:h + 1], gch(h),
                                           ALU.mult, ALU.mult)
                self.dinc(v, sv, 1)
            og_done = sv.n
            for h in range(H):
                self.bank_set(2 + h, sv)

            for grp in range(4):
                bi = grp % 2
                @block.tensor
                def _(t, grp=grp, bi=bi):
                    t.wait_ge(sv.h, og_done)
                    self.bank_wait(t, bi)
                    for j in range(4):
                        kc = grp * 4 + j
                        self.minc(t.matmul(bank[bi][:, j * 128:(j + 1) * 128],
                                           og[:, kc * 128:(kc + 1) * 128], self.ident[:],
                                           start=True, stop=True), sp, 1)
                pe_cnt = sp.n

                @block.scalar
                def _(s, grp=grp, bi=bi, pe_cnt=pe_cnt):
                    s.wait_ge(sp.h, pe_cnt)
                    for j in range(4):
                        kc = grp * 4 + j
                        s.copy(ogT[:, kc * 128:(kc + 1) * 128], bank[bi][:, j * 128:(j + 1) * 128])
                    self.dinc(s, sa, 1)
                self.bank_set(bi, sa)
            ogT_mm = sp.n
            ogT_done = sa.n

            @block.tensor
            def _(t, n=n):
                t.wait_ge(sa.h, ogT_done)
                t.wait_ge(sv.h, wo_done)
                self.bank_wait(t, 2)
                self.bank_wait(t, 3)
                for half in range(2):
                    for kc in range(16):
                        self.minc(t.matmul(bank[2 + half][:], ogT[:, kc * 128:(kc + 1) * 128],
                                           wo[:, kc * D + half * 512:kc * D + (half + 1) * 512],
                                           start=(kc == 0), stop=(kc == 15)), sp, 1)
            wo_mm = sp.n

            @block.vector
            def _(v, n=n):
                v.wait_ge(sp.h, wo_mm)
                v.tensor_scalar(yv[:], xch(), self.a1v[:, n:n + 1], self.nma1[:, n:n + 1],
                                ALU.mult, ALU.add)
                v.drain()
                for half in range(2):
                    v.scalar_tensor_tensor(af[:, half * 512:(half + 1) * 512], bank[2 + half][:],
                                           1.0, yv[:, half * 512:(half + 1) * 512], ALU.mult, ALU.add)
                self.dinc(v, sv, 1)
            af_done = sv.n
            self.bank_set(2, sv)
            self.bank_set(3, sv)

            @block.scalar
            def _(s):
                s.wait_ge(sv.h, af_done)
                s.activation(scr5[:], af[:, 0:512], AF.Square, accum_out=sm1[:, 0:1])
                s.activation(scr5[:], af[:, 512:1024], AF.Square, accum_out=sm1[:, 1:2])
                s.activation(scr5[:], af[:, 0:512], AF.Identity, accum_out=sm1[:, 2:3])
                s.activation(scr5[:], af[:, 512:1024], AF.Identity, accum_out=sm1[:, 3:4])
                self.dinc(s, sa, 1)
            ost_done = sa.n

            @block.vector
            def _(v):
                v.wait_ge(sa.h, ost_done)
                v.tensor_tensor(sm1[:, 4:5], sm1[:, 2:3], sm1[:, 3:4], ALU.add)
                v.drain()
                v.tensor_scalar(sm1[:, 5:6], sm1[:, 4:5], 1.0 / D, None, ALU.mult)
                v.drain()
                v.tensor_tensor(sm1[:, 4:5], sm1[:, 0:1], sm1[:, 1:2], ALU.add)
                v.drain()
                v.tensor_tensor(sm1[:, 2:3], sm1[:, 5:6], sm1[:, 5:6], ALU.mult)
                v.drain()
                v.scalar_tensor_tensor(sm1[:, 4:5], sm1[:, 4:5], 1.0 / D, sm1[:, 2:3],
                                       ALU.mult, ALU.subtract)
                v.drain()
                v.tensor_scalar(sm1[:, 4:5], sm1[:, 4:5], EPS, None, ALU.add)
                self.dinc(v, sv, 1)
            ve3 = sv.n

            @block.scalar
            def _(s):
                s.wait_ge(sv.h, ve3)
                s.activation(sm1[:, 4:5], sm1[:, 4:5], AF.Ln)
                s.drain()
                s.activation(sm1[:, 4:5], sm1[:, 4:5], AF.Exp, scale=-0.5)
                self.dinc(s, sa, 1)
            al_done = sa.n

            @block.vector
            def _(v, n=n):
                v.wait_ge(sa.h, al_done)
                if n > 0:
                    v.wait_ge(sg.h, prev["spill"])
                v.scalar_tensor_tensor(sm1[:, 2:3], sm1[:, 5:6], -1.0, sm1[:, 4:5],
                                       ALU.mult, ALU.mult)
                v.drain()
                v.tensor_scalar(xp[:], xch(), 1.0, sm1[:, 2:3], ALU.mult, ALU.add)
                v.drain()
                v.scalar_tensor_tensor(xp[:], af[:], sm1[:, 4:5], xp[:], ALU.mult, ALU.add)
                self.dinc(v, sv, 1)
            xp_done = sv.n

            @block.gpsimd
            def _(g, n=n):
                g.wait_ge(sv.h, xp_done)
                self.minc(g.dma_start(out=d["xp_s"][n * 128:(n + 1) * 128, :], in_=xp[:]), sg, 16)
            prev = dict(supd=supd, o_mm=o_mm, og=og_done, ogT_mm=ogT_mm, spill=sg.n)
            group_done[gidx] = xp_done

    def init_q(self, ctx):
        nc, d = self.nc, self.d
        sb = lambda name, shape, dt: ctx.enter_context(nc.sbuf_tensor(name, shape, dt))
        self.q_wraw = sb("q_wraw", [128, 8 * 1024], F32)
        self.q_wab = sb("q_wab", [128, 8 * 1024], F32)
        self.q_wsgn = sb("q_wsgn", [128, 8 * 1024], BF16)
        self.q_wout = sb("q_wout", [128, 8 * 1024], BF16)
        self.q_red = sb("q_red", [128, 8], F32)
        self.q_thr = sb("q_thr", [128, 8], F32)
        self.q_mcol = sb("q_mcol", [128, 8], F32)

        units = []
        for s_i in range(9):
            w1src = d["sw1"] if s_i == 0 else d["ew1"][(s_i - 1) * 2048:s_i * 2048, :]
            w2src = d["sw2"] if s_i == 0 else d["ew2"][(s_i - 1) * 1024:s_i * 1024, :]
            for it in range(16):
                units.append((w1src[it * 128:(it + 1) * 128, :],
                              d["w1n_s"][s_i * 2048 + it * 128:s_i * 2048 + (it + 1) * 128, :], 0))
            for it in range(8):
                r0 = s_i * 1024 + it * 128
                units.append((w2src[it * 128:(it + 1) * 128, 0:1024],
                              d["w2n_s"][r0:r0 + 128, 0:1024], 1))
                units.append((w2src[it * 128:(it + 1) * 128, 1024:2048],
                              d["w2n_s"][r0:r0 + 128, 1024:2048], 2))
        self.q_units = units
        self.q_pos = 0
        self.q_gmark = {}
        self.q_stmark = {}

    def emit_q(self, kmax):
        block = self.block
        si, sg, sv, sa = self.si, self.sg, self.sv, self.sa
        units = self.q_units
        red, thr, mcol = self.q_red, self.q_thr, self.q_mcol
        ngrp = len(units) // 4
        assert len(units) % 4 == 0
        for gi in range(ngrp):
            grp = units[gi * 4:gi * 4 + 4]
            sb0 = (gi % 2) * 4
            sl = lambda j, t, sb0=sb0: t[:, (sb0 + j) * 1024:(sb0 + j + 1) * 1024]
            cidx = lambda j, sb0=sb0: slice(sb0 + j, sb0 + j + 1)

            @block.sync
            def _(sync, gi=gi, grp=grp, sl=sl):
                if gi - 2 in self.q_gmark:
                    sync.wait_ge(sv.h, self.q_gmark[gi - 2])
                for j, (srcap, dstap, kind) in enumerate(grp):
                    self.minc(sync.dma_start(out=sl(j, self.q_wraw), in_=srcap), si, 16)
            ld = si.n

            @block.scalar
            def _(s, gi=gi, grp=grp, sl=sl, cidx=cidx):
                s.wait_ge(si.h, ld)
                if gi - 2 in self.q_gmark:
                    s.wait_ge(sv.h, self.q_gmark[gi - 2])
                for j in range(4):
                    s.activation(sl(j, self.q_wab), sl(j, self.q_wraw), AF.Abs,
                                 accum_out=red[:, cidx(j)])
                    s.activation(sl(j, self.q_wsgn), sl(j, self.q_wraw), AF.Sign)
                self.dinc(s, sa, 1)
            absd = sa.n

            @block.vector
            def _(v, gi=gi, grp=grp, sl=sl, cidx=cidx):
                v.wait_ge(sa.h, absd)
                if gi - 2 in self.q_stmark:
                    v.wait_ge(sg.h, self.q_stmark[gi - 2])
                for j, (srcap, dstap, kind) in enumerate(grp):
                    if kind == 2:
                        v.tensor_tensor(red[:, cidx(j)], red[:, cidx(j - 1)], red[:, cidx(j)], ALU.add)
                v.drain()
                for j, (srcap, dstap, kind) in enumerate(grp):
                    if kind == 1:
                        continue
                    FD = 1024 if kind == 0 else 2048
                    v.tensor_scalar(thr[:, cidx(j)], red[:, cidx(j)], 0.5 / FD, 5e-6, ALU.mult, ALU.max)
                    v.tensor_scalar(mcol[:, cidx(j)], red[:, cidx(j)], 1.0 / FD, 1e-5, ALU.mult, ALU.max)
                v.drain()
                for j, (srcap, dstap, kind) in enumerate(grp):
                    jj = j + 1 if kind == 1 else j
                    v.tensor_scalar(sl(j, self.q_wab), sl(j, self.q_wab),
                                    thr[:, cidx(jj)], None, ALU.is_gt)
                v.drain()
                for j, (srcap, dstap, kind) in enumerate(grp):
                    jj = j + 1 if kind == 1 else j
                    v.scalar_tensor_tensor(sl(j, self.q_wout), sl(j, self.q_wab),
                                           mcol[:, cidx(jj)], sl(j, self.q_wsgn),
                                           ALU.mult, ALU.mult)
                self.dinc(v, sv, 1)
            self.q_gmark[gi] = sv.n

            @block.gpsimd
            def _(g, gi=gi, grp=grp, sl=sl):
                g.wait_ge(sv.h, self.q_gmark[gi])
                for j, (srcap, dstap, kind) in enumerate(grp):
                    self.minc(g.dma_start(out=dstap, in_=sl(j, self.q_wout)), sg, 16)
            self.q_stmark[gi] = sg.n

    def phase_b(self, ctx):
        nc, d, block = self.nc, self.d, self.block
        si, sg, sc, sp, sv, sa = self.si, self.sg, self.sc, self.sp, self.sv, self.sa
        bank = self.bank
        sb = lambda name, shape, dt: ctx.enter_context(nc.sbuf_tensor(name, shape, dt))

        hT = sb("b_hT", [128, 8 * D], BF16)
        w1T = sb("b_w1T", [128, 2 * 8 * 2048], BF16)
        w2T = sb("b_w2T", [128, 2 * 16 * D], BF16)
        acc = sb("b_acc", [128, 8 * D], F32)
        mid = sb("b_mid", [128, 16 * 256], BF16)
        xph = sb("b_xph", [128, 2 * D], F32)
        scrB = sb("b_scrB", [128, D], F32)
        hb = sb("b_hb", [128, D], BF16)
        u1 = sb("b_u1", [128, D], BF16)
        rT = sb("b_rT", [128, 64], BF16)
        lgs = sb("b_lgs", [128, 2 * 8], F32)
        gates = sb("b_gates", [128, 8 * 8], F32)
        st8 = sb("b_st8", [128, 8], F32)
        sq8 = sb("b_sq8", [128, 8], F32)
        mu8 = sb("b_mu8", [128, 8], F32)
        va8 = sb("b_va8", [128, 8], F32)
        tA8 = sb("b_tA8", [128, 8], F32)
        tB8 = sb("b_tB8", [128, 8], F32)
        be8 = sb("b_be8", [128, 8], F32)
        nm8 = sb("b_nm8", [128, 8], F32)
        sxs = sb("b_sxs", [128, 2 * 4], F32)

        with contextlib.ExitStack() as rctx:
            rsb = rctx.enter_context(nc.sbuf_tensor("b_rsb", [8, D], BF16))

            @block.gpsimd
            def _(g):
                self.minc(g.dma_start(out=rsb[:], in_=d["router"]), sg, 16)
            r_ld = sg.n

            @block.tensor
            def _(t):
                t.wait_ge(sg.h, r_ld)
                t.wait_ge(si.h, 16)
                self.bank_wait(t, 3)
                for dc in range(8):
                    self.minc(t.matmul(bank[3][:, dc * 8:(dc + 1) * 8],
                                       rsb[0:8, dc * 128:(dc + 1) * 128],
                                       self.ident[0:8, 0:8], start=True, stop=True), sp, 1)
            rt_mm = sp.n

        self.fgw = sb("b_fgw", [128, D], BF16)
        self.fgb = sb("b_fgb", [128, D], BF16)

        @block.gpsimd
        def _(g):
            g.wait_ge(sp.h, rt_mm)
            self.minc(g.dma_start(out=self.fgw[:], in_=d["fgw_rep"]), si, 16)
            self.minc(g.dma_start(out=self.fgb[:], in_=d["fgb_rep"]), si, 16)
        fg_ld = si.n

        @block.scalar
        def _(s):
            s.wait_ge(sp.h, rt_mm)
            s.copy(rT[:], bank[3][:, 0:64])
            self.dinc(s, sa, 1)
        self.bank_set(3, sa)
        rt_done = sa.n

        spec_done = {}
        for hf in range(2):
            b1dve = {}
            b1tr = {}
            for tc in range(8):
                row0 = hf * 1024 + tc * 128
                pp = tc % 2
                xphs = xph[:, pp * D:(pp + 1) * D]
                hbs = hb[:]
                lgss = lgs[:, pp * 8:(pp + 1) * 8]
                sxss = sxs[:, pp * 4:(pp + 1) * 4]

                @block.sync
                def _(sync, row0=row0, tc=tc, hf=hf, xphs=xphs):
                    if tc >= 2:
                        sync.wait_ge(sv.h, b1dve[tc - 2])
                    elif hf > 0:
                        sync.wait_ge(sv.h, sv.n)
                    self.minc(sync.dma_start(out=xphs, in_=d["xp_s"][row0:row0 + 128, :]), si, 16)
                ld = si.n

                @block.scalar
                def _(s, tc=tc, xphs=xphs):
                    s.wait_ge(si.h, ld)
                    s.activation(scrB[:], xphs, AF.Square, accum_out=sq8[:, tc:tc + 1])
                    s.activation(scrB[:], xphs, AF.Identity, accum_out=st8[:, tc:tc + 1])
                    self.dinc(s, sa, 1)
                std = sa.n

                @block.vector
                def _(v, tc=tc):
                    v.wait_ge(sa.h, std)
                    cl = slice(tc, tc + 1)
                    v.tensor_scalar(mu8[:, cl], st8[:, cl], 1.0 / D, None, ALU.mult)
                    v.drain()
                    v.tensor_tensor(tA8[:, cl], mu8[:, cl], mu8[:, cl], ALU.mult)
                    v.drain()
                    v.scalar_tensor_tensor(va8[:, cl], sq8[:, cl], 1.0 / D, tA8[:, cl],
                                           ALU.mult, ALU.subtract)
                    v.drain()
                    v.tensor_scalar(tA8[:, cl], va8[:, cl], EPS, None, ALU.add)
                    self.dinc(v, sv, 1)
                ve = sv.n

                @block.scalar
                def _(s, tc=tc):
                    s.wait_ge(sv.h, ve)
                    cl = slice(tc, tc + 1)
                    s.activation(tB8[:, cl], tA8[:, cl], AF.Ln)
                    s.drain()
                    s.activation(tB8[:, cl], tB8[:, cl], AF.Exp, scale=-0.5)
                    self.dinc(s, sa, 1)
                a1d = sa.n

                @block.vector
                def _(v, tc=tc):
                    v.wait_ge(sa.h, a1d)
                    cl = slice(tc, tc + 1)
                    v.reciprocal(be8[:, cl], tA8[:, cl])
                    v.drain()
                    v.tensor_tensor(be8[:, cl], va8[:, cl], be8[:, cl], ALU.mult)
                    v.drain()
                    v.tensor_scalar(be8[:, cl], be8[:, cl], EPS, None, ALU.add)
                    self.dinc(v, sv, 1)
                v1d = sv.n

                @block.scalar
                def _(s, tc=tc):
                    s.wait_ge(sv.h, v1d)
                    cl = slice(tc, tc + 1)
                    s.activation(nm8[:, cl], be8[:, cl], AF.Ln)
                    s.drain()
                    s.activation(nm8[:, cl], nm8[:, cl], AF.Exp, scale=-0.5)
                    self.dinc(s, sa, 1)
                a2d = sa.n

                @block.vector
                def _(v, tc=tc, xphs=xphs, hbs=hbs):
                    v.wait_ge(sa.h, a2d)
                    v.wait_ge(si.h, fg_ld)
                    if tc >= 1:
                        v.wait_ge(sp.h, b1tr[tc - 1])
                    cl = slice(tc, tc + 1)
                    v.tensor_tensor(be8[:, cl], tB8[:, cl], nm8[:, cl], ALU.mult)
                    v.drain()
                    v.scalar_tensor_tensor(nm8[:, cl], mu8[:, cl], -1.0, be8[:, cl],
                                           ALU.mult, ALU.mult)
                    v.drain()
                    v.tensor_scalar(u1[:], xphs, be8[:, cl], nm8[:, cl], ALU.mult, ALU.add)
                    v.drain()
                    v.tensor_tensor(u1[:], u1[:], self.fgw[:], ALU.mult)
                    v.drain()
                    v.tensor_tensor(hbs, u1[:], self.fgb[:], ALU.add)
                    self.dinc(v, sv, 1)
                hbd = sv.n

                for grp in range(2):
                    bi = grp % 2
                    @block.tensor
                    def _(t, grp=grp, bi=bi, tc=tc, hbs=hbs):
                        t.wait_ge(sv.h, hbd)
                        self.bank_wait(t, bi)
                        for j in range(4):
                            dc = grp * 4 + j
                            self.minc(t.matmul(bank[bi][:, j * 128:(j + 1) * 128],
                                               hbs[:, dc * 128:(dc + 1) * 128], self.ident[:],
                                               start=True, stop=True), sp, 1)
                    pe_cnt = sp.n

                    @block.scalar
                    def _(s, grp=grp, bi=bi, tc=tc, pe_cnt=pe_cnt, hf=hf):
                        s.wait_ge(sp.h, pe_cnt)
                        if hf > 0 and tc == 0 and grp == 0:
                            s.wait_ge(sp.h, sp.n)
                        for j in range(4):
                            dc = grp * 4 + j
                            s.copy(hT[:, dc * D + tc * 128:dc * D + (tc + 1) * 128],
                                   bank[bi][:, j * 128:(j + 1) * 128])
                        self.dinc(s, sa, 1)
                    self.bank_set(bi, sa)
                b1tr[tc] = sp.n
                hT_tc = sa.n

                @block.tensor
                def _(t, tc=tc):
                    t.wait_ge(sa.h, max(hT_tc, rt_done))
                    self.bank_wait(t, 2)
                    for dc in range(8):
                        self.minc(t.matmul(bank[2][:, 0:8],
                                           hT[:, dc * D + tc * 128:dc * D + (tc + 1) * 128],
                                           rT[:, dc * 8:(dc + 1) * 8],
                                           start=(dc == 0), stop=(dc == 7)), sp, 1)
                lg_mm = sp.n

                @block.vector
                def _(v, tc=tc, lgss=lgss, sxss=sxss):
                    v.wait_ge(sp.h, lg_mm)
                    v.tensor_copy(lgss, bank[2][:, 0:8])
                    v.drain()
                    v.tensor_reduce(sxss[:, 0:1], lgss, AX.X, ALU.max)
                    v.drain()
                    v.tensor_scalar(sxss[:, 1:2], sxss[:, 0:1], -1.0, None, ALU.mult)
                    self.dinc(v, sv, 1)
                self.bank_set(2, sv)
                lmx = sv.n

                @block.scalar
                def _(s, tc=tc, lgss=lgss, sxss=sxss, pp=pp):
                    s.wait_ge(sv.h, lmx)
                    s.activation(scrB[:, pp * 8:pp * 8 + 8], lgss, AF.Exp, bias=sxss[:, 1:2],
                                 accum_out=sxss[:, 2:3])
                    self.dinc(s, sa, 1)
                expd = sa.n

                @block.vector
                def _(v, tc=tc, lgss=lgss, sxss=sxss):
                    v.wait_ge(sa.h, expd)
                    v.reciprocal(sxss[:, 3:4], sxss[:, 2:3])
                    v.tensor_scalar(lgss, lgss, sxss[:, 0:1], None, ALU.is_equal)
                    v.drain()
                    v.tensor_scalar(gates[:, tc * 8:(tc + 1) * 8], lgss, sxss[:, 3:4],
                                    None, ALU.mult)
                    self.dinc(v, sv, 1)
                b1dve[tc] = sv.n
                gates_done = sv.n

            for s_i in range(B_SPECIES):
                p = 0 if B_SINGLEBUF else s_i % 2
                w1buf = w1T[:, p * 8 * 2048:(p + 1) * 8 * 2048]
                w2buf = w2T[:, p * 16 * D:(p + 1) * 16 * D]

                @block.sync
                def _(sync, s_i=s_i, p=p, w1buf=w1buf, hf=hf):
                    if B_SINGLEBUF:
                        sync.wait_ge(sp.h, sp.n)
                    elif (hf, s_i) != (0, 0) and not (hf == 0 and s_i == 1):
                        key = (hf, s_i - 2) if s_i >= 2 else (hf - 1, 8 - s_i)
                        if key in spec_done:
                            sync.wait_ge(sp.h, spec_done[key])
                    for dc in range(8):
                        self.minc(sync.dma_start(
                            out=w1buf[:, dc * 2048:(dc + 1) * 2048],
                            in_=d["w1n_s"][s_i * 2048:(s_i + 1) * 2048, dc * 128:(dc + 1) * 128],
                            transpose=True), si, 16)
                w1_ld = si.n

                @block.sync
                def _(sync, s_i=s_i, p=p, w2buf=w2buf, hf=hf):
                    for fc in range(16):
                        self.minc(sync.dma_start(
                            out=w2buf[:, fc * D:(fc + 1) * D],
                            in_=d["w2n_s"][s_i * D:(s_i + 1) * D, fc * 128:(fc + 1) * 128],
                            transpose=True), sc, 16)
                w2_ld = sc.n

                for tg in range(4):
                    for fq in range(4):
                        b0 = (fq % 2) * 2
                        @block.tensor
                        def _(t, tg=tg, fq=fq, b0=b0, s_i=s_i):
                            if tg == 0 and fq == 0:
                                t.wait_ge(si.h, w1_ld)
                                t.wait_ge(sa.h, hT_tc)
                            self.bank_wait(t, b0)
                            self.bank_wait(t, b0 + 1)
                            for j in range(4):
                                fc = fq * 4 + j
                                dst = bank[b0 + j // 2][:, (j % 2) * 256:(j % 2) * 256 + 256]
                                for dc in range(8):
                                    self.minc(t.matmul(
                                        dst, w1buf[:, dc * 2048 + fc * 128:dc * 2048 + (fc + 1) * 128],
                                        hT[:, dc * D + tg * 256:dc * D + (tg + 1) * 256],
                                        start=(dc == 0), stop=(dc == 7)), sp, 1)
                        mm_cnt = sp.n

                        @block.scalar
                        def _(s, tg=tg, fq=fq, b0=b0, mm_cnt=mm_cnt):
                            s.wait_ge(sp.h, mm_cnt)
                            s.activation(mid[:, (fq * 4) * 256:(fq * 4) * 256 + 512],
                                         bank[b0][:], AF.Silu)
                            s.activation(mid[:, (fq * 4 + 2) * 256:(fq * 4 + 2) * 256 + 512],
                                         bank[b0 + 1][:], AF.Silu)
                            self.dinc(s, sa, 1)
                        self.bank_set(b0, sa)
                        self.bank_set(b0 + 1, sa)
                    silu_all = sa.n

                    for tc2 in range(2):
                        b0 = 4 + tc2 * 2
                        tgc = tg * 2 + tc2

                        @block.tensor
                        def _(t, tc2=tc2, b0=b0, tgc=tgc, s_i=s_i):
                            t.wait_ge(sa.h, silu_all)
                            if tg == 0 and tc2 == 0:
                                t.wait_ge(sc.h, w2_ld)
                            self.bank_wait(t, b0)
                            self.bank_wait(t, b0 + 1)
                            for fc in range(16):
                                lh = mid[:, fc * 256 + tc2 * 128:fc * 256 + (tc2 + 1) * 128]
                                self.minc(t.matmul(bank[b0][:], lh, w2buf[:, fc * D:fc * D + 512],
                                                   start=(fc == 0), stop=(fc == 15)), sp, 1)
                                self.minc(t.matmul(bank[b0 + 1][:], lh, w2buf[:, fc * D + 512:(fc + 1) * D],
                                                   start=(fc == 0), stop=(fc == 15)), sp, 1)
                        eo_mm = sp.n

                        @block.vector
                        def _(v, tc2=tc2, b0=b0, tgc=tgc, s_i=s_i):
                            v.wait_ge(sp.h, eo_mm)
                            if s_i == 0:
                                v.wait_ge(sv.h, gates_done)
                            asl = acc[:, tgc * D:(tgc + 1) * D]
                            for half in range(2):
                                pslice = bank[b0 + half][:]
                                osl = acc[:, tgc * D + half * 512:tgc * D + (half + 1) * 512]
                                if s_i == 0:
                                    v.tensor_scalar(osl, pslice, 1.0, None, ALU.mult)
                                else:
                                    v.scalar_tensor_tensor(osl, pslice,
                                                           gates[:, tgc * 8 + s_i - 1:tgc * 8 + s_i],
                                                           osl, ALU.mult, ALU.add)
                            self.dinc(v, sv, 1)
                        self.bank_set(b0, sv)
                        self.bank_set(b0 + 1, sv)
                spec_done[(hf, s_i)] = sp.n

            for tc in range(8):
                row0 = hf * 1024 + tc * 128
                accd = sv.n
                pp = tc % 2
                xphs = xph[:, pp * D:(pp + 1) * D]

                @block.scalar
                def _(s, tc=tc):
                    s.wait_ge(sv.h, accd)
                    asl = acc[:, tc * D:(tc + 1) * D]
                    s.activation(scrB[:], asl, AF.Square, accum_out=sq8[:, tc:tc + 1])
                    s.activation(scrB[:], asl, AF.Identity, accum_out=st8[:, tc:tc + 1])
                    self.dinc(s, sa, 1)
                std = sa.n

                @block.sync
                def _(sync, row0=row0, tc=tc, xphs=xphs):
                    sync.wait_ge(sv.h, sv.n)
                    sync.wait_ge(self.so.h, self.so.n)
                    self.minc(sync.dma_start(out=xphs, in_=d["xp_s"][row0:row0 + 128, :]), si, 16)
                xld = si.n

                @block.vector
                def _(v, tc=tc):
                    v.wait_ge(sa.h, std)
                    cl = slice(tc, tc + 1)
                    v.tensor_scalar(mu8[:, cl], st8[:, cl], 1.0 / D, None, ALU.mult)
                    v.drain()
                    v.tensor_tensor(tA8[:, cl], mu8[:, cl], mu8[:, cl], ALU.mult)
                    v.drain()
                    v.scalar_tensor_tensor(va8[:, cl], sq8[:, cl], 1.0 / D, tA8[:, cl],
                                           ALU.mult, ALU.subtract)
                    v.drain()
                    v.tensor_scalar(va8[:, cl], va8[:, cl], EPS, None, ALU.add)
                    self.dinc(v, sv, 1)
                ved = sv.n

                @block.scalar
                def _(s, tc=tc):
                    s.wait_ge(sv.h, ved)
                    cl = slice(tc, tc + 1)
                    s.activation(va8[:, cl], va8[:, cl], AF.Ln)
                    s.drain()
                    s.activation(va8[:, cl], va8[:, cl], AF.Exp, scale=-0.5)
                    self.dinc(s, sa, 1)
                ald = sa.n

                @block.vector
                def _(v, tc=tc, xphs=xphs):
                    v.wait_ge(sa.h, ald)
                    v.wait_ge(si.h, xld)
                    cl = slice(tc, tc + 1)
                    v.scalar_tensor_tensor(nm8[:, cl], mu8[:, cl], -1.0, va8[:, cl],
                                           ALU.mult, ALU.mult)
                    v.drain()
                    v.tensor_scalar(xphs, xphs, 1.0, nm8[:, cl], ALU.mult, ALU.add)
                    v.drain()
                    v.scalar_tensor_tensor(xphs, acc[:, tc * D:(tc + 1) * D], va8[:, cl],
                                           xphs, ALU.mult, ALU.add)
                    self.dinc(v, sv, 1)
                outd = sv.n

                @block.sync
                def _(sync, row0=row0, outd=outd, xphs=xphs):
                    sync.wait_ge(sv.h, outd)
                    self.minc(sync.dma_start(out=d["out"][row0:row0 + 128, :], in_=xphs), self.so, 16)
                if hf == 1 and tc == 7:
                    self.dbg("xph_last", xph[:])
            if hf == 0:
                self.dbg("gates0", gates[:])
                self.dbg("acc0", acc[:, 0:D])
                self.dbg("hT0", hT[:, 0:D])
                self.dbg("w1T0", w1T[:, 0:2048])
                self.dbg("mid0", mid[:, 0:1024])
            else:
                self.dbg("accF", acc[:])
                self.dbg("gatesF", gates[:])

        @block.sync
        def _(sync):
            sync.wait_ge(self.so.h, self.so.n)

    def emit_dumps(self):
        d, block = self.d, self.block
        if not self.dump:
            return

        @block.sync
        def _(sync):
            for s in (self.sg, self.sv, self.sa, self.sp, self.sc):
                sync.wait_ge(s.h, s.n)
            for name in self.dump:
                if name.startswith("@"):
                    continue
                self.minc(sync.dma_start(out=d["dump_" + name], in_=d[name]), self.so, 16)
            sync.wait_ge(self.so.h, self.so.n)


def _host_constants():
    g = 1.0 - np.exp2(-5.0 - np.arange(H, dtype=np.float64))
    i = np.arange(C, dtype=np.float64)
    dmask = np.where(i[:, None] >= i[None, :], g[:, None, None] ** (i[:, None] - i[None, :]), 0.0)
    dmaskT = dmask.transpose(0, 2, 1)
    dmaskT = np.ascontiguousarray(dmaskT.transpose(1, 0, 2).reshape(C, 4 * C))
    cross = g[:, None] ** (i + 1.0)
    crossrep = np.broadcast_to(cross.reshape(1, 4 * C), (128, 4 * C))
    kdec = g[:, None] ** (C - 1.0 - i)
    kdecP = np.ascontiguousarray(kdec.T)
    inv_freq = 1.0 / 10000.0 ** (np.arange(0, DK, 2, dtype=np.float64) / DK)
    freqs = np.arange(T, dtype=np.float64)[:, None] * inv_freq[None, :]
    cosT = np.cos(freqs).T
    sinT = np.sin(freqs).T
    ident = np.eye(128, dtype=np.float32)
    f32 = lambda a: np.ascontiguousarray(np.asarray(a, np.float64), dtype=np.float32)
    bf = lambda a: np.ascontiguousarray(np.asarray(a, np.float32).astype(ml_dtypes.bfloat16))
    return {"dmaskT": f32(dmaskT), "crossrep": bf(crossrep), "kdecP": f32(kdecP),
            "cosT": bf(cosT), "sinT": bf(sinT), "ident": bf(ident)}


def _rep128(vec):
    v = np.asarray(vec, np.float32)
    return np.ascontiguousarray(np.broadcast_to(v[None, :], (128, v.shape[0])))


def kernel(**inputs):
    consts = _host_constants()
    gnw = np.asarray(inputs["L0_attn_g_norm_w"], np.float32)
    gnwP = np.ascontiguousarray(np.tile(gnw, 4).reshape(16, 128).T)
    f32c = lambda a: np.ascontiguousarray(np.asarray(a, np.float32))

    shared_in = {
        "Wq": f32c(inputs["L0_attn_Wq"]), "Wk": f32c(inputs["L0_attn_Wk"]),
        "Wv": f32c(inputs["L0_attn_Wv"]), "Wg": f32c(inputs["L0_attn_Wg"]),
        "Wo": f32c(inputs["L0_attn_Wo"]),
        "lnw_rep": _rep128(inputs["L0_attn_ln_w"]),
        "lnb_rep": _rep128(inputs["L0_attn_ln_b"]),
        "fgw_rep": _rep128(inputs["L0_ffn_ln_gamma"]),
        "fgb_rep": _rep128(inputs["L0_ffn_ln_beta"]),
        "gnwP": gnwP,
        "router": f32c(inputs["L0_ffn_router_w"]),
        "sw1": f32c(inputs["L0_ffn_shared_w1"]), "sw2": f32c(inputs["L0_ffn_shared_w2"]),
        "ew1": f32c(inputs["L0_ffn_expert_w1"]).reshape(8 * 2 * D, D),
        "ew2": f32c(inputs["L0_ffn_expert_w2"]).reshape(8 * D, 2 * D),
        **consts,
    }
    x = np.asarray(inputs["L0_attn_x"], np.float32)
    in_maps = [dict(shared_in, x=np.ascontiguousarray(x[c])) for c in range(N_CORES)]

    b = Build(DUMP)
    nc = b.build()
    res = run_bass_kernel_spmd(nc, in_maps, core_ids=list(range(N_CORES)))
    if DUMP:
        kernel.last_results = res.results
    return np.stack([np.asarray(res.results[c]["out"]) for c in range(N_CORES)], axis=0)
